# version 01
# baseline (speedup 1.0000x reference)
import numpy as np
from contextlib import ExitStack

import concourse.bass as bass
import concourse.mybir as mybir
from concourse.bass_utils import run_bass_kernel_spmd

F32 = mybir.dt.float32
BF16 = mybir.dt.bfloat16
AF = mybir.ActivationFunctionType
ALU = mybir.AluOpType
AX = mybir.AxisListType

CHUNK = 256
D_MODEL = 2048
D_INNER = 4096
D_STATE = 128
NHEADS = 64
KD = D_MODEL // 128
KI = D_INNER // 128
KH = 8192 // 128
EP = 67
E_XBC0 = 32
N_XBC = 34
H_FFN = 8192
MAGIC = 12582912.0


def _spans(total, width=512):
    return [(s, min(width, total - s)) for s in range(0, total, width)]


class PB:
    def __init__(self, nc, block, ctx):
        self.nc, self.block, self.ctx = nc, block, ctx
        self._sems, self.n = {}, {}

    def sem(self, name):
        if name not in self._sems:
            self._sems[name] = self.ctx.enter_context(self.nc.semaphore(name))
            self.n[name] = 0
        return self._sems[name]

    def inc(self, instr, name, k=1):
        instr.then_inc(self.sem(name), k)
        self.n[name] += k
        return self.n[name]

    def wait(self, eng, name, val=None):
        if val is None:
            val = self.n.get(name, 0)
        self.sem(name)
        if val > 0:
            eng.wait_ge(self.sem(name), val)

    def step(self, eng, instr, name):
        v = self.inc(instr, name)
        eng.wait_ge(self.sem(name), v)
        return v

    def sb(self, name, shape, dtype, ctx=None, side="left"):
        return (ctx or self.ctx).enter_context(self.nc.sbuf_tensor(name, shape, dtype, side=side))

    def ps(self, name, shape, dtype=F32, ctx=None):
        return (ctx or self.ctx).enter_context(self.nc.psum_tensor(name, shape, dtype))

    def barrier(self):
        names = tuple(nm for nm in self.n if self.n[nm] > 0)
        for sec in (self.V, self.S, self.T, self.G):
            def f(e, names=names):
                for nm in names:
                    self.wait(e, nm)
            sec(f)

    def V(self, f):
        def w(v):
            self.wait(v, "V")
            f(v)
        self.block.vector(w)

    def S(self, f):
        def w(s):
            self.wait(s, "S")
            f(s)
        self.block.scalar(w)

    def T(self, f):
        def w(t):
            self.wait(t, "T")
            f(t)
        self.block.tensor(w)

    def G(self, f):
        def w(g):
            self.wait(g, "G")
            f(g)
        self.block.gpsimd(w)


def build(n_cores=8, n_chunks=4, taps=()):
    T = n_chunks * CHUNK
    XROWS = -(-(T + 3) // 128) * 128
    NTT = XROWS // 128
    NTM = T // 128
    NC = n_chunks
    TS_ = _spans(T)
    NS = len(TS_)
    taps = set(taps)

    nc = bass.Bass("TRN2", target_bir_lowering=False)

    x_core = nc.dram_tensor("x_core", [XROWS, D_MODEL], F32, kind="ExternalInput").ap()
    wip = nc.dram_tensor("wip", [EP, 128, KD, 128], F32, kind="ExternalInput").ap()
    wout = nc.dram_tensor("wout", [16, 128, KI, 128], F32, kind="ExternalInput").ap()
    wg = nc.dram_tensor("wg", [KH, 128, KD, 128], F32, kind="ExternalInput").ap()
    wu = nc.dram_tensor("wu", [KH, 128, KD, 128], F32, kind="ExternalInput").ap()
    wd = nc.dram_tensor("wd", [16, 128, KH, 128], F32, kind="ExternalInput").ap()
    conv_w = nc.dram_tensor("conv_w", [128, N_XBC * 4], F32, kind="ExternalInput").ap()
    conv_b = nc.dram_tensor("conv_b", [128, N_XBC], F32, kind="ExternalInput").ap()
    norm_w = nc.dram_tensor("norm_w", [128, 32], F32, kind="ExternalInput").ap()
    dtb = nc.dram_tensor("dtb", [64, 1], F32, kind="ExternalInput").ap()
    alog = nc.dram_tensor("alog", [64, 1], F32, kind="ExternalInput").ap()
    dvec = nc.dram_tensor("dvec", [1, 64], F32, kind="ExternalInput").ap()
    ggate = nc.dram_tensor("ggate", [128, 1], F32, kind="ExternalInput").ap()

    out = nc.dram_tensor("out", [T, D_MODEL], F32, kind="ExternalOutput").ap()
    x1_scr = nc.dram_tensor("x1_scr", [T, D_MODEL], F32, kind="Internal").ap()
    ag_in = nc.dram_tensor("ag_in", [128, NHEADS * 64], F32, kind="Internal").ap()
    ag_out = nc.dram_tensor("ag_out", [256, NHEADS * 64], F32, kind="Internal").ap()

    tap_t = {}
    for name, shape in (("xln", [KD * 128, XROWS]), ("xbc", [N_XBC * 128, T]),
                        ("dt", [64, T]), ("cs", [64, T]), ("yfm", [D_INNER, T]),
                        ("x1", [T, D_MODEL]), ("q", [D_MODEL, T]), ("pfm", [H_FFN, T])):
        if name in taps:
            tap_t[name] = nc.dram_tensor("tap_" + name, shape, F32, kind="ExternalOutput").ap()

    with ExitStack() as ctx:
        block = ctx.enter_context(nc.Block())
        b = PB(nc, block, ctx)

        tapbuf = b.sb("tapbuf", [128, XROWS if tap_t else 1], F32)

        def tap_fm(name, tiles, cols):
            if name not in tap_t:
                return
            dst = tap_t[name]
            buf = tapbuf
            for i, tile in enumerate(tiles):
                def v(v, i=i, tile=tile):
                    b.wait(v, "T"); b.wait(v, "S"); b.wait(v, "dT")
                    b.step(v, v.tensor_copy(buf[:, 0:cols], tile[:, 0:cols]), "V")
                b.V(v)
                def g(g, i=i):
                    b.wait(g, "V")
                    g.dma_start(out=dst[i * 128:(i + 1) * 128, 0:cols], in_=buf[:, 0:cols]).then_inc(b.sem("dT"), 16)
                    b.n["dT"] += 16
                b.G(g)

        ident_b = b.sb("ident_b", [128, 128], BF16)
        ident_f = b.sb("ident_f", [128, 128], F32)
        ones_col_b = b.sb("ones_col_b", [128, 1], BF16)
        ones_row_f = b.sb("ones_row_f", [1, 128], F32)
        mask01 = b.sb("mask01", [128, 256], F32)
        cw_sb = b.sb("cw_sb", [128, N_XBC * 4], F32)
        cb_sb = b.sb("cb_sb", [128, N_XBC], F32)
        nw_sb = b.sb("nw_sb", [128, 32], F32)
        dtb_sb = b.sb("dtb_sb", [64, 1], F32)
        alog_sb = b.sb("alog_sb", [64, 1], F32)
        expa_sb = b.sb("expa_sb", [64, 1], F32)
        dvec_sb = b.sb("dvec_sb", [1, 64], F32)
        dbc_sb = b.sb("dbc_sb", [128, 64], F32)
        ggate_sb = b.sb("ggate_sb", [128, 1], F32)

        def g0(g):
            for dst, src in ((cw_sb, conv_w), (cb_sb, conv_b), (nw_sb, norm_w),
                             (dtb_sb, dtb), (alog_sb, alog), (dvec_sb, dvec),
                             (ggate_sb, ggate)):
                g.dma_start(out=dst[:], in_=src).then_inc(b.sem("dC"), 16)
                b.n["dC"] += 16
            for tile, val in ((ident_b, 0.0), (ident_f, 0.0), (ones_col_b, 1.0),
                              (ones_row_f, 1.0), (mask01, 1.0)):
                b.inc(g.memset(tile[:], val), "G")
            b.wait(g, "G")
            b.inc(g.affine_select(out=ident_b[:], in_=ident_b[:], compare_op=ALU.not_equal,
                                  fill=1.0, base=0, pattern=[[-1, 128]], channel_multiplier=1), "G")
            b.inc(g.affine_select(out=ident_f[:], in_=ident_f[:], compare_op=ALU.not_equal,
                                  fill=1.0, base=0, pattern=[[-1, 128]], channel_multiplier=1), "G")
            b.inc(g.affine_select(out=mask01[:], in_=mask01[:], compare_op=ALU.is_ge,
                                  fill=0.0, base=0, pattern=[[1, 256]], channel_multiplier=-1), "G")
        b.G(g0)

        def s0(s):
            b.wait(s, "dC")
            b.inc(s.activation(expa_sb[:], alog_sb[:], AF.Exp), "S")
        b.S(s0)

        with ExitStack() as c0:
            dbc_ps = b.ps("dbc_ps", [128, 64], F32, c0)
            def t0(t):
                b.wait(t, "dC")
                b.wait(t, "G")
                b.inc(t.matmul(dbc_ps[:], ones_row_f[:], dvec_sb[:], start=True, stop=True), "T")
            b.T(t0)
            def v0(v):
                b.wait(v, "T")
                b.inc(v.tensor_copy(dbc_sb[:], dbc_ps[:]), "V")
            b.V(v0)


        def ln_pass(prefix, xln_fm):
            b.barrier()
            with ExitStack() as pc:
                xt = [b.sb(f"{prefix}_xt{i}", [128, D_MODEL], F32, pc) for i in range(2)]
                xlt = [b.sb(f"{prefix}_xlt{i}", [128, D_MODEL], BF16, pc) for i in range(2)]
                sq = b.sb(f"{prefix}_sq", [128, D_MODEL], F32, pc)
                sqs = b.sb(f"{prefix}_sqs", [128, D_MODEL], F32, pc)
                stat = b.sb(f"{prefix}_stat", [128, 8], F32, pc)
                trp = [b.ps(f"{prefix}_trp{i}", [128, 128], BF16, pc) for i in range(2)]
                xt_free = [0] * NTT
                xlt_free = [0] * NTT
                ev_done = [0, 0]
                for tt in range(NTT):
                    buf = tt % 2
                    dn = f"{prefix}_ld{buf}"
                    def g(g, tt=tt, buf=buf, dn=dn):
                        if tt >= 2:
                            b.wait(g, "V", xt_free[tt - 2])
                        g.dma_start(out=xt[buf][:], in_=x_core[tt * 128:(tt + 1) * 128, :]).then_inc(b.sem(dn), 16)
                        b.n[dn] = b.n.get(dn, 0) + 16
                    b.G(g)
                    def v(v, tt=tt, buf=buf, dn=dn, dv=b.n[dn]):
                        b.wait(v, dn, dv)
                        b.step(v, v.reduce_sum(stat[:, 0:1], xt[buf][:], axis=AX.X), "V")
                        b.step(v, v.tensor_scalar(stat[:, 1:2], stat[:, 0:1], 1.0 / D_MODEL, None, ALU.mult), "V")
                        xt_free[tt] = b.step(v, v.tensor_scalar(sq[:], xt[buf][:], stat[:, 1:2], None, ALU.subtract), "V")
                    b.V(v)
                    def s(s):
                        b.wait(s, "V")
                        b.inc(s.activation(sqs[:], sq[:], AF.Square, accum_out=stat[:, 2:3]), "S")
                    b.S(s)
                    def v2(v, sv=b.n["S"]):
                        b.wait(v, "S", sv)
                        b.step(v, v.tensor_scalar(stat[:, 3:4], stat[:, 2:3], 1.0 / D_MODEL, 1e-5, ALU.mult, ALU.add), "V")
                        b.inc(v.reciprocal(stat[:, 4:5], stat[:, 3:4]), "V")
                    b.V(v2)
                    def s2(s, vv=b.n["V"]):
                        b.wait(s, "V", vv)
                        b.inc(s.activation(stat[:, 5:6], stat[:, 4:5], AF.Sqrt), "S")
                    b.S(s2)
                    def v3(v, tt=tt, buf=buf, sv=b.n["S"]):
                        b.wait(v, "S", sv)
                        if tt >= 2:
                            b.wait(v, "T", xlt_free[tt - 2])
                        b.inc(v.tensor_scalar(xlt[buf][:], sq[:], stat[:, 5:6], None, ALU.mult), "V")
                    b.V(v3)
                    xlv = b.n["V"]
                    for k in range(KD):
                        pbk = k % 2
                        def t(t, tt=tt, buf=buf, k=k, pbk=pbk, xlv=xlv):
                            b.wait(t, "V", xlv)
                            b.wait(t, "S", ev_done[pbk])
                            xlt_free[tt] = b.inc(t.transpose(trp[pbk][:], xlt[buf][:, k * 128:(k + 1) * 128], ident_b[:]), "T")
                        b.T(t)
                        def s3(s, tt=tt, k=k, pbk=pbk, tv=b.n["T"]):
                            b.wait(s, "T", tv)
                            ev_done[pbk] = b.inc(s.copy(xln_fm[k][:, tt * 128:(tt + 1) * 128], trp[pbk][:]), "S")
                        b.S(s3)
                def vf(v):
                    b.wait(v, "S")
                    b.step(v, v.engine_nop(), "V")
                b.V(vf)

        def p2(xln_fm, xbc_s):
            b.barrier()
            with ExitStack() as pc:
                wbuf = [b.sb(f"p2_w{i}", [128, KD * 128], BF16, pc) for i in range(2)]
                mp = [b.ps(f"p2_mp{i}", [128, 512], F32, pc) for i in range(4)]
                hp = b.ps("p2_hp", [128, (N_XBC + 1) * 3], F32, pc)
                xp = b.sb("p2_xp", [128, T + 3], F32, pc)
                dt_sb = b.sb("dt_sb", [64, T], F32, pc)
                da_sb = b.sb("da_sb", [64, T], F32, pc)
                ecs_sb = b.sb("ecs_sb", [64, T], F32, pc)
                dcy_sb = b.sb("dcy_sb", [64, T], F32, pc)
                gate_rst = b.sb("gate_rst", [64, T], F32, pc)
                cacc = b.sb("p2_cacc", [128, T], F32, pc)
                hsb = b.sb("p2_hsb", [128, (N_XBC + 1) * 3], F32, pc)
                k3 = b.sb("p2_k3", [128, N_XBC * 3], F32, pc)
                h6 = b.sb("p2_h6", [128, 6], F32, pc)

                etiles = list(range(E_XBC0, EP))
                wdma = [0, 0]
                wfree = [0, 0]
                mp_mm = [0] * 4
                mp_ev = [0] * 4
                xp_free = [0]

                def gp(g):
                    b.inc(g.memset(gate_rst[:], 1.0), "G")
                    b.wait(g, "G")
                    for c in range(NC):
                        b.inc(g.memset(gate_rst[:, c * CHUNK:c * CHUNK + 1], 0.0), "G")
                    g.dma_start(out=wbuf[0][:], in_=wip[etiles[0]].rearrange("p k m -> p (k m)")).then_inc(b.sem("p2w0"), 16)
                    b.n["p2w0"] = b.n.get("p2w0", 0) + 16
                b.G(gp)
                wdma[0] = b.n["p2w0"]

                for i, e in enumerate(etiles):
                    par = i % 2
                    gi = e - E_XBC0
                    if i + 1 < len(etiles):
                        def g(g, i=i):
                            npar = (i + 1) % 2
                            if i >= 1:
                                b.wait(g, "T", wfree[npar])
                            g.dma_start(out=wbuf[npar][:], in_=wip[etiles[i + 1]].rearrange("p k m -> p (k m)")).then_inc(b.sem(f"p2w{npar}"), 16)
                            b.n[f"p2w{npar}"] = b.n.get(f"p2w{npar}", 0) + 16
                            wdma[npar] = b.n[f"p2w{npar}"]
                        b.G(g)
                    bk0 = 2 * par
                    def t(t, i=i, par=par, bk0=bk0, gi=gi):
                        b.wait(t, f"p2w{par}", wdma[par])
                        for si in range(NS):
                            b.wait(t, "V", mp_ev[bk0 + si])
                        for si, (st, wdt) in enumerate(TS_):
                            for k in range(KD):
                                last = t.matmul(mp[bk0 + si][:, 0:wdt], wbuf[par][:, k * 128:(k + 1) * 128],
                                                xln_fm[k][:, 3 + st:3 + st + wdt],
                                                start=(k == 0), stop=(k == KD - 1))
                            mp_mm[bk0 + si] = b.inc(last, "T")
                        for k in range(KD):
                            last = t.matmul(hp[:, gi * 3:gi * 3 + 3], wbuf[par][:, k * 128:(k + 1) * 128],
                                            xln_fm[k][:, 0:3], start=(k == 0), stop=(k == KD - 1))
                        wfree[par] = b.inc(last, "T")
                    b.T(t)
                    halo_mm = b.n["T"]
                    if gi < N_XBC:
                        def v(v, i=i, bk0=bk0, gi=gi):
                            b.wait(v, "S")
                            b.wait(v, "V", xp_free[0])
                            for si, (st, wdt) in enumerate(TS_):
                                b.wait(v, "T", mp_mm[bk0 + si])
                                mp_ev[bk0 + si] = b.inc(v.tensor_copy(xp[:, 3 + st:3 + st + wdt], mp[bk0 + si][:, 0:wdt]), "V")
                            v.wait_ge(b.sem("V"), b.n["V"])
                            b.step(v, v.tensor_copy(k3[:, gi * 3:gi * 3 + 3], xp[:, 3:6]), "V")
                            w4 = cw_sb[:, gi * 4:gi * 4 + 4]
                            b.step(v, v.tensor_scalar(cacc[:, 3:T], xp[:, 3:T], w4[:, 0:1], None, ALU.mult), "V")
                            b.step(v, v.scalar_tensor_tensor(cacc[:, 3:T], xp[:, 4:T + 1], w4[:, 1:2], cacc[:, 3:T], ALU.mult, ALU.add), "V")
                            b.step(v, v.scalar_tensor_tensor(cacc[:, 3:T], xp[:, 5:T + 2], w4[:, 2:3], cacc[:, 3:T], ALU.mult, ALU.add), "V")
                            xp_free[0] = b.step(v, v.scalar_tensor_tensor(cacc[:, 3:T], xp[:, 6:T + 3], w4[:, 3:4], cacc[:, 3:T], ALU.mult, ALU.add), "V")
                        b.V(v)
                        def s(s, gi=gi, vv=None):
                            b.wait(s, "V", xp_free[0])
                            b.inc(s.activation(xbc_s[gi][:, 3:T], cacc[:, 3:T], AF.Silu,
                                               bias=cb_sb[:, gi:gi + 1]), "S")
                        b.S(s)
                    else:
                        def s(s, bk0=bk0):
                            for si, (st, wdt) in enumerate(TS_):
                                b.wait(s, "T", mp_mm[bk0 + si])
                                b.inc(s.activation(da_sb[0:64, st:st + wdt], mp[bk0 + si][0:64, 0:wdt],
                                                   AF.Exp, bias=dtb_sb[:, 0:1]), "S")
                        b.S(s)
                        def v(v, bk0=bk0, sv=b.n["S"]):
                            b.wait(v, "S", sv)
                            for si in range(NS):
                                mp_ev[bk0 + si] = b.inc(v.engine_nop(), "V")
                        b.V(v)

                def vh(v, hm=b.n["T"]):
                    b.wait(v, "T", hm)
                    b.step(v, v.tensor_copy(hsb[:], hp[:]), "V")
                b.V(vh)
                for gi in range(N_XBC):
                    def v(v, gi=gi):
                        b.wait(v, "S")
                        b.step(v, v.tensor_copy(h6[:, 0:3], hsb[:, gi * 3:gi * 3 + 3]), "V")
                        b.step(v, v.tensor_copy(h6[:, 3:6], k3[:, gi * 3:gi * 3 + 3]), "V")
                        w4 = cw_sb[:, gi * 4:gi * 4 + 4]
                        b.step(v, v.tensor_scalar(cacc[:, 0:3], h6[:, 0:3], w4[:, 0:1], None, ALU.mult), "V")
                        b.step(v, v.scalar_tensor_tensor(cacc[:, 0:3], h6[:, 1:4], w4[:, 1:2], cacc[:, 0:3], ALU.mult, ALU.add), "V")
                        b.step(v, v.scalar_tensor_tensor(cacc[:, 0:3], h6[:, 2:5], w4[:, 2:3], cacc[:, 0:3], ALU.mult, ALU.add), "V")
                        b.step(v, v.scalar_tensor_tensor(cacc[:, 0:3], h6[:, 3:6], w4[:, 3:4], cacc[:, 0:3], ALU.mult, ALU.add), "V")
                    b.V(v)
                    def s(s, gi=gi, vv=b.n["V"]):
                        b.wait(s, "V", vv)
                        b.inc(s.activation(xbc_s[gi][:, 0:3], cacc[:, 0:3], AF.Silu,
                                           bias=cb_sb[:, gi:gi + 1]), "S")
                    b.S(s)

                def sd(s):
                    b.inc(s.activation(dt_sb[0:64, :], da_sb[0:64, :], AF.Ln, bias=1.0), "S")
                b.S(sd)
                def vd(v, sv=b.n["S"]):
                    b.wait(v, "S", sv)
                    b.wait(v, "G")
                    b.step(v, v.tensor_scalar(da_sb[0:64, :], dt_sb[0:64, :], expa_sb[:, 0:1], -1.0, ALU.mult, ALU.mult), "V")
                    b.step(v, v.tensor_tensor_scan(cs_sb[0:64, :], gate_rst[:, :], da_sb[0:64, :], 0.0, ALU.mult, ALU.add), "V")
                b.V(vd)
                def se(s, vv=b.n["V"]):
                    b.wait(s, "V", vv)
                    b.inc(s.activation(ecs_sb[0:64, :], cs_sb[0:64, :], AF.Exp), "S")
                    for c in range(NC):
                        sl = slice(c * CHUNK, (c + 1) * CHUNK)
                        b.inc(s.activation(dcy_sb[0:64, sl], cs_sb[0:64, sl], AF.Exp,
                                           bias=cs_sb[:, (c + 1) * CHUNK - 1:(c + 1) * CHUNK], scale=-1.0), "S")
                b.S(se)
                dchain = b.n["S"]

                for nm, src in (("dt", dt_sb), ("cs", cs_sb)):
                    if nm in tap_t:
                        def gd(g, nm=nm, src=src):
                            b.wait(g, "V"); b.wait(g, "S")
                            g.dma_start(out=tap_t[nm][:, :], in_=src[0:64, :]).then_inc(b.sem("dT"), 16)
                            b.n["dT"] += 16
                        b.G(gd)
                trf = [b.ps(f"p2_trf{i}", [128, 64], F32, pc) for i in range(2)]
                ev = [0, 0]
                ntr = 0
                for c in range(NC):
                    for lt in range(2):
                        col = (c * 2 + lt) * 64
                        sl = slice(c * CHUNK + lt * 128, c * CHUNK + (lt + 1) * 128)
                        for src, dst in ((cs_sb, cs_tok), (ecs_sb, ecs_tok),
                                         (dt_sb, dt_tok), (dcy_sb, dcy_tok)):
                            pbk = ntr % 2
                            def t(t, src=src, sl=sl, pbk=pbk):
                                b.wait(t, "S", dchain)
                                b.wait(t, "V", ev[pbk])
                                b.inc(t.transpose(trf[pbk][:, 0:64], src[0:64, sl], ident_f[0:64, 0:64]), "T")
                            b.T(t)
                            def v(v, dst=dst, col=col, pbk=pbk, tv=b.n["T"]):
                                b.wait(v, "T", tv)
                                ev[pbk] = b.inc(v.tensor_copy(dst[:, col:col + 64], trf[pbk][:, 0:64]), "V")
                            b.V(v)
                            ntr += 1
                    def vt(v, c=c):
                        b.wait(v, "S", dchain)
                        b.step(v, v.tensor_copy(t4_st[:, c:c + 1], cs_sb[0:64, (c + 1) * CHUNK - 1:(c + 1) * CHUNK]), "V")
                    b.V(vt)
                pbk = ntr % 2
                def tt4(t, pbk=pbk):
                    b.wait(t, "V")
                    b.wait(t, "V", ev[pbk])
                    b.inc(t.transpose(trf[pbk][0:NC, 0:64], t4_st[0:64, 0:NC], ident_f[0:64, 0:64]), "T")
                b.T(tt4)
                t4sb = b.sb("p2_t4sb", [NC, 64], F32, pc)
                def vt4(v, pbk=pbk, tv=b.n["T"]):
                    b.wait(v, "T", tv)
                    ev[pbk] = b.inc(v.tensor_copy(t4sb[:], trf[pbk][0:NC, 0:64]), "V")
                b.V(vt4)
                def gt4(g, vv=b.n["V"]):
                    b.wait(g, "V", vv)
                    for c in range(NC):
                        g.dma_start(out=et_row[0:1, c * 64:(c + 1) * 64], in_=t4sb[c:c + 1, :]).then_inc(b.sem("dS"), 16)
                        b.n["dS"] += 16
                b.G(gt4)
                def st4(s, dv=b.n["dS"]):
                    b.wait(s, "dS", dv)
                    b.inc(s.activation(et_row[0:1, :], et_row[0:1, :], AF.Exp), "S")
                b.S(st4)
                def vcp(v, sv=b.n["S"]):
                    b.wait(v, "S", sv)
                    b.step(v, v.memset(etc_row[0:1, 0:64], 1.0), "V")
                    for c in range(1, NC):
                        b.step(v, v.tensor_tensor(etc_row[0:1, c * 64:(c + 1) * 64],
                                                  etc_row[0:1, (c - 1) * 64:c * 64],
                                                  et_row[0:1, (c - 1) * 64:c * 64], ALU.mult), "V")
                b.V(vcp)
                bcp = b.ps("p2_bcp", [128, NC * 64], F32, pc)
                def tbc(t, vv=b.n["V"]):
                    b.wait(t, "V", vv)
                    b.inc(t.matmul(bcp[:], ones_row_f[:], et_row[0:1, :], start=True, stop=True), "T")
                b.T(tbc)
                def vbc(v, tv=b.n["T"]):
                    b.wait(v, "T", tv)
                    b.step(v, v.tensor_copy(et_bc[:], bcp[:]), "V")
                b.V(vbc)
                def tbc2(t, vv=b.n["V"]):
                    b.wait(t, "V", vv)
                    b.inc(t.matmul(bcp[:], ones_row_f[:], etc_row[0:1, :], start=True, stop=True), "T")
                b.T(tbc2)
                def vbc2(v, tv=b.n["T"]):
                    b.wait(v, "T", tv)
                    b.step(v, v.tensor_copy(etc_bc[:], bcp[:]), "V")
                b.V(vbc2)
                def vfin(v):
                    b.wait(v, "S")
                    b.wait(v, "T")
                    b.step(v, v.engine_nop(), "V")
                b.V(vfin)

        def p3(xbc_s, yfm, S_sb):
            b.barrier()
            with ExitStack() as pc:
                xstk = [b.sb(f"p3_xs{st}", [128, D_INNER], BF16, pc) for st in range(2)]
                Btk = [b.sb(f"p3_B{st}", [128, 128], BF16, pc) for st in range(2)]
                Xh = b.sb("p3_Xh", [128, 128], BF16, pc)
                Bd = b.sb("p3_Bd", [128, 256], BF16, pc)
                GmA = b.sb("p3_GmA", [128, 256], F32, pc)
                GmB = b.sb("p3_GmB", [128, 128], F32, pc)
                Emin = b.sb("p3_Emin", [128, 384], F32, pc)
                Eexp = b.sb("p3_Eexp", [128, 384], F32, pc)
                Mm = [b.sb(f"p3_M{i}", [128, 384], BF16, pc) for i in range(2)]
                Sbf = b.sb("p3_Sbf", [128, 64], BF16, pc)
                stg = b.sb("p3_stg", [128, 256], BF16, pc)
                oh2 = [b.sb(f"p3_oh{i}", [64, 8 * 128], F32, pc) for i in range(2)]
                Eps = [b.ps(f"p3_E{i}", [128, 384], F32, pc) for i in range(2)]
                Yps = [b.ps(f"p3_Y{i}", [128, 256], F32, pc) for i in range(2)]
                Ops = [b.ps(f"p3_O{i}", [128, 256], F32, pc) for i in range(2)]
                Sps = b.ps("p3_S", [128, 128], F32, pc)
                Trp = b.ps("p3_Tr", [128, 512], BF16, pc)

                def gz(g):
                    b.inc(g.memset(S_sb[:], 0.0), "G")
                b.G(gz)

                st_ = {
                    "E_mm": [0, 0], "E_ev": [0, 0],
                    "Y_mm": [0, 0], "Y_ev": [0, 0],
                    "O_mm": [0, 0], "O_ev": [0, 0],
                    "S_mm": 0, "S_ev": 0, "Sbf_rd": 0,
                    "Tr_ev": 0, "stg_rd": 0, "Xh_rd": 0, "Bd_rd": 0,
                    "yadd": 0, "oh_rd": [0, 0], "oh_gen": [0, 0],
                }

                def chunk_start(c):
                    for stt in range(2):
                        sl = slice(c * CHUNK + stt * 128, c * CHUNK + (stt + 1) * 128)
                        for grp in range(8):
                            def t(t, stt=stt, grp=grp, sl=sl):
                                b.wait(t, "S", st_["Tr_ev"])
                                b.wait(t, "S")
                                for j in range(4):
                                    last = t.transpose(Trp[:, j * 128:(j + 1) * 128], xbc_s[grp * 4 + j][:, sl], ident_b[:])
                                b.inc(last, "T")
                            b.T(t)
                            def s(s, stt=stt, grp=grp, tv=b.n["T"]):
                                b.wait(s, "T", tv)
                                st_["Tr_ev"] = b.inc(s.copy(xstk[stt][:, grp * 512:(grp + 1) * 512], Trp[:]), "S")
                            b.S(s)
                        def t2(t, stt=stt, sl=sl):
                            b.wait(t, "S", st_["Tr_ev"])
                            b.inc(t.transpose(Trp[:, 0:128], xbc_s[32][:, sl], ident_b[:]), "T")
                        b.T(t2)
                        def s2(s, stt=stt, tv=b.n["T"]):
                            b.wait(s, "T", tv)
                            st_["Tr_ev"] = b.inc(s.copy(Btk[stt][:], Trp[:, 0:128]), "S")
                        b.S(s2)
                    def tg(t, c=c):
                        b.wait(t, "V", st_["E_ev"][0]); b.wait(t, "V", st_["E_ev"][1])
                        b.wait(t, "S", st_["Tr_ev"])
                        last = t.matmul(Eps[0][:, 0:256], xbc_s[32][:, c * CHUNK:c * CHUNK + 128],
                                        xbc_s[33][:, c * CHUNK:(c + 1) * CHUNK], start=True, stop=True)
                        b.inc(last, "T")
                        last = t.matmul(Eps[1][:, 0:128], xbc_s[32][:, c * CHUNK + 128:(c + 1) * CHUNK],
                                        xbc_s[33][:, c * CHUNK + 128:(c + 1) * CHUNK], start=True, stop=True)
                        b.inc(last, "T")
                    b.T(tg)
                    def vg(v, tv=b.n["T"]):
                        b.wait(v, "T", tv)
                        b.step(v, v.tensor_tensor(GmA[:], Eps[0][:, 0:256], mask01[:], ALU.mult), "V")
                        b.step(v, v.tensor_tensor(GmB[:], Eps[1][:, 0:128], mask01[:, 0:128], ALU.mult), "V")
                        st_["E_ev"][0] = st_["E_ev"][1] = b.n["V"]
                    b.V(vg)

                def head_local(c, h):
                    pair, par = h // 2, h % 2
                    ep = h % 2
                    yp = pair % 2
                    ccol = c * CHUNK
                    hs = slice(h * 64, (h + 1) * 64)
                    yc = par * 64
                    ct0 = (c * 2 + 0) * 64 + h
                    ct1 = (c * 2 + 1) * 64 + h

                    if h % 8 == 0:
                        gb = (h // 8) % 2
                        def goh(g, h=h, gb=gb):
                            b.wait(g, "T", st_["oh_rd"][gb])
                            b.inc(g.memset(oh2[gb][:], 1.0), "G")
                            b.wait(g, "G")
                            oh3 = oh2[gb].ap().rearrange("k (hh j) -> k hh j", j=128)
                            b.inc(g.affine_select(out=oh3, in_=oh3, compare_op=ALU.is_equal,
                                                  fill=0.0, base=-h, pattern=[[-1, 8], [0, 128]],
                                                  channel_multiplier=1), "G")
                        b.G(goh)
                        st_["oh_gen"][gb] = b.n["G"]

                    def t(t, c=c, h=h, ep=ep):
                        gb = (h // 8) % 2
                        b.wait(t, "V", st_["E_ev"][ep])
                        b.wait(t, "G", st_["oh_gen"][gb])
                        last = t.matmul(Eps[ep][:, 0:256], oh2[gb][:, (h % 8) * 128:(h % 8 + 1) * 128],
                                        cs_sb[0:64, ccol:ccol + CHUNK], start=True, stop=True)
                        st_["E_mm"][ep] = b.inc(last, "T")
                        st_["oh_rd"][gb] = st_["E_mm"][ep]
                    b.T(t)
                    def v(v, ep=ep, ct0=ct0, ct1=ct1):
                        b.wait(v, "T", st_["E_mm"][ep])
                        b.step(v, v.tensor_scalar(Emin[:, 0:256], Eps[ep][:, 0:256],
                                                  cs_tok[:, ct0:ct0 + 1], 0.0, ALU.subtract, ALU.min), "V")
                        st_["E_ev"][ep] = b.step(v, v.tensor_scalar(Emin[:, 256:384], Eps[ep][:, 128:256],
                                                                    cs_tok[:, ct1:ct1 + 1], 0.0, ALU.subtract, ALU.min), "V")
                    b.V(v)
                    def s(s, vv=b.n["V"]):
                        b.wait(s, "V", vv)
                        b.inc(s.activation(Eexp[:, 0:384], Emin[:, 0:384], AF.Exp), "S")
                    b.S(s)
                    mi = h % 2
                    def v2(v, mi=mi, sv=b.n["S"]):
                        b.wait(v, "S", sv)
                        b.wait(v, "T", st_["M_rd"][mi] if "M_rd" in st_ else 0)
                        b.step(v, v.tensor_tensor(Mm[mi][:, 0:256], GmA[:], Eexp[:, 0:256], ALU.mult), "V")
                        b.step(v, v.tensor_tensor(Mm[mi][:, 256:384], GmB[:], Eexp[:, 256:384], ALU.mult), "V")
                    b.V(v2)
                    mv = b.n["V"]
                    def v3(v, c=c, h=h, hs=hs, ct0=ct0, ct1=ct1):
                        b.wait(v, "T", st_["Xh_rd"])
                        b.inc(v.tensor_scalar(Xh[:, 0:64], xstk[0][:, hs], dt_tok[:, ct0:ct0 + 1], None, ALU.mult), "V")
                        b.inc(v.tensor_scalar(Xh[:, 64:128], xstk[1][:, hs], dt_tok[:, ct1:ct1 + 1], None, ALU.mult), "V")
                        b.wait(v, "T", st_["Bd_rd"])
                        b.inc(v.tensor_scalar(Bd[:, 0:128], Btk[0][:], dcy_tok[:, ct0:ct0 + 1], None, ALU.mult), "V")
                        b.inc(v.tensor_scalar(Bd[:, 128:256], Btk[1][:], dcy_tok[:, ct1:ct1 + 1], None, ALU.mult), "V")
                        if c > 0:
                            b.wait(v, "T", st_["Sbf_rd"])
                            b.inc(v.tensor_copy(Sbf[:], S_sb[:, hs]), "V")
                    b.V(v3)
                    opv = b.n["V"]
                    def t2(t, c=c, h=h, mi=mi, yp=yp, yc=yc, hs=hs, mv=mv, opv=opv, ccol=ccol):
                        b.wait(t, "V", mv)
                        b.wait(t, "V", opv)
                        b.wait(t, "V", st_["Y_ev"][yp])
                        b.wait(t, "V", st_["O_ev"][yp])
                        t.matmul(Yps[yp][:, yc:yc + 64], Mm[mi][:, 0:128], Xh[:, 0:64], start=True, stop=True)
                        t.matmul(Yps[yp][:, 128 + yc:192 + yc], Mm[mi][:, 128:256], Xh[:, 0:64], start=True, stop=False)
                        last = t.matmul(Yps[yp][:, 128 + yc:192 + yc], Mm[mi][:, 256:384], Xh[:, 64:128], start=False, stop=True)
                        st_["M_rd"][mi] = b.inc(last, "T")
                        st_["Y_mm"][yp] = st_["M_rd"][mi]
                        b.wait(t, "V", st_["S_ev"])
                        t.matmul(Sps[:, yc:yc + 64], Bd[:, 0:128], Xh[:, 0:64], start=True, stop=False)
                        last = t.matmul(Sps[:, yc:yc + 64], Bd[:, 128:256], Xh[:, 64:128], start=False, stop=True)
                        st_["S_mm"] = b.inc(last, "T")
                        st_["Xh_rd"] = st_["S_mm"]
                        st_["Bd_rd"] = st_["S_mm"]
                        if c > 0:
                            t.matmul(Ops[yp][:, yc:yc + 64], xbc_s[33][:, ccol:ccol + 128], Sbf[:], start=True, stop=True)
                            last = t.matmul(Ops[yp][:, 128 + yc:192 + yc], xbc_s[33][:, ccol + 128:ccol + 256], Sbf[:], start=True, stop=True)
                            st_["Sbf_rd"] = b.inc(last, "T")
                            st_["O_mm"][yp] = st_["Sbf_rd"]
                    b.T(t2)
                    def v5(v, c=c, h=h, yp=yp, yc=yc, hs=hs, ct0=ct0, ct1=ct1, par=par):
                        b.wait(v, "T", st_["Y_mm"][yp])
                        if par == 0:
                            b.wait(v, "T", st_["stg_rd"])
                        for lt, ct in ((0, ct0), (1, ct1)):
                            dst = stg[:, lt * 128 + yc:lt * 128 + yc + 64]
                            b.inc(v.scalar_tensor_tensor(dst, xstk[lt][:, hs], dbc_sb[:, h:h + 1],
                                                         Yps[yp][:, lt * 128 + yc:lt * 128 + yc + 64],
                                                         ALU.mult, ALU.add), "V")
                        st_["Y_ev"][yp] = b.n["V"]
                        if c > 0:
                            b.wait(v, "T", st_["O_mm"][yp])
                            v.wait_ge(b.sem("V"), b.n["V"])
                            for lt, ct in ((0, ct0), (1, ct1)):
                                dst = stg[:, lt * 128 + yc:lt * 128 + yc + 64]
                                b.inc(v.scalar_tensor_tensor(dst, Ops[yp][:, lt * 128 + yc:lt * 128 + yc + 64],
                                                             ecs_tok[:, ct:ct + 1], dst, ALU.mult, ALU.add), "V")
                            st_["O_ev"][yp] = b.n["V"]
                        b.wait(v, "T", st_["S_mm"])
                        v.wait_ge(b.sem("V"), b.n["V"])
                        b.inc(v.scalar_tensor_tensor(S_sb[:, hs], S_sb[:, hs], et_bc[:, c * 64 + h:c * 64 + h + 1],
                                                     Sps[:, yc:yc + 64], ALU.mult, ALU.add), "V")
                        st_["S_ev"] = b.n["V"]
                    b.V(v5)
                    return b.n["V"]

                def head_corr(c, h):
                    pair, par = h // 2, h % 2
                    yp = pair % 2
                    ccol = c * CHUNK
                    hs = slice(h * 64, (h + 1) * 64)
                    yc = par * 64
                    ct0 = (c * 2 + 0) * 64 + h
                    ct1 = (c * 2 + 1) * 64 + h
                    def v6(v, c=c, h=h, hs=hs):
                        b.wait(v, "T", st_["Sbf_rd"])
                        b.inc(v.tensor_scalar(Sbf[:], S_sb[:, hs], etc_bc[:, c * 64 + h:c * 64 + h + 1], None, ALU.mult), "V")
                    b.V(v6)
                    def t3(t, yp=yp, yc=yc, ccol=ccol, vv=b.n["V"]):
                        b.wait(t, "V", vv)
                        b.wait(t, "V", st_["O_ev"][yp])
                        t.matmul(Ops[yp][:, yc:yc + 64], xbc_s[33][:, ccol:ccol + 128], Sbf[:], start=True, stop=True)
                        last = t.matmul(Ops[yp][:, 128 + yc:192 + yc], xbc_s[33][:, ccol + 128:ccol + 256], Sbf[:], start=True, stop=True)
                        st_["Sbf_rd"] = b.inc(last, "T")
                        st_["O_mm"][yp] = st_["Sbf_rd"]
                    b.T(t3)
                    def v7(v, yp=yp, yc=yc, ct0=ct0, ct1=ct1, par=par):
                        b.wait(v, "T", st_["O_mm"][yp])
                        if par == 0:
                            b.wait(v, "T", st_["stg_rd"])
                        for lt, ct in ((0, ct0), (1, ct1)):
                            b.inc(v.tensor_scalar(stg[:, lt * 128 + yc:lt * 128 + yc + 64],
                                                  Ops[yp][:, lt * 128 + yc:lt * 128 + yc + 64],
                                                  ecs_tok[:, ct:ct + 1], None, ALU.mult), "V")
                        st_["O_ev"][yp] = b.n["V"]
                    b.V(v7)
                    return b.n["V"]

                def pair_out(c, pair, stv, correction):
                    def t4(t, c=c, stv=stv, correction=correction):
                        b.wait(t, "V", stv)
                        b.wait(t, "S", st_["Tr_ev"])
                        if correction:
                            b.wait(t, "V", st_["yadd"])
                        for lt in range(2):
                            last = t.transpose(Trp[:, lt * 128:(lt + 1) * 128], stg[:, lt * 128:(lt + 1) * 128], ident_b[:])
                        st_["stg_rd"] = b.inc(last, "T")
                    b.T(t4)
                    tv = b.n["T"]
                    if not correction:
                        def s4(s, c=c, pair=pair, tv=tv):
                            b.wait(s, "T", tv)
                            for lt in range(2):
                                b.inc(s.copy(yfm[pair][:, c * CHUNK + lt * 128:c * CHUNK + (lt + 1) * 128],
                                             Trp[:, lt * 128:(lt + 1) * 128]), "S")
                            st_["Tr_ev"] = b.n["S"]
                        b.S(s4)
                    else:
                        def v8(v, c=c, pair=pair, tv=tv):
                            b.wait(v, "T", tv)
                            for lt in range(2):
                                sl = slice(c * CHUNK + lt * 128, c * CHUNK + (lt + 1) * 128)
                                b.inc(v.tensor_tensor(yfm[pair][:, sl], yfm[pair][:, sl],
                                                      Trp[:, lt * 128:(lt + 1) * 128], ALU.add), "V")
                            st_["yadd"] = b.n["V"]
                        b.V(v8)
                        def s5(s, vv=b.n["V"]):
                            b.wait(s, "V", vv)
                            st_["Tr_ev"] = b.inc(s.nop(), "S")
                        b.S(s5)

                st_["M_rd"] = [0, 0]
                for c in range(NC):
                    chunk_start(c)
                    for h in range(NHEADS):
                        stv = head_local(c, h)
                        if h % 2 == 1:
                            pair_out(c, h // 2, stv, False)

                def gx(g):
                    b.wait(g, "V")
                    g.dma_start(out=ag_in[:], in_=S_sb[:]).then_inc(b.sem("dS"), 16)
                    b.n["dS"] += 16
                    b.wait(g, "dS")
                    g.collective_compute(
                        "AllGather", ALU.bypass,
                        replica_groups=[[2 * i, 2 * i + 1] for i in range(max(1, n_cores // 2))],
                        ins=[ag_in.opt()], outs=[ag_out.opt()],
                    ).then_inc(b.sem("CC"), 1)
                    b.n["CC"] += 1
                    b.wait(g, "CC")
                    g.dma_start(out=S_sb[:], in_=ag_out[0:128, :]).then_inc(b.sem("dS"), 16)
                    b.n["dS"] += 16
                b.G(gx)
                def vgate(v, dv=b.n["dS"]):
                    b.wait(v, "dS", dv)
                    b.step(v, v.tensor_scalar(S_sb[:], S_sb[:], ggate_sb[:, 0:1], None, ALU.mult), "V")
                b.V(vgate)

                for c in range(NC):
                    for h in range(NHEADS):
                        stv = head_corr(c, h)
                        if h % 2 == 1:
                            pair_out(c, h // 2, stv, True)

                def vfin(v):
                    b.wait(v, "T"); b.wait(v, "S")
                    b.step(v, v.engine_nop(), "V")
                b.V(vfin)

        def p4a(xln_fm, yfm, ofm):
            b.barrier()
            with ExitStack() as pc:
                wob = [b.sb(f"p4_wo{i}", [128, KI * 128], BF16, pc, side="right") for i in range(2)]
                mp = [b.ps(f"p4_mp{i}", [128, 512], F32, pc) for i in range(4)]
                vps = [b.ps(f"p4_vp{si}", [1, 512], F32, pc) for si in range(NS)]
                zs = b.sb("p4_zs", [128, 512], F32, pc, side="right")
                sqb = b.sb("p4_sqb", [128, 512], BF16, pc, side="right")
                rrow = b.sb("p4_rrow", [1, T], F32, pc, side="right")
                rbc = b.sb("p4_rbc", [128, T], F32, pc, side="right")
                wdma = [0, 0]; wfree = [0, 0]
                mp_mm = [0] * 4; mp_ev = [0] * 4
                vps_mm = 0

                def gp(g):
                    g.dma_start(out=wob[0][:, 0:KD * 128], in_=wip[0].rearrange("p k m -> p (k m)")).then_inc(b.sem("p4z0"), 16)
                    b.n["p4z0"] = b.n.get("p4z0", 0) + 16
                b.G(gp)
                wdma[0] = b.n["p4z0"]

                for zt in range(32):
                    par = zt % 2
                    bk0 = 2 * par
                    if zt + 1 < 32:
                        def g(g, zt=zt):
                            npar = (zt + 1) % 2
                            if zt >= 1:
                                b.wait(g, "T", wfree[npar])
                            g.dma_start(out=wob[npar][:, 0:KD * 128], in_=wip[zt + 1].rearrange("p k m -> p (k m)")).then_inc(b.sem(f"p4z{npar}"), 16)
                            b.n[f"p4z{npar}"] = b.n.get(f"p4z{npar}", 0) + 16
                            wdma[npar] = b.n[f"p4z{npar}"]
                        b.G(g)
                    def t(t, zt=zt, par=par, bk0=bk0):
                        b.wait(t, f"p4z{par}", wdma[par])
                        for si in range(NS):
                            b.wait(t, "V", mp_ev[bk0 + si])
                        for si, (st, wdt) in enumerate(TS_):
                            for k in range(KD):
                                last = t.matmul(mp[bk0 + si][:, 0:wdt], wob[par][:, k * 128:(k + 1) * 128],
                                                xln_fm[k][:, 3 + st:3 + st + wdt],
                                                start=(k == 0), stop=(k == KD - 1))
                            mp_mm[bk0 + si] = b.inc(last, "T")
                        wfree[par] = b.n["T"]
                    b.T(t)
                    for si, (st, wdt) in enumerate(TS_):
                        def s(s, bk0=bk0, si=si, wdt=wdt):
                            b.wait(s, "T", mp_mm[bk0 + si])
                            b.inc(s.activation(zs[:, 0:wdt], mp[bk0 + si][:, 0:wdt], AF.Silu), "S")
                        b.S(s)
                        def v(v, zt=zt, bk0=bk0, si=si, st=st, wdt=wdt, sv=b.n["S"]):
                            b.wait(v, "S", sv)
                            mp_ev[bk0 + si] = b.step(v, v.tensor_tensor(yfm[zt][:, st:st + wdt], yfm[zt][:, st:st + wdt],
                                                                        zs[:, 0:wdt], ALU.mult), "V")
                        b.V(v)
                        def s2(s, zt=zt, st=st, wdt=wdt, vv=b.n["V"]):
                            b.wait(s, "V", vv)
                            b.inc(s.activation(sqb[:, 0:wdt], yfm[zt][:, st:st + wdt], AF.Square), "S")
                        b.S(s2)
                        def t2(t, zt=zt, si=si, wdt=wdt, sv=b.n["S"]):
                            nonlocal vps_mm
                            b.wait(t, "S", sv)
                            last = t.matmul(vps[si][0:1, 0:wdt], ones_col_b[:], sqb[:, 0:wdt],
                                            start=(zt == 0), stop=(zt == 31))
                            vps_mm = b.inc(last, "T")
                        b.T(t2)

                def vr(v):
                    b.wait(v, "T", vps_mm)
                    for si, (st, wdt) in enumerate(TS_):
                        b.step(v, v.tensor_scalar(rrow[0:1, st:st + wdt], vps[si][0:1, 0:wdt],
                                                  1.0 / D_INNER, 1e-5, ALU.mult, ALU.add), "V")
                    b.inc(v.reciprocal(rrow[0:1, :], rrow[0:1, :]), "V")
                b.V(vr)
                def sr(s, vv=b.n["V"]):
                    b.wait(s, "V", vv)
                    b.inc(s.activation(rrow[0:1, :], rrow[0:1, :], AF.Sqrt), "S")
                b.S(sr)
                def tr(t, sv=b.n["S"]):
                    b.wait(t, "S", sv)
                    for si, (st, wdt) in enumerate(TS_):
                        b.wait(t, "V", mp_ev[si])
                        last = t.matmul(mp[si][:, 0:wdt], ones_row_f[:], rrow[0:1, st:st + wdt], start=True, stop=True)
                        mp_mm[si] = b.inc(last, "T")
                b.T(tr)
                for si, (st, wdt) in enumerate(TS_):
                    def v(v, si=si, st=st, wdt=wdt):
                        b.wait(v, "T", mp_mm[si])
                        mp_ev[si] = b.step(v, v.tensor_copy(rbc[:, st:st + wdt], mp[si][:, 0:wdt]), "V")
                    b.V(v)
                for p in range(32):
                    def v(v, p=p):
                        b.inc(v.scalar_tensor_tensor(yfm[p][:], yfm[p][:], nw_sb[:, p:p + 1], rbc[:], ALU.mult, ALU.mult), "V")
                    b.V(v)
                ynv = b.n["V"]
                tap_fm("yfm", yfm, T)

                def gp2(g):
                    b.wait(g, "T", wfree[0]); b.wait(g, "T", wfree[1])
                    g.dma_start(out=wob[0][:], in_=wout[0].rearrange("p k m -> p (k m)")).then_inc(b.sem("p4o0"), 16)
                    b.n["p4o0"] = b.n.get("p4o0", 0) + 16
                    wdma[0] = b.n["p4o0"]
                b.G(gp2)
                for dp in range(16):
                    par = dp % 2
                    bk0 = 2 * par
                    if dp + 1 < 16:
                        def g(g, dp=dp):
                            npar = (dp + 1) % 2
                            if dp >= 1:
                                b.wait(g, "T", wfree[npar])
                            g.dma_start(out=wob[npar][:], in_=wout[dp + 1].rearrange("p k m -> p (k m)")).then_inc(b.sem(f"p4o{npar}"), 16)
                            b.n[f"p4o{npar}"] = b.n.get(f"p4o{npar}", 0) + 16
                            wdma[npar] = b.n[f"p4o{npar}"]
                        b.G(g)
                    def t(t, dp=dp, par=par, bk0=bk0):
                        b.wait(t, f"p4o{par}", wdma[par])
                        b.wait(t, "V", ynv)
                        for si in range(NS):
                            b.wait(t, "V", mp_ev[bk0 + si])
                        for si, (st, wdt) in enumerate(TS_):
                            for k in range(KI):
                                last = t.matmul(mp[bk0 + si][:, 0:wdt], wob[par][:, k * 128:(k + 1) * 128],
                                                yfm[k][:, st:st + wdt], start=(k == 0), stop=(k == KI - 1))
                            mp_mm[bk0 + si] = b.inc(last, "T")
                        wfree[par] = b.n["T"]
                    b.T(t)
                    for si, (st, wdt) in enumerate(TS_):
                        def v(v, dp=dp, bk0=bk0, si=si, st=st, wdt=wdt):
                            b.wait(v, "T", mp_mm[bk0 + si])
                            mp_ev[bk0 + si] = b.step(v, v.tensor_copy(ofm[dp][:, st:st + wdt], mp[bk0 + si][:, 0:wdt]), "V")
                        b.V(v)
                def vfin(v):
                    b.wait(v, "T"); b.wait(v, "S")
                    b.step(v, v.engine_nop(), "V")
                b.V(vfin)

        def p4b(ofm, q_fm):
            b.barrier()
            with ExitStack() as pc:
                xm = [b.sb(f"p4b_xm{i}", [128, D_MODEL], F32, pc, side="right") for i in range(2)]
                x1t = b.sb("p4b_x1t", [128, D_MODEL], F32, pc, side="right")
                qtk = b.sb("p4b_qtk", [128, D_MODEL], BF16, pc, side="right")
                st2 = b.sb("p4b_st2", [128, 8], F32, pc, side="right")
                sq2 = b.sb("p4b_sq2", [128, D_MODEL], F32, pc, side="right")
                sq3 = b.sb("p4b_sq3", [128, D_MODEL], F32, pc, side="right")
                trp2 = [b.ps(f"p4b_tp{i}", [128, 128], F32, pc) for i in range(2)]
                trq = [b.ps(f"p4b_tq{i}", [128, 128], BF16, pc) for i in range(2)]
                tr_ev2 = [0, 0]
                trq_ev = [0, 0]
                xm_free = [0] * NTM
                parkd = 0
                for tt in range(NTM):
                    buf = tt % 2
                    def g(g, tt=tt, buf=buf):
                        if tt >= 2:
                            b.wait(g, "V", xm_free[tt - 2])
                        g.dma_start(out=xm[buf][:], in_=x_core[3 + tt * 128:3 + (tt + 1) * 128, :]).then_inc(b.sem(f"p4x{buf}"), 16)
                        b.n[f"p4x{buf}"] = b.n.get(f"p4x{buf}", 0) + 16
                    b.G(g)
                    x0d = b.n[f"p4x{buf}"]
                    for dp in range(16):
                        pbk = dp % 2
                        def t(t, tt=tt, dp=dp, pbk=pbk):
                            b.wait(t, "V", tr_ev2[pbk])
                            b.inc(t.transpose(trp2[pbk][:], ofm[dp][:, tt * 128:(tt + 1) * 128], ident_f[:]), "T")
                        b.T(t)
                        def v(v, tt=tt, dp=dp, pbk=pbk, buf=buf, tv=b.n["T"], x0d=x0d, parkd=parkd):
                            b.wait(v, "T", tv)
                            b.wait(v, f"p4x{buf}", x0d)
                            b.wait(v, "dP", parkd)
                            tr_ev2[pbk] = b.inc(v.tensor_tensor(x1t[:, dp * 128:(dp + 1) * 128], trp2[pbk][:],
                                                                xm[buf][:, dp * 128:(dp + 1) * 128], ALU.add), "V")
                        b.V(v)
                    xm_free[tt] = b.n["V"]
                    x1v = b.n["V"]
                    if "x1" in tap_t:
                        def gx(g, tt=tt, x1v=x1v):
                            b.wait(g, "V", x1v)
                            g.dma_start(out=tap_t["x1"][tt * 128:(tt + 1) * 128, :], in_=x1t[:]).then_inc(b.sem("dP"), 16)
                            b.n["dP"] += 16
                        b.G(gx)
                    def gp(g, tt=tt, x1v=x1v):
                        b.wait(g, "V", x1v)
                        g.dma_start(out=x1_scr[tt * 128:(tt + 1) * 128, :], in_=x1t[:]).then_inc(b.sem("dP"), 16)
                        b.n["dP"] += 16
                    b.G(gp)
                    parkd = b.n["dP"]
                    def v2(v):
                        b.step(v, v.reduce_sum(st2[:, 0:1], x1t[:], axis=AX.X), "V")
                        b.step(v, v.tensor_scalar(st2[:, 1:2], st2[:, 0:1], 1.0 / D_MODEL, None, ALU.mult), "V")
                        b.step(v, v.tensor_scalar(sq2[:], x1t[:], st2[:, 1:2], None, ALU.subtract), "V")
                    b.V(v2)
                    def s2(s, vv=b.n["V"]):
                        b.wait(s, "V", vv)
                        b.inc(s.activation(sq3[:], sq2[:], AF.Square, accum_out=st2[:, 2:3]), "S")
                    b.S(s2)
                    def v3(v, sv=b.n["S"]):
                        b.wait(v, "S", sv)
                        b.step(v, v.tensor_scalar(st2[:, 3:4], st2[:, 2:3], 1.0 / D_MODEL, 1e-5, ALU.mult, ALU.add), "V")
                        b.inc(v.reciprocal(st2[:, 4:5], st2[:, 3:4]), "V")
                    b.V(v3)
                    def s3(s, vv=b.n["V"]):
                        b.wait(s, "V", vv)
                        b.inc(s.activation(st2[:, 5:6], st2[:, 4:5], AF.Sqrt), "S")
                    b.S(s3)
                    def v4(v, sv=b.n["S"]):
                        b.wait(v, "S", sv)
                        b.step(v, v.tensor_scalar(sq2[:], sq2[:], st2[:, 5:6], None, ALU.mult), "V")
                        b.step(v, v.reduce_sum(st2[:, 0:1], sq2[:], axis=AX.X), "V")
                        b.step(v, v.tensor_scalar(st2[:, 1:2], st2[:, 0:1], 1.0 / D_MODEL, None, ALU.mult), "V")
                        b.step(v, v.tensor_scalar(sq2[:], sq2[:], st2[:, 1:2], None, ALU.subtract), "V")
                    b.V(v4)
                    def s4(s, vv=b.n["V"]):
                        b.wait(s, "V", vv)
                        b.inc(s.activation(sq3[:], sq2[:], AF.Square, accum_out=st2[:, 2:3]), "S")
                    b.S(s4)
                    def v5(v, sv=b.n["S"]):
                        b.wait(v, "S", sv)
                        b.step(v, v.tensor_scalar(st2[:, 3:4], st2[:, 2:3], 1.0 / D_MODEL, 1e-8, ALU.mult, ALU.add), "V")
                        b.inc(v.reciprocal(st2[:, 4:5], st2[:, 3:4]), "V")
                    b.V(v5)
                    def s5(s, vv=b.n["V"]):
                        b.wait(s, "V", vv)
                        b.inc(s.activation(st2[:, 5:6], st2[:, 4:5], AF.Sqrt), "S")
                    b.S(s5)
                    def v6(v, sv=b.n["S"]):
                        b.wait(v, "S", sv)
                        b.step(v, v.tensor_scalar(sq2[:], sq2[:], st2[:, 5:6], None, ALU.mult), "V")
                        b.step(v, v.reduce_max(st2[:, 6:7], sq2[:], axis=AX.X, apply_absolute_value=True), "V")
                        b.step(v, v.reciprocal(st2[:, 7:8], st2[:, 6:7]), "V")
                        b.step(v, v.tensor_scalar(sq2[:], sq2[:], st2[:, 7:8], 127.0, ALU.mult, ALU.mult), "V")
                        b.step(v, v.tensor_scalar(sq2[:], sq2[:], MAGIC, None, ALU.add), "V")
                        b.step(v, v.tensor_scalar(sq2[:], sq2[:], MAGIC, None, ALU.subtract), "V")
                        b.step(v, v.tensor_scalar(qtk[:], sq2[:], st2[:, 6:7], 1.0 / 127.0, ALU.mult, ALU.mult), "V")
                    b.V(v6)
                    qv = b.n["V"]
                    for k in range(KD):
                        pbk = k % 2
                        def t2(t, tt=tt, k=k, pbk=pbk, qv=qv):
                            b.wait(t, "V", qv)
                            b.wait(t, "S", trq_ev[pbk])
                            b.inc(t.transpose(trq[pbk][:], qtk[:, k * 128:(k + 1) * 128], ident_b[:]), "T")
                        b.T(t2)
                        def s6(s, tt=tt, k=k, pbk=pbk, tv=b.n["T"]):
                            b.wait(s, "T", tv)
                            trq_ev[pbk] = b.inc(s.copy(q_fm[k][:, tt * 128:(tt + 1) * 128], trq[pbk][:]), "S")
                        b.S(s6)
                    def v7(v, tv=b.n["T"]):
                        b.wait(v, "T", tv)
                        b.step(v, v.engine_nop(), "V")
                    b.V(v7)
                def vfin(v):
                    b.wait(v, "S"); b.wait(v, "T")
                    b.step(v, v.engine_nop(), "V")
                b.V(vfin)
            tap_fm("q", q_fm, T)

        def p5a(q_fm, pfm):
            b.barrier()
            with ExitStack() as pc:
                gwb = [b.sb(f"p5_gw{i}", [128, KD * 128], BF16, pc) for i in range(2)]
                uwb = [b.sb(f"p5_uw{i}", [128, KD * 128], BF16, pc) for i in range(2)]
                gsb = b.sb("p5_gs", [128, 512], F32, pc)
                psg = [b.ps(f"p5_g{i}", [128, 512], F32, pc) for i in range(2)]
                psu = [b.ps(f"p5_u{i}", [128, 512], F32, pc) for i in range(2)]
                wdma = [0, 0]; wfree = [0, 0]
                g_mm = [0, 0]; u_mm = [0, 0]; gu_ev = [0, 0]

                def gp(g):
                    g.dma_start(out=gwb[0][:], in_=wg[0].rearrange("p k m -> p (k m)")).then_inc(b.sem("p5g0"), 16)
                    g.dma_start(out=uwb[0][:], in_=wu[0].rearrange("p k m -> p (k m)")).then_inc(b.sem("p5g0"), 16)
                    b.n["p5g0"] = b.n.get("p5g0", 0) + 32
                b.G(gp)
                wdma[0] = b.n["p5g0"]

                for j in range(KH):
                    par = j % 2
                    if j + 1 < KH:
                        def g(g, j=j):
                            npar = (j + 1) % 2
                            if j >= 1:
                                b.wait(g, "T", wfree[npar])
                            g.dma_start(out=gwb[npar][:], in_=wg[j + 1].rearrange("p k m -> p (k m)")).then_inc(b.sem(f"p5g{npar}"), 16)
                            g.dma_start(out=uwb[npar][:], in_=wu[j + 1].rearrange("p k m -> p (k m)")).then_inc(b.sem(f"p5g{npar}"), 16)
                            b.n[f"p5g{npar}"] = b.n.get(f"p5g{npar}", 0) + 32
                            wdma[npar] = b.n[f"p5g{npar}"]
                        b.G(g)
                    def s(s, j=j, par=par):
                        b.wait(s, f"p5g{par}", wdma[par])
                        if j >= 2:
                            b.wait(s, "T", wfree[par])
                        b.inc(s.activation(gwb[par][:], gwb[par][:], AF.Sign), "S")
                        b.inc(s.activation(uwb[par][:], uwb[par][:], AF.Sign), "S")
                    b.S(s)
                    def t(t, j=j, par=par, sv=b.n["S"]):
                        b.wait(t, "S", sv)
                        for si in range(NS):
                            b.wait(t, "V", gu_ev[si % 2])
                        for si, (st, wdt) in enumerate(TS_):
                            for k in range(KD):
                                last = t.matmul(psg[si % 2][:, 0:wdt], gwb[par][:, k * 128:(k + 1) * 128],
                                                q_fm[k][:, st:st + wdt], start=(k == 0), stop=(k == KD - 1))
                            g_mm[si % 2] = b.inc(last, "T")
                            for k in range(KD):
                                last = t.matmul(psu[si % 2][:, 0:wdt], uwb[par][:, k * 128:(k + 1) * 128],
                                                q_fm[k][:, st:st + wdt], start=(k == 0), stop=(k == KD - 1))
                            u_mm[si % 2] = b.inc(last, "T")
                        wfree[par] = b.n["T"]
                    b.T(t)
                    for si, (st, wdt) in enumerate(TS_):
                        def s2(s, si=si, wdt=wdt):
                            b.wait(s, "T", g_mm[si % 2])
                            b.inc(s.activation(gsb[:, 0:wdt], psg[si % 2][:, 0:wdt], AF.Silu), "S")
                        b.S(s2)
                        def v(v, j=j, si=si, st=st, wdt=wdt, sv=b.n["S"]):
                            b.wait(v, "S", sv)
                            b.wait(v, "T", u_mm[si % 2])
                            gu_ev[si % 2] = b.step(v, v.tensor_tensor(pfm[j][:, st:st + wdt], gsb[:, 0:wdt],
                                                                      psu[si % 2][:, 0:wdt], ALU.mult), "V")
                        b.V(v)
                def vfin(v):
                    b.wait(v, "T"); b.wait(v, "S")
                    b.step(v, v.engine_nop(), "V")
                b.V(vfin)
            tap_fm("pfm", pfm, T)

        def p5b(pfm):
            b.barrier()
            with ExitStack() as pc:
                dwb = [b.sb(f"p5_dw{i}", [128, KH * 128], BF16, pc, side="right") for i in range(2)]
                dfm = [b.sb(f"p5_df{dp}", [128, 512], BF16, pc, side="right") for dp in range(16)]
                xrl = [b.sb(f"p5_xr{i}", [128, D_MODEL], F32, pc, side="right") for i in range(1)]
                ob = [b.sb(f"p5_ob{i}", [128, D_MODEL], F32, pc, side="right") for i in range(2)]
                psd = [b.ps(f"p5_d{i}", [128, 512], F32, pc) for i in range(2)]
                trp3 = [b.ps(f"p5_t{i}", [128, 128], BF16, pc) for i in range(2)]
                ddma = [0, 0]; wfree = [0, 0]
                d_mm = [0, 0]; d_ev = [0, 0]
                tr_ev3 = [0, 0]
                ob_done = [0, 0]

                pv = b.n["V"]
                for si, (st, wdt) in enumerate(TS_):
                    for dp in range(16):
                        idx = si * 16 + dp
                        par = idx % 2
                        def g(g, dp=dp, par=par, idx=idx):
                            if idx >= 2:
                                b.wait(g, "T", wfree[par])
                            g.dma_start(out=dwb[par][:], in_=wd[dp].rearrange("p k m -> p (k m)")).then_inc(b.sem(f"p5d{par}"), 16)
                            b.n[f"p5d{par}"] = b.n.get(f"p5d{par}", 0) + 16
                            ddma[par] = b.n[f"p5d{par}"]
                        b.G(g)
                        def s(s, par=par):
                            b.wait(s, f"p5d{par}", ddma[par])
                            b.inc(s.activation(dwb[par][:], dwb[par][:], AF.Sign), "S")
                        b.S(s)
                        def t(t, par=par, st=st, wdt=wdt, sv=b.n["S"], pv=pv):
                            b.wait(t, "S", sv)
                            b.wait(t, "V", pv)
                            b.wait(t, "V", d_ev[par])
                            for jb in range(KH):
                                last = t.matmul(psd[par][:, 0:wdt], dwb[par][:, jb * 128:(jb + 1) * 128],
                                                pfm[jb][:, st:st + wdt], start=(jb == 0), stop=(jb == KH - 1))
                            d_mm[par] = b.inc(last, "T")
                            wfree[par] = d_mm[par]
                        b.T(t)
                        def v(v, dp=dp, par=par, wdt=wdt):
                            b.wait(v, "T", d_mm[par])
                            d_ev[par] = b.inc(v.tensor_copy(dfm[dp][:, 0:wdt], psd[par][:, 0:wdt]), "V")
                        b.V(v)
                    dfmv = b.n["V"]
                    for qq in range(wdt // 128):
                        tt = st // 128 + qq
                        buf = tt % 2
                        def g2(g, tt=tt, buf=buf):
                            b.wait(g, "V")
                            g.dma_start(out=xrl[0][:], in_=x1_scr[tt * 128:(tt + 1) * 128, :]).then_inc(b.sem("p5xr"), 16)
                            b.n["p5xr"] = b.n.get("p5xr", 0) + 16
                        b.G(g2)
                        xrd = b.n["p5xr"]
                        def vob(v, buf=buf, odv=ob_done[buf]):
                            b.wait(v, "dO", odv)
                            b.step(v, v.engine_nop(), "V")
                        b.V(vob)
                        for dp in range(16):
                            pbk = dp % 2
                            def t2(t, qq=qq, dp=dp, pbk=pbk, dfmv=dfmv):
                                b.wait(t, "V", dfmv)
                                b.wait(t, "V", tr_ev3[pbk])
                                b.inc(t.transpose(trp3[pbk][:], dfm[dp][:, qq * 128:(qq + 1) * 128], ident_b[:]), "T")
                            b.T(t2)
                            def v2(v, dp=dp, pbk=pbk, buf=buf, tv=b.n["T"], xrd=xrd):
                                b.wait(v, "T", tv)
                                b.wait(v, "p5xr", xrd)
                                tr_ev3[pbk] = b.inc(v.tensor_tensor(ob[buf][:, dp * 128:(dp + 1) * 128],
                                                                    xrl[0][:, dp * 128:(dp + 1) * 128],
                                                                    trp3[pbk][:], ALU.add), "V")
                            b.V(v2)
                        obv = b.n["V"]
                        def g3(g, tt=tt, buf=buf, obv=obv):
                            b.wait(g, "V", obv)
                            g.dma_start(out=out[tt * 128:(tt + 1) * 128, :], in_=ob[buf][:]).then_inc(b.sem("dO"), 16)
                            b.n["dO"] += 16
                        b.G(g3)
                        ob_done[buf] = b.n["dO"]
                def vfin(v):
                    b.wait(v, "T")
                    b.step(v, v.engine_nop(), "V")
                b.V(vfin)

        xbc_ctx = ExitStack()
        xbc_s = [b.sb(f"xbc{g}", [128, T], BF16, xbc_ctx, side="right") for g in range(N_XBC)]
        csgrp_ctx = ExitStack()
        cs_sb = b.sb("cs_sb", [64, T], F32, csgrp_ctx, side="right")
        cs_tok = b.sb("cs_tok", [128, NC * 128], F32, csgrp_ctx, side="right")
        ecs_tok = b.sb("ecs_tok", [128, NC * 128], F32, csgrp_ctx, side="right")
        dt_tok = b.sb("dt_tok", [128, NC * 128], F32, csgrp_ctx, side="right")
        dcy_tok = b.sb("dcy_tok", [128, NC * 128], F32, csgrp_ctx, side="right")
        et_row = b.sb("et_row", [1, NC * 64], F32, csgrp_ctx, side="right")
        etc_row = b.sb("etc_row", [1, NC * 64], F32, csgrp_ctx, side="right")
        et_bc = b.sb("et_bc", [128, NC * 64], F32, csgrp_ctx, side="right")
        etc_bc = b.sb("etc_bc", [128, NC * 64], F32, csgrp_ctx, side="right")
        t4_st = b.sb("t4_st", [64, NC], F32, csgrp_ctx, side="right")

        with ExitStack() as x1ctx:
            xln_fm = [b.sb(f"xlnA{k}", [128, XROWS], BF16, x1ctx) for k in range(KD)]
            ln_pass("p1", xln_fm)
            p2(xln_fm, xbc_s)
            tap_fm("xln", xln_fm, XROWS)
        if "xbc" in tap_t:
            tap_fm("xbc", xbc_s, T)

        yfm_ctx = ExitStack()
        yfm = [b.sb(f"yfm{p}", [128, T], BF16, yfm_ctx) for p in range(32)]
        with ExitStack() as p3ctx:
            S_sb = b.sb("S_sb", [128, NHEADS * 64], F32, p3ctx)
            p3(xbc_s, yfm, S_sb)
        csgrp_ctx.close()
        xbc_ctx.close()

        ofm_ctx = ExitStack()
        ofm = [b.sb(f"p4_ofm{dp}", [128, T], F32, ofm_ctx, side="right") for dp in range(16)]
        with ExitStack() as x2ctx:
            xln_fm2 = [b.sb(f"xlnB{k}", [128, XROWS], BF16, x2ctx) for k in range(KD)]
            ln_pass("p4ln", xln_fm2)
            p4a(xln_fm2, yfm, ofm)
        yfm_ctx.close()

        q_ctx = ExitStack()
        q_fm = [b.sb(f"qfm{k}", [128, T], BF16, q_ctx) for k in range(KD)]
        p4b(ofm, q_fm)
        ofm_ctx.close()

        pfm_ctx = ExitStack()
        pfm = [b.sb(f"pfm{j}", [128, T], BF16, pfm_ctx, side="right") for j in range(KH)]
        p5a(q_fm, pfm)
        q_ctx.close()
        p5b(pfm)
        pfm_ctx.close()

        def fin(sync):
            for name in sorted(b.n):
                if name in ("T", "V", "S", "G"):
                    continue
                if b.n[name]:
                    b.wait(sync, name)
            for name in ("T", "V", "S", "G"):
                if b.n.get(name):
                    b.wait(sync, name)
        block.sync(fin)

    return nc, tap_t


def _tile_w(wT, kblocks, mblocks):
    K, M = wT.shape
    assert K == kblocks * 128 and M == mblocks * 128
    return np.ascontiguousarray(wT.reshape(kblocks, 128, mblocks, 128).transpose(2, 1, 0, 3))


def prep_inputs(L0_attn_x, L0_attn_in_proj_w, L0_attn_conv_w, L0_attn_conv_b,
                L0_attn_dt_bias, L0_attn_A_log, L0_attn_D, L0_attn_norm_w,
                L0_attn_out_proj_w, L0_ffn_W_g, L0_ffn_W_u, L0_ffn_W_d,
                n_cores=8, n_chunks=4):
    T = n_chunks * CHUNK
    XROWS = -(-(T + 3) // 128) * 128
    f = np.float32
    x = np.asarray(L0_attn_x, f)

    wip_p = np.zeros((EP * 128, D_MODEL), f)
    wip_p[:2 * D_INNER + 2 * D_STATE + NHEADS] = np.asarray(L0_attn_in_proj_w, f)
    wip_t = _tile_w(np.ascontiguousarray(wip_p.T), KD, EP)
    wout_t = _tile_w(np.ascontiguousarray(np.asarray(L0_attn_out_proj_w, f).T), KI, 16)
    wg_t = _tile_w(np.ascontiguousarray(np.asarray(L0_ffn_W_g, f).T), KD, KH)
    wu_t = _tile_w(np.ascontiguousarray(np.asarray(L0_ffn_W_u, f).T), KD, KH)
    wd_t = _tile_w(np.ascontiguousarray(np.asarray(L0_ffn_W_d, f).T), KH, 16)
    cw = np.ascontiguousarray(np.asarray(L0_attn_conv_w, f).reshape(N_XBC, 128, 4).transpose(1, 0, 2).reshape(128, N_XBC * 4))
    cb = np.ascontiguousarray(np.asarray(L0_attn_conv_b, f).reshape(N_XBC, 128).T)
    nw = np.ascontiguousarray(np.asarray(L0_attn_norm_w, f).reshape(32, 128).T)
    dtbv = np.ascontiguousarray(np.asarray(L0_attn_dt_bias, f).reshape(64, 1))
    alogv = np.ascontiguousarray(np.asarray(L0_attn_A_log, f).reshape(64, 1))
    dvecv = np.ascontiguousarray(np.asarray(L0_attn_D, f).reshape(1, 64))

    shared = dict(wip=wip_t, wout=wout_t, wg=wg_t, wu=wu_t, wd=wd_t,
                  conv_w=cw, conv_b=cb, norm_w=nw, dtb=dtbv, alog=alogv, dvec=dvecv)
    in_maps = []
    for c in range(n_cores):
        bb, half = c // 2, c % 2
        xc = np.zeros((XROWS, D_MODEL), f)
        if half == 1:
            xc[0:3] = x[bb, T - 3:T]
        xc[3:3 + T] = x[bb, half * T:(half + 1) * T]
        m = dict(shared)
        m["x_core"] = xc
        m["ggate"] = np.full((128, 1), float(half), f)
        in_maps.append(m)
    return in_maps


def kernel(**inputs):
    n_cores, n_chunks = 8, 4
    in_maps = prep_inputs(**inputs, n_cores=n_cores, n_chunks=n_chunks)
    nc, _ = build(n_cores=n_cores, n_chunks=n_chunks)
    res = run_bass_kernel_spmd(nc, in_maps, core_ids=list(range(n_cores)))
    T = n_chunks * CHUNK
    outp = np.empty((4, 2 * T, D_MODEL), np.float32)
    for c in range(n_cores):
        bb, half = c // 2, c % 2
        outp[bb, half * T:(half + 1) * T] = res.results[c]["out"]
    return outp


# version 02
# speedup vs baseline: 1.0858x; 1.0858x over previous
import numpy as np
from contextlib import ExitStack

import concourse.bass as bass
import concourse.mybir as mybir
from concourse.bass_utils import run_bass_kernel_spmd

F32 = mybir.dt.float32
BF16 = mybir.dt.bfloat16
AF = mybir.ActivationFunctionType
ALU = mybir.AluOpType
AX = mybir.AxisListType

CHUNK = 256
D_MODEL = 2048
D_INNER = 4096
D_STATE = 128
NHEADS = 64
KD = D_MODEL // 128
KI = D_INNER // 128
KH = 8192 // 128
EP = 67
E_XBC0 = 32
N_XBC = 34
H_FFN = 8192
MAGIC = 12582912.0


def _spans(total, width=512):
    return [(s, min(width, total - s)) for s in range(0, total, width)]


class PB:
    def __init__(self, nc, block, ctx):
        self.nc, self.block, self.ctx = nc, block, ctx
        self._sems, self.n = {}, {}

    def sem(self, name):
        if name not in self._sems:
            self._sems[name] = self.ctx.enter_context(self.nc.semaphore(name))
            self.n[name] = 0
        return self._sems[name]

    def inc(self, instr, name, k=1):
        instr.then_inc(self.sem(name), k)
        self.n[name] += k
        return self.n[name]

    def wait(self, eng, name, val=None):
        if val is None:
            val = self.n.get(name, 0)
        self.sem(name)
        if val > 0:
            eng.wait_ge(self.sem(name), val)

    def step(self, eng, instr, name):
        v = self.inc(instr, name)
        eng.wait_ge(self.sem(name), v)
        return v

    def sb(self, name, shape, dtype, ctx=None, side="left"):
        return (ctx or self.ctx).enter_context(self.nc.sbuf_tensor(name, shape, dtype, side=side))

    def ps(self, name, shape, dtype=F32, ctx=None):
        return (ctx or self.ctx).enter_context(self.nc.psum_tensor(name, shape, dtype))

    def barrier(self):
        names = tuple(nm for nm in self.n if self.n[nm] > 0)
        for sec in (self.V, self.S, self.T, self.G):
            def f(e, names=names):
                for nm in names:
                    self.wait(e, nm)
            sec(f)

    def V(self, f):
        def w(v):
            self.wait(v, "V")
            f(v)
        self.block.vector(w)

    def S(self, f):
        def w(s):
            self.wait(s, "S")
            f(s)
        self.block.scalar(w)

    def T(self, f):
        def w(t):
            self.wait(t, "T")
            f(t)
        self.block.tensor(w)

    def G(self, f):
        def w(g):
            self.wait(g, "G")
            f(g)
        self.block.gpsimd(w)


def build(n_cores=8, n_chunks=4, taps=()):
    T = n_chunks * CHUNK
    XROWS = -(-(T + 3) // 128) * 128
    NTT = XROWS // 128
    NTM = T // 128
    NC = n_chunks
    TS_ = _spans(T)
    NS = len(TS_)
    taps = set(taps)

    nc = bass.Bass("TRN2", target_bir_lowering=False)

    x_core = nc.dram_tensor("x_core", [XROWS, D_MODEL], F32, kind="ExternalInput").ap()
    wip = nc.dram_tensor("wip", [EP, 128, KD, 128], F32, kind="ExternalInput").ap()
    wout = nc.dram_tensor("wout", [16, 128, KI, 128], F32, kind="ExternalInput").ap()
    wg = nc.dram_tensor("wg", [KH, 128, KD, 128], F32, kind="ExternalInput").ap()
    wu = nc.dram_tensor("wu", [KH, 128, KD, 128], F32, kind="ExternalInput").ap()
    wd = nc.dram_tensor("wd", [16, 128, KH, 128], F32, kind="ExternalInput").ap()
    conv_w = nc.dram_tensor("conv_w", [128, N_XBC * 4], F32, kind="ExternalInput").ap()
    conv_b = nc.dram_tensor("conv_b", [128, N_XBC], F32, kind="ExternalInput").ap()
    norm_w = nc.dram_tensor("norm_w", [128, 32], F32, kind="ExternalInput").ap()
    dtb = nc.dram_tensor("dtb", [64, 1], F32, kind="ExternalInput").ap()
    alog = nc.dram_tensor("alog", [64, 1], F32, kind="ExternalInput").ap()
    dvec = nc.dram_tensor("dvec", [1, 64], F32, kind="ExternalInput").ap()
    ggate = nc.dram_tensor("ggate", [128, 1], F32, kind="ExternalInput").ap()

    out = nc.dram_tensor("out", [T, D_MODEL], F32, kind="ExternalOutput").ap()
    x1_scr = nc.dram_tensor("x1_scr", [T, D_MODEL], F32, kind="Internal").ap()
    ag_in = nc.dram_tensor("ag_in", [128, NHEADS * 64], F32, kind="Internal").ap()
    ag_out = nc.dram_tensor("ag_out", [256, NHEADS * 64], F32, kind="Internal").ap()

    tap_t = {}
    for name, shape in (("xln", [KD * 128, XROWS]), ("xbc", [N_XBC * 128, T]),
                        ("dt", [64, T]), ("cs", [64, T]), ("yfm", [D_INNER, T]),
                        ("x1", [T, D_MODEL]), ("q", [D_MODEL, T]), ("pfm", [H_FFN, T])):
        if name in taps:
            tap_t[name] = nc.dram_tensor("tap_" + name, shape, F32, kind="ExternalOutput").ap()

    with ExitStack() as ctx:
        block = ctx.enter_context(nc.Block())
        b = PB(nc, block, ctx)

        tapbuf = b.sb("tapbuf", [128, XROWS if tap_t else 1], F32)

        def tap_fm(name, tiles, cols):
            if name not in tap_t:
                return
            dst = tap_t[name]
            buf = tapbuf
            for i, tile in enumerate(tiles):
                def v(v, i=i, tile=tile):
                    b.wait(v, "T"); b.wait(v, "S"); b.wait(v, "dT")
                    b.step(v, v.tensor_copy(buf[:, 0:cols], tile[:, 0:cols]), "V")
                b.V(v)
                def g(g, i=i):
                    b.wait(g, "V")
                    g.dma_start(out=dst[i * 128:(i + 1) * 128, 0:cols], in_=buf[:, 0:cols]).then_inc(b.sem("dT"), 16)
                    b.n["dT"] += 16
                b.G(g)

        ident_b = b.sb("ident_b", [128, 128], BF16)
        ident_f = b.sb("ident_f", [128, 128], F32)
        ones_col_b = b.sb("ones_col_b", [128, 1], BF16)
        ones_row_f = b.sb("ones_row_f", [1, 128], F32)
        mask01 = b.sb("mask01", [128, 256], F32)
        cw_sb = b.sb("cw_sb", [128, N_XBC * 4], F32)
        cb_sb = b.sb("cb_sb", [128, N_XBC], F32)
        nw_sb = b.sb("nw_sb", [128, 32], F32)
        dtb_sb = b.sb("dtb_sb", [64, 1], F32)
        alog_sb = b.sb("alog_sb", [64, 1], F32)
        expa_sb = b.sb("expa_sb", [64, 1], F32)
        dvec_sb = b.sb("dvec_sb", [1, 64], F32)
        dbc_sb = b.sb("dbc_sb", [128, 64], F32)
        ggate_sb = b.sb("ggate_sb", [128, 1], F32)

        def g0(g):
            for dst, src in ((cw_sb, conv_w), (cb_sb, conv_b), (nw_sb, norm_w),
                             (dtb_sb, dtb), (alog_sb, alog), (dvec_sb, dvec),
                             (ggate_sb, ggate)):
                g.dma_start(out=dst[:], in_=src).then_inc(b.sem("dC"), 16)
                b.n["dC"] += 16
            for tile, val in ((ident_b, 0.0), (ident_f, 0.0), (ones_col_b, 1.0),
                              (ones_row_f, 1.0), (mask01, 1.0)):
                b.inc(g.memset(tile[:], val), "G")
            b.wait(g, "G")
            b.inc(g.affine_select(out=ident_b[:], in_=ident_b[:], compare_op=ALU.not_equal,
                                  fill=1.0, base=0, pattern=[[-1, 128]], channel_multiplier=1), "G")
            b.inc(g.affine_select(out=ident_f[:], in_=ident_f[:], compare_op=ALU.not_equal,
                                  fill=1.0, base=0, pattern=[[-1, 128]], channel_multiplier=1), "G")
            b.inc(g.affine_select(out=mask01[:], in_=mask01[:], compare_op=ALU.is_ge,
                                  fill=0.0, base=0, pattern=[[1, 256]], channel_multiplier=-1), "G")
        b.G(g0)

        def s0(s):
            b.wait(s, "dC")
            b.inc(s.activation(expa_sb[:], alog_sb[:], AF.Exp), "S")
        b.S(s0)

        with ExitStack() as c0:
            dbc_ps = b.ps("dbc_ps", [128, 64], F32, c0)
            def t0(t):
                b.wait(t, "dC")
                b.wait(t, "G")
                b.inc(t.matmul(dbc_ps[:], ones_row_f[:], dvec_sb[:], start=True, stop=True), "T")
            b.T(t0)
            def v0(v):
                b.wait(v, "T")
                b.inc(v.tensor_copy(dbc_sb[:], dbc_ps[:]), "V")
            b.V(v0)


        def ln_pass(prefix, xln_fm):
            b.barrier()
            with ExitStack() as pc:
                xt = [b.sb(f"{prefix}_xt{i}", [128, D_MODEL], F32, pc) for i in range(2)]
                xlt = [b.sb(f"{prefix}_xlt{i}", [128, D_MODEL], BF16, pc) for i in range(2)]
                sq = b.sb(f"{prefix}_sq", [128, D_MODEL], F32, pc)
                sqs = b.sb(f"{prefix}_sqs", [128, D_MODEL], F32, pc)
                stat = b.sb(f"{prefix}_stat", [128, 8], F32, pc)
                trp = [b.ps(f"{prefix}_trp{i}", [128, 128], BF16, pc) for i in range(2)]
                xt_free = [0] * NTT
                xlt_free = [0] * NTT
                ev_done = [0, 0]
                for tt in range(NTT):
                    buf = tt % 2
                    dn = f"{prefix}_ld{buf}"
                    def g(g, tt=tt, buf=buf, dn=dn):
                        if tt >= 2:
                            b.wait(g, "V", xt_free[tt - 2])
                        g.dma_start(out=xt[buf][:], in_=x_core[tt * 128:(tt + 1) * 128, :]).then_inc(b.sem(dn), 16)
                        b.n[dn] = b.n.get(dn, 0) + 16
                    b.G(g)
                    def v(v, tt=tt, buf=buf, dn=dn, dv=b.n[dn]):
                        b.wait(v, dn, dv)
                        b.step(v, v.reduce_sum(stat[:, 0:1], xt[buf][:], axis=AX.X), "V")
                        b.step(v, v.tensor_scalar(stat[:, 1:2], stat[:, 0:1], 1.0 / D_MODEL, None, ALU.mult), "V")
                        xt_free[tt] = b.step(v, v.tensor_scalar(sq[:], xt[buf][:], stat[:, 1:2], None, ALU.subtract), "V")
                    b.V(v)
                    def v2(v):
                        b.step(v, v.scalar_tensor_tensor(sqs[:], sq[:], 1.0, sq[:], ALU.mult, ALU.mult,
                                                         accum_out=stat[:, 2:3]), "V")
                        b.step(v, v.tensor_scalar(stat[:, 3:4], stat[:, 2:3], 1.0 / D_MODEL, 1e-5, ALU.mult, ALU.add), "V")
                        b.inc(v.reciprocal(stat[:, 4:5], stat[:, 3:4]), "V")
                    b.V(v2)
                    def s2(s, vv=b.n["V"]):
                        b.wait(s, "V", vv)
                        b.inc(s.activation(stat[:, 5:6], stat[:, 4:5], AF.Sqrt), "S")
                    b.S(s2)
                    def v3(v, tt=tt, buf=buf, sv=b.n["S"]):
                        b.wait(v, "S", sv)
                        if tt >= 2:
                            b.wait(v, "T", xlt_free[tt - 2])
                        b.inc(v.tensor_scalar(xlt[buf][:], sq[:], stat[:, 5:6], None, ALU.mult), "V")
                    b.V(v3)
                    xlv = b.n["V"]
                    for k in range(KD):
                        pbk = k % 2
                        def t(t, tt=tt, buf=buf, k=k, pbk=pbk, xlv=xlv):
                            b.wait(t, "V", xlv)
                            b.wait(t, "V", ev_done[pbk])
                            xlt_free[tt] = b.inc(t.transpose(trp[pbk][:], xlt[buf][:, k * 128:(k + 1) * 128], ident_b[:]), "T")
                        b.T(t)
                        def s3(v, tt=tt, k=k, pbk=pbk, tv=b.n["T"]):
                            b.wait(v, "T", tv)
                            ev_done[pbk] = b.inc(v.tensor_copy(xln_fm[k][:, tt * 128:(tt + 1) * 128], trp[pbk][:]), "V")
                        b.V(s3)
                def vf(v):
                    b.step(v, v.engine_nop(), "V")
                b.V(vf)

        def p2(xln_fm, xbc_s):
            b.barrier()
            with ExitStack() as pc:
                wbuf = [b.sb(f"p2_w{i}", [128, KD * 128], BF16, pc) for i in range(2)]
                mp = [b.ps(f"p2_mp{i}", [128, 512], F32, pc) for i in range(4)]
                hp = b.ps("p2_hp", [128, (N_XBC + 1) * 3], F32, pc)
                xp = b.sb("p2_xp", [128, T + 3], F32, pc)
                dt_sb = b.sb("dt_sb", [64, T], F32, pc)
                da_sb = b.sb("da_sb", [64, T], F32, pc)
                ecs_sb = b.sb("ecs_sb", [64, T], F32, pc)
                dcy_sb = b.sb("dcy_sb", [64, T], F32, pc)
                gate_rst = b.sb("gate_rst", [64, T], F32, pc)
                cacc = b.sb("p2_cacc", [128, T], F32, pc)
                hsb = b.sb("p2_hsb", [128, (N_XBC + 1) * 3], F32, pc)
                k3 = b.sb("p2_k3", [128, N_XBC * 3], F32, pc)
                h6 = b.sb("p2_h6", [128, 6], F32, pc)

                etiles = list(range(E_XBC0, EP))
                wdma = [0, 0]
                wfree = [0, 0]
                mp_mm = [0] * 4
                mp_ev = [0] * 4
                xp_free = [0]

                def gp(g):
                    b.inc(g.memset(gate_rst[:], 1.0), "G")
                    b.wait(g, "G")
                    for c in range(NC):
                        b.inc(g.memset(gate_rst[:, c * CHUNK:c * CHUNK + 1], 0.0), "G")
                    g.dma_start(out=wbuf[0][:], in_=wip[etiles[0]].rearrange("p k m -> p (k m)")).then_inc(b.sem("p2w0"), 16)
                    b.n["p2w0"] = b.n.get("p2w0", 0) + 16
                b.G(gp)
                wdma[0] = b.n["p2w0"]

                for i, e in enumerate(etiles):
                    par = i % 2
                    gi = e - E_XBC0
                    if i + 1 < len(etiles):
                        def g(g, i=i):
                            npar = (i + 1) % 2
                            if i >= 1:
                                b.wait(g, "T", wfree[npar])
                            g.dma_start(out=wbuf[npar][:], in_=wip[etiles[i + 1]].rearrange("p k m -> p (k m)")).then_inc(b.sem(f"p2w{npar}"), 16)
                            b.n[f"p2w{npar}"] = b.n.get(f"p2w{npar}", 0) + 16
                            wdma[npar] = b.n[f"p2w{npar}"]
                        b.G(g)
                    bk0 = 2 * par
                    def t(t, i=i, par=par, bk0=bk0, gi=gi):
                        b.wait(t, f"p2w{par}", wdma[par])
                        for si in range(NS):
                            b.wait(t, "V", mp_ev[bk0 + si])
                        for si, (st, wdt) in enumerate(TS_):
                            for k in range(KD):
                                last = t.matmul(mp[bk0 + si][:, 0:wdt], wbuf[par][:, k * 128:(k + 1) * 128],
                                                xln_fm[k][:, 3 + st:3 + st + wdt],
                                                start=(k == 0), stop=(k == KD - 1))
                            mp_mm[bk0 + si] = b.inc(last, "T")
                        for k in range(KD):
                            last = t.matmul(hp[:, gi * 3:gi * 3 + 3], wbuf[par][:, k * 128:(k + 1) * 128],
                                            xln_fm[k][:, 0:3], start=(k == 0), stop=(k == KD - 1))
                        wfree[par] = b.inc(last, "T")
                    b.T(t)
                    halo_mm = b.n["T"]
                    if gi < N_XBC:
                        def v(v, i=i, bk0=bk0, gi=gi):
                            b.wait(v, "S")
                            b.wait(v, "V", xp_free[0])
                            for si, (st, wdt) in enumerate(TS_):
                                b.wait(v, "T", mp_mm[bk0 + si])
                                mp_ev[bk0 + si] = b.inc(v.tensor_copy(xp[:, 3 + st:3 + st + wdt], mp[bk0 + si][:, 0:wdt]), "V")
                            v.wait_ge(b.sem("V"), b.n["V"])
                            b.step(v, v.tensor_copy(k3[:, gi * 3:gi * 3 + 3], xp[:, 3:6]), "V")
                            w4 = cw_sb[:, gi * 4:gi * 4 + 4]
                            b.step(v, v.tensor_scalar(cacc[:, 3:T], xp[:, 3:T], w4[:, 0:1], None, ALU.mult), "V")
                            b.step(v, v.scalar_tensor_tensor(cacc[:, 3:T], xp[:, 4:T + 1], w4[:, 1:2], cacc[:, 3:T], ALU.mult, ALU.add), "V")
                            b.step(v, v.scalar_tensor_tensor(cacc[:, 3:T], xp[:, 5:T + 2], w4[:, 2:3], cacc[:, 3:T], ALU.mult, ALU.add), "V")
                            xp_free[0] = b.step(v, v.scalar_tensor_tensor(cacc[:, 3:T], xp[:, 6:T + 3], w4[:, 3:4], cacc[:, 3:T], ALU.mult, ALU.add), "V")
                        b.V(v)
                        def s(s, gi=gi, vv=None):
                            b.wait(s, "V", xp_free[0])
                            b.inc(s.activation(xbc_s[gi][:, 3:T], cacc[:, 3:T], AF.Silu,
                                               bias=cb_sb[:, gi:gi + 1]), "S")
                        b.S(s)
                    else:
                        def s(s, bk0=bk0):
                            for si, (st, wdt) in enumerate(TS_):
                                b.wait(s, "T", mp_mm[bk0 + si])
                                b.inc(s.activation(da_sb[0:64, st:st + wdt], mp[bk0 + si][0:64, 0:wdt],
                                                   AF.Exp, bias=dtb_sb[:, 0:1]), "S")
                        b.S(s)
                        def v(v, bk0=bk0, sv=b.n["S"]):
                            b.wait(v, "S", sv)
                            for si in range(NS):
                                mp_ev[bk0 + si] = b.inc(v.engine_nop(), "V")
                        b.V(v)

                def vh(v, hm=b.n["T"]):
                    b.wait(v, "T", hm)
                    b.step(v, v.tensor_copy(hsb[:], hp[:]), "V")
                b.V(vh)
                for gi in range(N_XBC):
                    def v(v, gi=gi):
                        b.wait(v, "S")
                        b.step(v, v.tensor_copy(h6[:, 0:3], hsb[:, gi * 3:gi * 3 + 3]), "V")
                        b.step(v, v.tensor_copy(h6[:, 3:6], k3[:, gi * 3:gi * 3 + 3]), "V")
                        w4 = cw_sb[:, gi * 4:gi * 4 + 4]
                        b.step(v, v.tensor_scalar(cacc[:, 0:3], h6[:, 0:3], w4[:, 0:1], None, ALU.mult), "V")
                        b.step(v, v.scalar_tensor_tensor(cacc[:, 0:3], h6[:, 1:4], w4[:, 1:2], cacc[:, 0:3], ALU.mult, ALU.add), "V")
                        b.step(v, v.scalar_tensor_tensor(cacc[:, 0:3], h6[:, 2:5], w4[:, 2:3], cacc[:, 0:3], ALU.mult, ALU.add), "V")
                        b.step(v, v.scalar_tensor_tensor(cacc[:, 0:3], h6[:, 3:6], w4[:, 3:4], cacc[:, 0:3], ALU.mult, ALU.add), "V")
                    b.V(v)
                    def s(s, gi=gi, vv=b.n["V"]):
                        b.wait(s, "V", vv)
                        b.inc(s.activation(xbc_s[gi][:, 0:3], cacc[:, 0:3], AF.Silu,
                                           bias=cb_sb[:, gi:gi + 1]), "S")
                    b.S(s)

                def sd(s):
                    b.inc(s.activation(dt_sb[0:64, :], da_sb[0:64, :], AF.Ln, bias=1.0), "S")
                b.S(sd)
                def vd(v, sv=b.n["S"]):
                    b.wait(v, "S", sv)
                    b.wait(v, "G")
                    b.step(v, v.tensor_scalar(da_sb[0:64, :], dt_sb[0:64, :], expa_sb[:, 0:1], -1.0, ALU.mult, ALU.mult), "V")
                    b.step(v, v.tensor_tensor_scan(cs_sb[0:64, :], gate_rst[:, :], da_sb[0:64, :], 0.0, ALU.mult, ALU.add), "V")
                b.V(vd)
                def se(s, vv=b.n["V"]):
                    b.wait(s, "V", vv)
                    b.inc(s.activation(ecs_sb[0:64, :], cs_sb[0:64, :], AF.Exp), "S")
                    for c in range(NC):
                        sl = slice(c * CHUNK, (c + 1) * CHUNK)
                        b.inc(s.activation(dcy_sb[0:64, sl], cs_sb[0:64, sl], AF.Exp,
                                           bias=cs_sb[:, (c + 1) * CHUNK - 1:(c + 1) * CHUNK], scale=-1.0), "S")
                b.S(se)
                dchain = b.n["S"]

                for nm, src in (("dt", dt_sb), ("cs", cs_sb)):
                    if nm in tap_t:
                        def gd(g, nm=nm, src=src):
                            b.wait(g, "V"); b.wait(g, "S")
                            g.dma_start(out=tap_t[nm][:, :], in_=src[0:64, :]).then_inc(b.sem("dT"), 16)
                            b.n["dT"] += 16
                        b.G(gd)
                trf = [b.ps(f"p2_trf{i}", [128, 64], F32, pc) for i in range(2)]
                ev = [0, 0]
                ntr = 0
                for c in range(NC):
                    for lt in range(2):
                        col = (c * 2 + lt) * 64
                        sl = slice(c * CHUNK + lt * 128, c * CHUNK + (lt + 1) * 128)
                        for src, dst in ((cs_sb, cs_tok), (ecs_sb, ecs_tok),
                                         (dt_sb, dt_tok), (dcy_sb, dcy_tok)):
                            pbk = ntr % 2
                            def t(t, src=src, sl=sl, pbk=pbk):
                                b.wait(t, "S", dchain)
                                b.wait(t, "V", ev[pbk])
                                b.inc(t.transpose(trf[pbk][:, 0:64], src[0:64, sl], ident_f[0:64, 0:64]), "T")
                            b.T(t)
                            def v(v, dst=dst, col=col, pbk=pbk, tv=b.n["T"]):
                                b.wait(v, "T", tv)
                                ev[pbk] = b.inc(v.tensor_copy(dst[:, col:col + 64], trf[pbk][:, 0:64]), "V")
                            b.V(v)
                            ntr += 1
                    def vt(v, c=c):
                        b.wait(v, "S", dchain)
                        b.step(v, v.tensor_copy(t4_st[:, c:c + 1], cs_sb[0:64, (c + 1) * CHUNK - 1:(c + 1) * CHUNK]), "V")
                    b.V(vt)
                pbk = ntr % 2
                def tt4(t, pbk=pbk):
                    b.wait(t, "V")
                    b.wait(t, "V", ev[pbk])
                    b.inc(t.transpose(trf[pbk][0:NC, 0:64], t4_st[0:64, 0:NC], ident_f[0:64, 0:64]), "T")
                b.T(tt4)
                t4sb = b.sb("p2_t4sb", [NC, 64], F32, pc)
                def vt4(v, pbk=pbk, tv=b.n["T"]):
                    b.wait(v, "T", tv)
                    ev[pbk] = b.inc(v.tensor_copy(t4sb[:], trf[pbk][0:NC, 0:64]), "V")
                b.V(vt4)
                def gt4(g, vv=b.n["V"]):
                    b.wait(g, "V", vv)
                    for c in range(NC):
                        g.dma_start(out=et_row[0:1, c * 64:(c + 1) * 64], in_=t4sb[c:c + 1, :]).then_inc(b.sem("dS"), 16)
                        b.n["dS"] += 16
                b.G(gt4)
                def st4(s, dv=b.n["dS"]):
                    b.wait(s, "dS", dv)
                    b.inc(s.activation(et_row[0:1, :], et_row[0:1, :], AF.Exp), "S")
                b.S(st4)
                def vcp(v, sv=b.n["S"]):
                    b.wait(v, "S", sv)
                    b.step(v, v.memset(etc_row[0:1, 0:64], 1.0), "V")
                    for c in range(1, NC):
                        b.step(v, v.tensor_tensor(etc_row[0:1, c * 64:(c + 1) * 64],
                                                  etc_row[0:1, (c - 1) * 64:c * 64],
                                                  et_row[0:1, (c - 1) * 64:c * 64], ALU.mult), "V")
                b.V(vcp)
                bcp = b.ps("p2_bcp", [128, NC * 64], F32, pc)
                def tbc(t, vv=b.n["V"]):
                    b.wait(t, "V", vv)
                    b.inc(t.matmul(bcp[:], ones_row_f[:], et_row[0:1, :], start=True, stop=True), "T")
                b.T(tbc)
                def vbc(v, tv=b.n["T"]):
                    b.wait(v, "T", tv)
                    b.step(v, v.tensor_copy(et_bc[:], bcp[:]), "V")
                b.V(vbc)
                def tbc2(t, vv=b.n["V"]):
                    b.wait(t, "V", vv)
                    b.inc(t.matmul(bcp[:], ones_row_f[:], etc_row[0:1, :], start=True, stop=True), "T")
                b.T(tbc2)
                def vbc2(v, tv=b.n["T"]):
                    b.wait(v, "T", tv)
                    b.step(v, v.tensor_copy(etc_bc[:], bcp[:]), "V")
                b.V(vbc2)
                def vfin(v):
                    b.wait(v, "S")
                    b.wait(v, "T")
                    b.step(v, v.engine_nop(), "V")
                b.V(vfin)

        def p3(xbc_s, yfm, S_sb):
            b.barrier()
            with ExitStack() as pc:
                xstk = [b.sb(f"p3_xs{st}", [128, D_INNER], BF16, pc) for st in range(2)]
                Btk = [b.sb(f"p3_B{st}", [128, 128], BF16, pc) for st in range(2)]
                Xh = b.sb("p3_Xh", [128, 128], BF16, pc)
                Bd = b.sb("p3_Bd", [128, 256], BF16, pc)
                GmA = b.sb("p3_GmA", [128, 256], F32, pc)
                GmB = b.sb("p3_GmB", [128, 128], F32, pc)
                Emin = b.sb("p3_Emin", [128, 384], F32, pc)
                Eexp = b.sb("p3_Eexp", [128, 384], F32, pc)
                Mm = [b.sb(f"p3_M{i}", [128, 384], BF16, pc) for i in range(2)]
                Sbf = b.sb("p3_Sbf", [128, 64], BF16, pc)
                stg = b.sb("p3_stg", [128, 256], BF16, pc)
                oh2 = [b.sb(f"p3_oh{i}", [64, 8 * 128], F32, pc) for i in range(2)]
                Eps = [b.ps(f"p3_E{i}", [128, 384], F32, pc) for i in range(2)]
                Yps = [b.ps(f"p3_Y{i}", [128, 256], F32, pc) for i in range(2)]
                Ops = [b.ps(f"p3_O{i}", [128, 256], F32, pc) for i in range(2)]
                Sps = b.ps("p3_S", [128, 128], F32, pc)
                Trp = b.ps("p3_Tr", [128, 512], BF16, pc)

                def gz(g):
                    b.inc(g.memset(S_sb[:], 0.0), "G")
                b.G(gz)

                st_ = {
                    "E_mm": [0, 0], "E_ev": [0, 0],
                    "Y_mm": [0, 0], "Y_ev": [0, 0],
                    "O_mm": [0, 0], "O_ev": [0, 0],
                    "S_mm": 0, "S_ev": 0, "Sbf_rd": 0,
                    "Tr_ev": 0, "stg_rd": 0, "Xh_rd": 0, "Bd_rd": 0,
                    "yadd": 0, "oh_rd": [0, 0], "oh_gen": [0, 0],
                }

                def chunk_start(c):
                    for stt in range(2):
                        sl = slice(c * CHUNK + stt * 128, c * CHUNK + (stt + 1) * 128)
                        for grp in range(8):
                            def t(t, stt=stt, grp=grp, sl=sl):
                                b.wait(t, "V", st_["Tr_ev"])
                                b.wait(t, "S")
                                for j in range(4):
                                    last = t.transpose(Trp[:, j * 128:(j + 1) * 128], xbc_s[grp * 4 + j][:, sl], ident_b[:])
                                b.inc(last, "T")
                            b.T(t)
                            def s(v, stt=stt, grp=grp, tv=b.n["T"]):
                                b.wait(v, "T", tv)
                                st_["Tr_ev"] = b.inc(v.tensor_copy(xstk[stt][:, grp * 512:(grp + 1) * 512], Trp[:]), "V")
                            b.V(s)
                        def t2(t, stt=stt, sl=sl):
                            b.wait(t, "V", st_["Tr_ev"])
                            b.inc(t.transpose(Trp[:, 0:128], xbc_s[32][:, sl], ident_b[:]), "T")
                        b.T(t2)
                        def s2(v, stt=stt, tv=b.n["T"]):
                            b.wait(v, "T", tv)
                            st_["Tr_ev"] = b.inc(v.tensor_copy(Btk[stt][:], Trp[:, 0:128]), "V")
                        b.V(s2)
                    def tg(t, c=c):
                        b.wait(t, "V", st_["E_ev"][0]); b.wait(t, "V", st_["E_ev"][1])
                        b.wait(t, "V", st_["Tr_ev"])
                        last = t.matmul(Eps[0][:, 0:256], xbc_s[32][:, c * CHUNK:c * CHUNK + 128],
                                        xbc_s[33][:, c * CHUNK:(c + 1) * CHUNK], start=True, stop=True)
                        b.inc(last, "T")
                        last = t.matmul(Eps[1][:, 0:128], xbc_s[32][:, c * CHUNK + 128:(c + 1) * CHUNK],
                                        xbc_s[33][:, c * CHUNK + 128:(c + 1) * CHUNK], start=True, stop=True)
                        b.inc(last, "T")
                    b.T(tg)
                    def vg(v, tv=b.n["T"]):
                        b.wait(v, "T", tv)
                        b.step(v, v.tensor_tensor(GmA[:], Eps[0][:, 0:256], mask01[:], ALU.mult), "V")
                        b.step(v, v.tensor_tensor(GmB[:], Eps[1][:, 0:128], mask01[:, 0:128], ALU.mult), "V")
                        st_["E_ev"][0] = st_["E_ev"][1] = b.n["V"]
                    b.V(vg)

                def head_local(c, h):
                    pair, par = h // 2, h % 2
                    ep = h % 2
                    yp = pair % 2
                    ccol = c * CHUNK
                    hs = slice(h * 64, (h + 1) * 64)
                    yc = par * 64
                    ct0 = (c * 2 + 0) * 64 + h
                    ct1 = (c * 2 + 1) * 64 + h

                    if h % 8 == 0:
                        gb = (h // 8) % 2
                        def goh(g, h=h, gb=gb):
                            b.wait(g, "T", st_["oh_rd"][gb])
                            b.inc(g.memset(oh2[gb][:], 1.0), "G")
                            b.wait(g, "G")
                            oh3 = oh2[gb].ap().rearrange("k (hh j) -> k hh j", j=128)
                            b.inc(g.affine_select(out=oh3, in_=oh3, compare_op=ALU.is_equal,
                                                  fill=0.0, base=-h, pattern=[[-1, 8], [0, 128]],
                                                  channel_multiplier=1), "G")
                        b.G(goh)
                        st_["oh_gen"][gb] = b.n["G"]

                    def t(t, c=c, h=h, ep=ep):
                        gb = (h // 8) % 2
                        b.wait(t, "V", st_["E_ev"][ep])
                        b.wait(t, "G", st_["oh_gen"][gb])
                        last = t.matmul(Eps[ep][:, 0:256], oh2[gb][:, (h % 8) * 128:(h % 8 + 1) * 128],
                                        cs_sb[0:64, ccol:ccol + CHUNK], start=True, stop=True)
                        st_["E_mm"][ep] = b.inc(last, "T")
                        st_["oh_rd"][gb] = st_["E_mm"][ep]
                    b.T(t)
                    def v(v, ep=ep, ct0=ct0, ct1=ct1):
                        b.wait(v, "T", st_["E_mm"][ep])
                        b.step(v, v.tensor_scalar(Emin[:, 0:256], Eps[ep][:, 0:256],
                                                  cs_tok[:, ct0:ct0 + 1], 0.0, ALU.subtract, ALU.min), "V")
                        st_["E_ev"][ep] = b.step(v, v.tensor_scalar(Emin[:, 256:384], Eps[ep][:, 128:256],
                                                                    cs_tok[:, ct1:ct1 + 1], 0.0, ALU.subtract, ALU.min), "V")
                    b.V(v)
                    def s(s, vv=b.n["V"]):
                        b.wait(s, "V", vv)
                        b.inc(s.activation(Eexp[:, 0:384], Emin[:, 0:384], AF.Exp), "S")
                    b.S(s)
                    mi = h % 2
                    def v2(v, mi=mi, sv=b.n["S"]):
                        b.wait(v, "S", sv)
                        b.wait(v, "T", st_["M_rd"][mi] if "M_rd" in st_ else 0)
                        b.step(v, v.tensor_tensor(Mm[mi][:, 0:256], GmA[:], Eexp[:, 0:256], ALU.mult), "V")
                        b.step(v, v.tensor_tensor(Mm[mi][:, 256:384], GmB[:], Eexp[:, 256:384], ALU.mult), "V")
                    b.V(v2)
                    mv = b.n["V"]
                    def v3(v, c=c, h=h, hs=hs, ct0=ct0, ct1=ct1):
                        b.wait(v, "T", st_["Xh_rd"])
                        b.inc(v.tensor_scalar(Xh[:, 0:64], xstk[0][:, hs], dt_tok[:, ct0:ct0 + 1], None, ALU.mult), "V")
                        b.inc(v.tensor_scalar(Xh[:, 64:128], xstk[1][:, hs], dt_tok[:, ct1:ct1 + 1], None, ALU.mult), "V")
                        b.wait(v, "T", st_["Bd_rd"])
                        b.inc(v.tensor_scalar(Bd[:, 0:128], Btk[0][:], dcy_tok[:, ct0:ct0 + 1], None, ALU.mult), "V")
                        b.inc(v.tensor_scalar(Bd[:, 128:256], Btk[1][:], dcy_tok[:, ct1:ct1 + 1], None, ALU.mult), "V")
                        if c > 0:
                            b.wait(v, "T", st_["Sbf_rd"])
                            b.inc(v.tensor_copy(Sbf[:], S_sb[:, hs]), "V")
                    b.V(v3)
                    opv = b.n["V"]
                    def t2(t, c=c, h=h, mi=mi, yp=yp, yc=yc, hs=hs, mv=mv, opv=opv, ccol=ccol):
                        b.wait(t, "V", mv)
                        b.wait(t, "V", opv)
                        b.wait(t, "V", st_["Y_ev"][yp])
                        b.wait(t, "V", st_["O_ev"][yp])
                        t.matmul(Yps[yp][:, yc:yc + 64], Mm[mi][:, 0:128], Xh[:, 0:64], start=True, stop=True)
                        t.matmul(Yps[yp][:, 128 + yc:192 + yc], Mm[mi][:, 128:256], Xh[:, 0:64], start=True, stop=False)
                        last = t.matmul(Yps[yp][:, 128 + yc:192 + yc], Mm[mi][:, 256:384], Xh[:, 64:128], start=False, stop=True)
                        st_["M_rd"][mi] = b.inc(last, "T")
                        st_["Y_mm"][yp] = st_["M_rd"][mi]
                        b.wait(t, "V", st_["S_ev"])
                        t.matmul(Sps[:, yc:yc + 64], Bd[:, 0:128], Xh[:, 0:64], start=True, stop=False)
                        last = t.matmul(Sps[:, yc:yc + 64], Bd[:, 128:256], Xh[:, 64:128], start=False, stop=True)
                        st_["S_mm"] = b.inc(last, "T")
                        st_["Xh_rd"] = st_["S_mm"]
                        st_["Bd_rd"] = st_["S_mm"]
                        if c > 0:
                            t.matmul(Ops[yp][:, yc:yc + 64], xbc_s[33][:, ccol:ccol + 128], Sbf[:], start=True, stop=True)
                            last = t.matmul(Ops[yp][:, 128 + yc:192 + yc], xbc_s[33][:, ccol + 128:ccol + 256], Sbf[:], start=True, stop=True)
                            st_["Sbf_rd"] = b.inc(last, "T")
                            st_["O_mm"][yp] = st_["Sbf_rd"]
                    b.T(t2)
                    def v5(v, c=c, h=h, yp=yp, yc=yc, hs=hs, ct0=ct0, ct1=ct1, par=par):
                        b.wait(v, "T", st_["Y_mm"][yp])
                        if par == 0:
                            b.wait(v, "T", st_["stg_rd"])
                        for lt, ct in ((0, ct0), (1, ct1)):
                            dst = stg[:, lt * 128 + yc:lt * 128 + yc + 64]
                            b.inc(v.scalar_tensor_tensor(dst, xstk[lt][:, hs], dbc_sb[:, h:h + 1],
                                                         Yps[yp][:, lt * 128 + yc:lt * 128 + yc + 64],
                                                         ALU.mult, ALU.add), "V")
                        st_["Y_ev"][yp] = b.n["V"]
                        if c > 0:
                            b.wait(v, "T", st_["O_mm"][yp])
                            v.wait_ge(b.sem("V"), b.n["V"])
                            for lt, ct in ((0, ct0), (1, ct1)):
                                dst = stg[:, lt * 128 + yc:lt * 128 + yc + 64]
                                b.inc(v.scalar_tensor_tensor(dst, Ops[yp][:, lt * 128 + yc:lt * 128 + yc + 64],
                                                             ecs_tok[:, ct:ct + 1], dst, ALU.mult, ALU.add), "V")
                            st_["O_ev"][yp] = b.n["V"]
                        b.wait(v, "T", st_["S_mm"])
                        v.wait_ge(b.sem("V"), b.n["V"])
                        b.inc(v.scalar_tensor_tensor(S_sb[:, hs], S_sb[:, hs], et_bc[:, c * 64 + h:c * 64 + h + 1],
                                                     Sps[:, yc:yc + 64], ALU.mult, ALU.add), "V")
                        st_["S_ev"] = b.n["V"]
                    b.V(v5)
                    return b.n["V"]

                def head_corr(c, h):
                    pair, par = h // 2, h % 2
                    yp = pair % 2
                    ccol = c * CHUNK
                    hs = slice(h * 64, (h + 1) * 64)
                    yc = par * 64
                    ct0 = (c * 2 + 0) * 64 + h
                    ct1 = (c * 2 + 1) * 64 + h
                    def v6(v, c=c, h=h, hs=hs):
                        b.wait(v, "T", st_["Sbf_rd"])
                        b.inc(v.tensor_scalar(Sbf[:], S_sb[:, hs], etc_bc[:, c * 64 + h:c * 64 + h + 1], None, ALU.mult), "V")
                    b.V(v6)
                    def t3(t, yp=yp, yc=yc, ccol=ccol, vv=b.n["V"]):
                        b.wait(t, "V", vv)
                        b.wait(t, "V", st_["O_ev"][yp])
                        t.matmul(Ops[yp][:, yc:yc + 64], xbc_s[33][:, ccol:ccol + 128], Sbf[:], start=True, stop=True)
                        last = t.matmul(Ops[yp][:, 128 + yc:192 + yc], xbc_s[33][:, ccol + 128:ccol + 256], Sbf[:], start=True, stop=True)
                        st_["Sbf_rd"] = b.inc(last, "T")
                        st_["O_mm"][yp] = st_["Sbf_rd"]
                    b.T(t3)
                    def v7(v, yp=yp, yc=yc, ct0=ct0, ct1=ct1, par=par):
                        b.wait(v, "T", st_["O_mm"][yp])
                        if par == 0:
                            b.wait(v, "T", st_["stg_rd"])
                        for lt, ct in ((0, ct0), (1, ct1)):
                            b.inc(v.tensor_scalar(stg[:, lt * 128 + yc:lt * 128 + yc + 64],
                                                  Ops[yp][:, lt * 128 + yc:lt * 128 + yc + 64],
                                                  ecs_tok[:, ct:ct + 1], None, ALU.mult), "V")
                        st_["O_ev"][yp] = b.n["V"]
                    b.V(v7)
                    return b.n["V"]

                def pair_out(c, pair, stv, correction):
                    def t4(t, c=c, stv=stv, correction=correction):
                        b.wait(t, "V", stv)
                        b.wait(t, "V", st_["Tr_ev"])
                        if correction:
                            b.wait(t, "V", st_["yadd"])
                        for lt in range(2):
                            last = t.transpose(Trp[:, lt * 128:(lt + 1) * 128], stg[:, lt * 128:(lt + 1) * 128], ident_b[:])
                        st_["stg_rd"] = b.inc(last, "T")
                    b.T(t4)
                    tv = b.n["T"]
                    if not correction:
                        def s4(v, c=c, pair=pair, tv=tv):
                            b.wait(v, "T", tv)
                            for lt in range(2):
                                b.inc(v.tensor_copy(yfm[pair][:, c * CHUNK + lt * 128:c * CHUNK + (lt + 1) * 128],
                                                    Trp[:, lt * 128:(lt + 1) * 128]), "V")
                            st_["Tr_ev"] = b.n["V"]
                        b.V(s4)
                    else:
                        def v8(v, c=c, pair=pair, tv=tv):
                            b.wait(v, "T", tv)
                            for lt in range(2):
                                sl = slice(c * CHUNK + lt * 128, c * CHUNK + (lt + 1) * 128)
                                b.inc(v.tensor_tensor(yfm[pair][:, sl], yfm[pair][:, sl],
                                                      Trp[:, lt * 128:(lt + 1) * 128], ALU.add), "V")
                            st_["yadd"] = b.n["V"]
                            st_["Tr_ev"] = b.n["V"]
                        b.V(v8)

                st_["M_rd"] = [0, 0]
                for c in range(NC):
                    chunk_start(c)
                    for h in range(NHEADS):
                        stv = head_local(c, h)
                        if h % 2 == 1:
                            pair_out(c, h // 2, stv, False)

                def gx(g):
                    b.wait(g, "V")
                    g.dma_start(out=ag_in[:], in_=S_sb[:]).then_inc(b.sem("dS"), 16)
                    b.n["dS"] += 16
                    b.wait(g, "dS")
                    g.collective_compute(
                        "AllGather", ALU.bypass,
                        replica_groups=[[2 * i, 2 * i + 1] for i in range(max(1, n_cores // 2))],
                        ins=[ag_in.opt()], outs=[ag_out.opt()],
                    ).then_inc(b.sem("CC"), 1)
                    b.n["CC"] += 1
                    b.wait(g, "CC")
                    g.dma_start(out=S_sb[:], in_=ag_out[0:128, :]).then_inc(b.sem("dS"), 16)
                    b.n["dS"] += 16
                b.G(gx)
                def vgate(v, dv=b.n["dS"]):
                    b.wait(v, "dS", dv)
                    b.step(v, v.tensor_scalar(S_sb[:], S_sb[:], ggate_sb[:, 0:1], None, ALU.mult), "V")
                b.V(vgate)

                for c in range(NC):
                    for h in range(NHEADS):
                        stv = head_corr(c, h)
                        if h % 2 == 1:
                            pair_out(c, h // 2, stv, True)

                def vfin(v):
                    b.wait(v, "T"); b.wait(v, "S")
                    b.step(v, v.engine_nop(), "V")
                b.V(vfin)

        def p4a(xln_fm, yfm, ofm):
            b.barrier()
            with ExitStack() as pc:
                wob = [b.sb(f"p4_wo{i}", [128, KI * 128], BF16, pc, side="right") for i in range(2)]
                mp = [b.ps(f"p4_mp{i}", [128, 512], F32, pc) for i in range(4)]
                vps = [b.ps(f"p4_vp{si}", [1, 512], F32, pc) for si in range(NS)]
                zs = b.sb("p4_zs", [128, 512], F32, pc, side="right")
                sqb = b.sb("p4_sqb", [128, 512], BF16, pc, side="right")
                rrow = b.sb("p4_rrow", [1, T], F32, pc, side="right")
                rbc = b.sb("p4_rbc", [128, T], F32, pc, side="right")
                wdma = [0, 0]; wfree = [0, 0]
                mp_mm = [0] * 4; mp_ev = [0] * 4
                vps_mm = 0

                def gp(g):
                    g.dma_start(out=wob[0][:, 0:KD * 128], in_=wip[0].rearrange("p k m -> p (k m)")).then_inc(b.sem("p4z0"), 16)
                    b.n["p4z0"] = b.n.get("p4z0", 0) + 16
                b.G(gp)
                wdma[0] = b.n["p4z0"]

                for zt in range(32):
                    par = zt % 2
                    bk0 = 2 * par
                    if zt + 1 < 32:
                        def g(g, zt=zt):
                            npar = (zt + 1) % 2
                            if zt >= 1:
                                b.wait(g, "T", wfree[npar])
                            g.dma_start(out=wob[npar][:, 0:KD * 128], in_=wip[zt + 1].rearrange("p k m -> p (k m)")).then_inc(b.sem(f"p4z{npar}"), 16)
                            b.n[f"p4z{npar}"] = b.n.get(f"p4z{npar}", 0) + 16
                            wdma[npar] = b.n[f"p4z{npar}"]
                        b.G(g)
                    def t(t, zt=zt, par=par, bk0=bk0):
                        b.wait(t, f"p4z{par}", wdma[par])
                        for si in range(NS):
                            b.wait(t, "V", mp_ev[bk0 + si])
                        for si, (st, wdt) in enumerate(TS_):
                            for k in range(KD):
                                last = t.matmul(mp[bk0 + si][:, 0:wdt], wob[par][:, k * 128:(k + 1) * 128],
                                                xln_fm[k][:, 3 + st:3 + st + wdt],
                                                start=(k == 0), stop=(k == KD - 1))
                            mp_mm[bk0 + si] = b.inc(last, "T")
                        wfree[par] = b.n["T"]
                    b.T(t)
                    for si, (st, wdt) in enumerate(TS_):
                        def s(s, bk0=bk0, si=si, wdt=wdt):
                            b.wait(s, "T", mp_mm[bk0 + si])
                            b.inc(s.activation(zs[:, 0:wdt], mp[bk0 + si][:, 0:wdt], AF.Silu), "S")
                        b.S(s)
                        def v(v, zt=zt, bk0=bk0, si=si, st=st, wdt=wdt, sv=b.n["S"]):
                            b.wait(v, "S", sv)
                            mp_ev[bk0 + si] = b.step(v, v.tensor_tensor(yfm[zt][:, st:st + wdt], yfm[zt][:, st:st + wdt],
                                                                        zs[:, 0:wdt], ALU.mult), "V")
                        b.V(v)
                        def s2(v, zt=zt, st=st, wdt=wdt):
                            b.inc(v.tensor_tensor(sqb[:, 0:wdt], yfm[zt][:, st:st + wdt],
                                                  yfm[zt][:, st:st + wdt], ALU.mult), "V")
                        b.V(s2)
                        def t2(t, zt=zt, si=si, wdt=wdt, sv=b.n["V"]):
                            nonlocal vps_mm
                            b.wait(t, "V", sv)
                            last = t.matmul(vps[si][0:1, 0:wdt], ones_col_b[:], sqb[:, 0:wdt],
                                            start=(zt == 0), stop=(zt == 31))
                            vps_mm = b.inc(last, "T")
                        b.T(t2)

                def vr(v):
                    b.wait(v, "T", vps_mm)
                    for si, (st, wdt) in enumerate(TS_):
                        b.step(v, v.tensor_scalar(rrow[0:1, st:st + wdt], vps[si][0:1, 0:wdt],
                                                  1.0 / D_INNER, 1e-5, ALU.mult, ALU.add), "V")
                    b.inc(v.reciprocal(rrow[0:1, :], rrow[0:1, :]), "V")
                b.V(vr)
                def sr(s, vv=b.n["V"]):
                    b.wait(s, "V", vv)
                    b.inc(s.activation(rrow[0:1, :], rrow[0:1, :], AF.Sqrt), "S")
                b.S(sr)
                def tr(t, sv=b.n["S"]):
                    b.wait(t, "S", sv)
                    for si, (st, wdt) in enumerate(TS_):
                        b.wait(t, "V", mp_ev[si])
                        last = t.matmul(mp[si][:, 0:wdt], ones_row_f[:], rrow[0:1, st:st + wdt], start=True, stop=True)
                        mp_mm[si] = b.inc(last, "T")
                b.T(tr)
                for si, (st, wdt) in enumerate(TS_):
                    def v(v, si=si, st=st, wdt=wdt):
                        b.wait(v, "T", mp_mm[si])
                        mp_ev[si] = b.step(v, v.tensor_copy(rbc[:, st:st + wdt], mp[si][:, 0:wdt]), "V")
                    b.V(v)
                for p in range(32):
                    def v(v, p=p):
                        b.inc(v.scalar_tensor_tensor(yfm[p][:], yfm[p][:], nw_sb[:, p:p + 1], rbc[:], ALU.mult, ALU.mult), "V")
                    b.V(v)
                ynv = b.n["V"]
                tap_fm("yfm", yfm, T)

                def gp2(g):
                    b.wait(g, "T", wfree[0]); b.wait(g, "T", wfree[1])
                    g.dma_start(out=wob[0][:], in_=wout[0].rearrange("p k m -> p (k m)")).then_inc(b.sem("p4o0"), 16)
                    b.n["p4o0"] = b.n.get("p4o0", 0) + 16
                    wdma[0] = b.n["p4o0"]
                b.G(gp2)
                for dp in range(16):
                    par = dp % 2
                    bk0 = 2 * par
                    if dp + 1 < 16:
                        def g(g, dp=dp):
                            npar = (dp + 1) % 2
                            if dp >= 1:
                                b.wait(g, "T", wfree[npar])
                            g.dma_start(out=wob[npar][:], in_=wout[dp + 1].rearrange("p k m -> p (k m)")).then_inc(b.sem(f"p4o{npar}"), 16)
                            b.n[f"p4o{npar}"] = b.n.get(f"p4o{npar}", 0) + 16
                            wdma[npar] = b.n[f"p4o{npar}"]
                        b.G(g)
                    def t(t, dp=dp, par=par, bk0=bk0):
                        b.wait(t, f"p4o{par}", wdma[par])
                        b.wait(t, "V", ynv)
                        for si in range(NS):
                            b.wait(t, "V", mp_ev[bk0 + si])
                        for si, (st, wdt) in enumerate(TS_):
                            for k in range(KI):
                                last = t.matmul(mp[bk0 + si][:, 0:wdt], wob[par][:, k * 128:(k + 1) * 128],
                                                yfm[k][:, st:st + wdt], start=(k == 0), stop=(k == KI - 1))
                            mp_mm[bk0 + si] = b.inc(last, "T")
                        wfree[par] = b.n["T"]
                    b.T(t)
                    for si, (st, wdt) in enumerate(TS_):
                        def v(v, dp=dp, bk0=bk0, si=si, st=st, wdt=wdt):
                            b.wait(v, "T", mp_mm[bk0 + si])
                            mp_ev[bk0 + si] = b.step(v, v.tensor_copy(ofm[dp][:, st:st + wdt], mp[bk0 + si][:, 0:wdt]), "V")
                        b.V(v)
                def vfin(v):
                    b.wait(v, "T"); b.wait(v, "S")
                    b.step(v, v.engine_nop(), "V")
                b.V(vfin)

        def p4b(ofm, q_fm):
            b.barrier()
            with ExitStack() as pc:
                xm = [b.sb(f"p4b_xm{i}", [128, D_MODEL], F32, pc, side="right") for i in range(2)]
                x1t = b.sb("p4b_x1t", [128, D_MODEL], F32, pc, side="right")
                qtk = b.sb("p4b_qtk", [128, D_MODEL], BF16, pc, side="right")
                st2 = b.sb("p4b_st2", [128, 8], F32, pc, side="right")
                sq2 = b.sb("p4b_sq2", [128, D_MODEL], F32, pc, side="right")
                sq3 = b.sb("p4b_sq3", [128, D_MODEL], F32, pc, side="right")
                trp2 = [b.ps(f"p4b_tp{i}", [128, 128], F32, pc) for i in range(2)]
                trq = [b.ps(f"p4b_tq{i}", [128, 128], BF16, pc) for i in range(2)]
                tr_ev2 = [0, 0]
                trq_ev = [0, 0]
                xm_free = [0] * NTM
                parkd = 0
                for tt in range(NTM):
                    buf = tt % 2
                    def g(g, tt=tt, buf=buf):
                        if tt >= 2:
                            b.wait(g, "V", xm_free[tt - 2])
                        g.dma_start(out=xm[buf][:], in_=x_core[3 + tt * 128:3 + (tt + 1) * 128, :]).then_inc(b.sem(f"p4x{buf}"), 16)
                        b.n[f"p4x{buf}"] = b.n.get(f"p4x{buf}", 0) + 16
                    b.G(g)
                    x0d = b.n[f"p4x{buf}"]
                    for dp in range(16):
                        pbk = dp % 2
                        def t(t, tt=tt, dp=dp, pbk=pbk):
                            b.wait(t, "V", tr_ev2[pbk])
                            b.inc(t.transpose(trp2[pbk][:], ofm[dp][:, tt * 128:(tt + 1) * 128], ident_f[:]), "T")
                        b.T(t)
                        def v(v, tt=tt, dp=dp, pbk=pbk, buf=buf, tv=b.n["T"], x0d=x0d, parkd=parkd):
                            b.wait(v, "T", tv)
                            b.wait(v, f"p4x{buf}", x0d)
                            b.wait(v, "dP", parkd)
                            tr_ev2[pbk] = b.inc(v.tensor_tensor(x1t[:, dp * 128:(dp + 1) * 128], trp2[pbk][:],
                                                                xm[buf][:, dp * 128:(dp + 1) * 128], ALU.add), "V")
                        b.V(v)
                    xm_free[tt] = b.n["V"]
                    x1v = b.n["V"]
                    if "x1" in tap_t:
                        def gx(g, tt=tt, x1v=x1v):
                            b.wait(g, "V", x1v)
                            g.dma_start(out=tap_t["x1"][tt * 128:(tt + 1) * 128, :], in_=x1t[:]).then_inc(b.sem("dP"), 16)
                            b.n["dP"] += 16
                        b.G(gx)
                    def gp(g, tt=tt, x1v=x1v):
                        b.wait(g, "V", x1v)
                        g.dma_start(out=x1_scr[tt * 128:(tt + 1) * 128, :], in_=x1t[:]).then_inc(b.sem("dP"), 16)
                        b.n["dP"] += 16
                    b.G(gp)
                    parkd = b.n["dP"]
                    def v2(v):
                        b.step(v, v.reduce_sum(st2[:, 0:1], x1t[:], axis=AX.X), "V")
                        b.step(v, v.tensor_scalar(st2[:, 1:2], st2[:, 0:1], 1.0 / D_MODEL, None, ALU.mult), "V")
                        b.step(v, v.tensor_scalar(sq2[:], x1t[:], st2[:, 1:2], None, ALU.subtract), "V")
                    b.V(v2)
                    def v3(v):
                        b.step(v, v.scalar_tensor_tensor(sq3[:], sq2[:], 1.0, sq2[:], ALU.mult, ALU.mult,
                                                         accum_out=st2[:, 2:3]), "V")
                        b.step(v, v.tensor_scalar(st2[:, 3:4], st2[:, 2:3], 1.0 / D_MODEL, 1e-5, ALU.mult, ALU.add), "V")
                        b.inc(v.reciprocal(st2[:, 4:5], st2[:, 3:4]), "V")
                    b.V(v3)
                    def s3(s, vv=b.n["V"]):
                        b.wait(s, "V", vv)
                        b.inc(s.activation(st2[:, 5:6], st2[:, 4:5], AF.Sqrt), "S")
                    b.S(s3)
                    def v4(v, sv=b.n["S"]):
                        b.wait(v, "S", sv)
                        b.step(v, v.tensor_scalar(sq2[:], sq2[:], st2[:, 5:6], None, ALU.mult), "V")
                        b.step(v, v.reduce_sum(st2[:, 0:1], sq2[:], axis=AX.X), "V")
                        b.step(v, v.tensor_scalar(st2[:, 1:2], st2[:, 0:1], 1.0 / D_MODEL, None, ALU.mult), "V")
                        b.step(v, v.tensor_scalar(sq2[:], sq2[:], st2[:, 1:2], None, ALU.subtract), "V")
                    b.V(v4)
                    def v5(v):
                        b.step(v, v.scalar_tensor_tensor(sq3[:], sq2[:], 1.0, sq2[:], ALU.mult, ALU.mult,
                                                         accum_out=st2[:, 2:3]), "V")
                        b.step(v, v.tensor_scalar(st2[:, 3:4], st2[:, 2:3], 1.0 / D_MODEL, 1e-8, ALU.mult, ALU.add), "V")
                        b.inc(v.reciprocal(st2[:, 4:5], st2[:, 3:4]), "V")
                    b.V(v5)
                    def s5(s, vv=b.n["V"]):
                        b.wait(s, "V", vv)
                        b.inc(s.activation(st2[:, 5:6], st2[:, 4:5], AF.Sqrt), "S")
                    b.S(s5)
                    def v6(v, sv=b.n["S"]):
                        b.wait(v, "S", sv)
                        b.step(v, v.tensor_scalar(sq2[:], sq2[:], st2[:, 5:6], None, ALU.mult), "V")
                        b.step(v, v.reduce_max(st2[:, 6:7], sq2[:], axis=AX.X, apply_absolute_value=True), "V")
                        b.step(v, v.reciprocal(st2[:, 7:8], st2[:, 6:7]), "V")
                        b.step(v, v.tensor_scalar(sq2[:], sq2[:], st2[:, 7:8], 127.0, ALU.mult, ALU.mult), "V")
                        b.step(v, v.tensor_scalar(sq2[:], sq2[:], MAGIC, None, ALU.add), "V")
                        b.step(v, v.tensor_scalar(sq2[:], sq2[:], MAGIC, None, ALU.subtract), "V")
                        b.step(v, v.tensor_scalar(qtk[:], sq2[:], st2[:, 6:7], 1.0 / 127.0, ALU.mult, ALU.mult), "V")
                    b.V(v6)
                    qv = b.n["V"]
                    for k in range(KD):
                        pbk = k % 2
                        def t2(t, tt=tt, k=k, pbk=pbk, qv=qv):
                            b.wait(t, "V", qv)
                            b.wait(t, "V", trq_ev[pbk])
                            b.inc(t.transpose(trq[pbk][:], qtk[:, k * 128:(k + 1) * 128], ident_b[:]), "T")
                        b.T(t2)
                        def s6(v, tt=tt, k=k, pbk=pbk, tv=b.n["T"]):
                            b.wait(v, "T", tv)
                            trq_ev[pbk] = b.inc(v.tensor_copy(q_fm[k][:, tt * 128:(tt + 1) * 128], trq[pbk][:]), "V")
                        b.V(s6)
                    def v7(v, tv=b.n["T"]):
                        b.wait(v, "T", tv)
                        b.step(v, v.engine_nop(), "V")
                    b.V(v7)
                def vfin(v):
                    b.wait(v, "S"); b.wait(v, "T")
                    b.step(v, v.engine_nop(), "V")
                b.V(vfin)
            tap_fm("q", q_fm, T)

        def p5a(q_fm, pfm):
            b.barrier()
            with ExitStack() as pc:
                gwb = [b.sb(f"p5_gw{i}", [128, KD * 128], BF16, pc) for i in range(2)]
                uwb = [b.sb(f"p5_uw{i}", [128, KD * 128], BF16, pc) for i in range(2)]
                gsb = b.sb("p5_gs", [128, 512], F32, pc)
                psg = [b.ps(f"p5_g{i}", [128, 512], F32, pc) for i in range(2)]
                psu = [b.ps(f"p5_u{i}", [128, 512], F32, pc) for i in range(2)]
                wdma = [0, 0]; wfree = [0, 0]
                g_mm = [0, 0]; u_mm = [0, 0]; gu_ev = [0, 0]

                def gp(g):
                    g.dma_start(out=gwb[0][:], in_=wg[0].rearrange("p k m -> p (k m)")).then_inc(b.sem("p5g0"), 16)
                    g.dma_start(out=uwb[0][:], in_=wu[0].rearrange("p k m -> p (k m)")).then_inc(b.sem("p5g0"), 16)
                    b.n["p5g0"] = b.n.get("p5g0", 0) + 32
                b.G(gp)
                wdma[0] = b.n["p5g0"]

                for j in range(KH):
                    par = j % 2
                    if j + 1 < KH:
                        def g(g, j=j):
                            npar = (j + 1) % 2
                            if j >= 1:
                                b.wait(g, "T", wfree[npar])
                            g.dma_start(out=gwb[npar][:], in_=wg[j + 1].rearrange("p k m -> p (k m)")).then_inc(b.sem(f"p5g{npar}"), 16)
                            g.dma_start(out=uwb[npar][:], in_=wu[j + 1].rearrange("p k m -> p (k m)")).then_inc(b.sem(f"p5g{npar}"), 16)
                            b.n[f"p5g{npar}"] = b.n.get(f"p5g{npar}", 0) + 32
                            wdma[npar] = b.n[f"p5g{npar}"]
                        b.G(g)
                    def s(v, j=j, par=par):
                        b.wait(v, f"p5g{par}", wdma[par])
                        if j >= 2:
                            b.wait(v, "T", wfree[par])
                        for wb in (gwb[par], uwb[par]):
                            b.step(v, v.tensor_scalar(wb[:], wb[:], 1e30, 1.0, ALU.mult, ALU.min), "V")
                            b.step(v, v.tensor_scalar(wb[:], wb[:], -1.0, None, ALU.max), "V")
                    b.V(s)
                    def t(t, j=j, par=par, sv=b.n["V"]):
                        b.wait(t, "V", sv)
                        for si in range(NS):
                            b.wait(t, "V", gu_ev[si % 2])
                        for si, (st, wdt) in enumerate(TS_):
                            for k in range(KD):
                                last = t.matmul(psg[si % 2][:, 0:wdt], gwb[par][:, k * 128:(k + 1) * 128],
                                                q_fm[k][:, st:st + wdt], start=(k == 0), stop=(k == KD - 1))
                            g_mm[si % 2] = b.inc(last, "T")
                            for k in range(KD):
                                last = t.matmul(psu[si % 2][:, 0:wdt], uwb[par][:, k * 128:(k + 1) * 128],
                                                q_fm[k][:, st:st + wdt], start=(k == 0), stop=(k == KD - 1))
                            u_mm[si % 2] = b.inc(last, "T")
                        wfree[par] = b.n["T"]
                    b.T(t)
                    for si, (st, wdt) in enumerate(TS_):
                        def s2(s, si=si, wdt=wdt):
                            b.wait(s, "T", g_mm[si % 2])
                            b.inc(s.activation(gsb[:, 0:wdt], psg[si % 2][:, 0:wdt], AF.Silu), "S")
                        b.S(s2)
                        def v(v, j=j, si=si, st=st, wdt=wdt, sv=b.n["S"]):
                            b.wait(v, "S", sv)
                            b.wait(v, "T", u_mm[si % 2])
                            gu_ev[si % 2] = b.step(v, v.tensor_tensor(pfm[j][:, st:st + wdt], gsb[:, 0:wdt],
                                                                      psu[si % 2][:, 0:wdt], ALU.mult), "V")
                        b.V(v)
                def vfin(v):
                    b.wait(v, "T"); b.wait(v, "S")
                    b.step(v, v.engine_nop(), "V")
                b.V(vfin)
            tap_fm("pfm", pfm, T)

        def p5b(pfm):
            b.barrier()
            with ExitStack() as pc:
                dwb = [b.sb(f"p5_dw{i}", [128, KH * 128], BF16, pc, side="right") for i in range(2)]
                dfm = [b.sb(f"p5_df{dp}", [128, 512], BF16, pc, side="right") for dp in range(16)]
                xrl = [b.sb(f"p5_xr{i}", [128, D_MODEL], F32, pc, side="right") for i in range(1)]
                ob = [b.sb(f"p5_ob{i}", [128, D_MODEL], F32, pc, side="right") for i in range(2)]
                psd = [b.ps(f"p5_d{i}", [128, 512], F32, pc) for i in range(2)]
                trp3 = [b.ps(f"p5_t{i}", [128, 128], BF16, pc) for i in range(2)]
                ddma = [0, 0]; wfree = [0, 0]
                d_mm = [0, 0]; d_ev = [0, 0]
                tr_ev3 = [0, 0]
                ob_done = [0, 0]

                pv = b.n["V"]
                for si, (st, wdt) in enumerate(TS_):
                    for dp in range(16):
                        idx = si * 16 + dp
                        par = idx % 2
                        def g(g, dp=dp, par=par, idx=idx):
                            if idx >= 2:
                                b.wait(g, "T", wfree[par])
                            g.dma_start(out=dwb[par][:], in_=wd[dp].rearrange("p k m -> p (k m)")).then_inc(b.sem(f"p5d{par}"), 16)
                            b.n[f"p5d{par}"] = b.n.get(f"p5d{par}", 0) + 16
                            ddma[par] = b.n[f"p5d{par}"]
                        b.G(g)
                        def s(s, par=par):
                            b.wait(s, f"p5d{par}", ddma[par])
                            b.inc(s.activation(dwb[par][:], dwb[par][:], AF.Sign), "S")
                        b.S(s)
                        def t(t, par=par, st=st, wdt=wdt, sv=b.n["S"], pv=pv):
                            b.wait(t, "S", sv)
                            b.wait(t, "V", pv)
                            b.wait(t, "V", d_ev[par])
                            for jb in range(KH):
                                last = t.matmul(psd[par][:, 0:wdt], dwb[par][:, jb * 128:(jb + 1) * 128],
                                                pfm[jb][:, st:st + wdt], start=(jb == 0), stop=(jb == KH - 1))
                            d_mm[par] = b.inc(last, "T")
                            wfree[par] = d_mm[par]
                        b.T(t)
                        def v(v, dp=dp, par=par, wdt=wdt):
                            b.wait(v, "T", d_mm[par])
                            d_ev[par] = b.inc(v.tensor_copy(dfm[dp][:, 0:wdt], psd[par][:, 0:wdt]), "V")
                        b.V(v)
                    dfmv = b.n["V"]
                    for qq in range(wdt // 128):
                        tt = st // 128 + qq
                        buf = tt % 2
                        def g2(g, tt=tt, buf=buf):
                            b.wait(g, "V")
                            g.dma_start(out=xrl[0][:], in_=x1_scr[tt * 128:(tt + 1) * 128, :]).then_inc(b.sem("p5xr"), 16)
                            b.n["p5xr"] = b.n.get("p5xr", 0) + 16
                        b.G(g2)
                        xrd = b.n["p5xr"]
                        def vob(v, buf=buf, odv=ob_done[buf]):
                            b.wait(v, "dO", odv)
                            b.step(v, v.engine_nop(), "V")
                        b.V(vob)
                        for dp in range(16):
                            pbk = dp % 2
                            def t2(t, qq=qq, dp=dp, pbk=pbk, dfmv=dfmv):
                                b.wait(t, "V", dfmv)
                                b.wait(t, "V", tr_ev3[pbk])
                                b.inc(t.transpose(trp3[pbk][:], dfm[dp][:, qq * 128:(qq + 1) * 128], ident_b[:]), "T")
                            b.T(t2)
                            def v2(v, dp=dp, pbk=pbk, buf=buf, tv=b.n["T"], xrd=xrd):
                                b.wait(v, "T", tv)
                                b.wait(v, "p5xr", xrd)
                                tr_ev3[pbk] = b.inc(v.tensor_tensor(ob[buf][:, dp * 128:(dp + 1) * 128],
                                                                    xrl[0][:, dp * 128:(dp + 1) * 128],
                                                                    trp3[pbk][:], ALU.add), "V")
                            b.V(v2)
                        obv = b.n["V"]
                        def g3(g, tt=tt, buf=buf, obv=obv):
                            b.wait(g, "V", obv)
                            g.dma_start(out=out[tt * 128:(tt + 1) * 128, :], in_=ob[buf][:]).then_inc(b.sem("dO"), 16)
                            b.n["dO"] += 16
                        b.G(g3)
                        ob_done[buf] = b.n["dO"]
                def vfin(v):
                    b.wait(v, "T")
                    b.step(v, v.engine_nop(), "V")
                b.V(vfin)

        xbc_ctx = ExitStack()
        xbc_s = [b.sb(f"xbc{g}", [128, T], BF16, xbc_ctx, side="right") for g in range(N_XBC)]
        csgrp_ctx = ExitStack()
        cs_sb = b.sb("cs_sb", [64, T], F32, csgrp_ctx, side="right")
        cs_tok = b.sb("cs_tok", [128, NC * 128], F32, csgrp_ctx, side="right")
        ecs_tok = b.sb("ecs_tok", [128, NC * 128], F32, csgrp_ctx, side="right")
        dt_tok = b.sb("dt_tok", [128, NC * 128], F32, csgrp_ctx, side="right")
        dcy_tok = b.sb("dcy_tok", [128, NC * 128], F32, csgrp_ctx, side="right")
        et_row = b.sb("et_row", [1, NC * 64], F32, csgrp_ctx, side="right")
        etc_row = b.sb("etc_row", [1, NC * 64], F32, csgrp_ctx, side="right")
        et_bc = b.sb("et_bc", [128, NC * 64], F32, csgrp_ctx, side="right")
        etc_bc = b.sb("etc_bc", [128, NC * 64], F32, csgrp_ctx, side="right")
        t4_st = b.sb("t4_st", [64, NC], F32, csgrp_ctx, side="right")

        with ExitStack() as x1ctx:
            xln_fm = [b.sb(f"xlnA{k}", [128, XROWS], BF16, x1ctx) for k in range(KD)]
            ln_pass("p1", xln_fm)
            p2(xln_fm, xbc_s)
            tap_fm("xln", xln_fm, XROWS)
        if "xbc" in tap_t:
            tap_fm("xbc", xbc_s, T)

        yfm_ctx = ExitStack()
        yfm = [b.sb(f"yfm{p}", [128, T], BF16, yfm_ctx) for p in range(32)]
        with ExitStack() as p3ctx:
            S_sb = b.sb("S_sb", [128, NHEADS * 64], F32, p3ctx)
            p3(xbc_s, yfm, S_sb)
        csgrp_ctx.close()
        xbc_ctx.close()

        ofm_ctx = ExitStack()
        ofm = [b.sb(f"p4_ofm{dp}", [128, T], F32, ofm_ctx, side="right") for dp in range(16)]
        with ExitStack() as x2ctx:
            xln_fm2 = [b.sb(f"xlnB{k}", [128, XROWS], BF16, x2ctx) for k in range(KD)]
            ln_pass("p4ln", xln_fm2)
            p4a(xln_fm2, yfm, ofm)
        yfm_ctx.close()

        q_ctx = ExitStack()
        q_fm = [b.sb(f"qfm{k}", [128, T], BF16, q_ctx) for k in range(KD)]
        p4b(ofm, q_fm)
        ofm_ctx.close()

        pfm_ctx = ExitStack()
        pfm = [b.sb(f"pfm{j}", [128, T], BF16, pfm_ctx, side="right") for j in range(KH)]
        p5a(q_fm, pfm)
        q_ctx.close()
        p5b(pfm)
        pfm_ctx.close()

        def fin(sync):
            for name in sorted(b.n):
                if name in ("T", "V", "S", "G"):
                    continue
                if b.n[name]:
                    b.wait(sync, name)
            for name in ("T", "V", "S", "G"):
                if b.n.get(name):
                    b.wait(sync, name)
        block.sync(fin)

    return nc, tap_t


def _tile_w(wT, kblocks, mblocks):
    K, M = wT.shape
    assert K == kblocks * 128 and M == mblocks * 128
    return np.ascontiguousarray(wT.reshape(kblocks, 128, mblocks, 128).transpose(2, 1, 0, 3))


def prep_inputs(L0_attn_x, L0_attn_in_proj_w, L0_attn_conv_w, L0_attn_conv_b,
                L0_attn_dt_bias, L0_attn_A_log, L0_attn_D, L0_attn_norm_w,
                L0_attn_out_proj_w, L0_ffn_W_g, L0_ffn_W_u, L0_ffn_W_d,
                n_cores=8, n_chunks=4):
    T = n_chunks * CHUNK
    XROWS = -(-(T + 3) // 128) * 128
    f = np.float32
    x = np.asarray(L0_attn_x, f)

    wip_p = np.zeros((EP * 128, D_MODEL), f)
    wip_p[:2 * D_INNER + 2 * D_STATE + NHEADS] = np.asarray(L0_attn_in_proj_w, f)
    wip_t = _tile_w(np.ascontiguousarray(wip_p.T), KD, EP)
    wout_t = _tile_w(np.ascontiguousarray(np.asarray(L0_attn_out_proj_w, f).T), KI, 16)
    wg_t = _tile_w(np.ascontiguousarray(np.asarray(L0_ffn_W_g, f).T), KD, KH)
    wu_t = _tile_w(np.ascontiguousarray(np.asarray(L0_ffn_W_u, f).T), KD, KH)
    wd_t = _tile_w(np.ascontiguousarray(np.asarray(L0_ffn_W_d, f).T), KH, 16)
    cw = np.ascontiguousarray(np.asarray(L0_attn_conv_w, f).reshape(N_XBC, 128, 4).transpose(1, 0, 2).reshape(128, N_XBC * 4))
    cb = np.ascontiguousarray(np.asarray(L0_attn_conv_b, f).reshape(N_XBC, 128).T)
    nw = np.ascontiguousarray(np.asarray(L0_attn_norm_w, f).reshape(32, 128).T)
    dtbv = np.ascontiguousarray(np.asarray(L0_attn_dt_bias, f).reshape(64, 1))
    alogv = np.ascontiguousarray(np.asarray(L0_attn_A_log, f).reshape(64, 1))
    dvecv = np.ascontiguousarray(np.asarray(L0_attn_D, f).reshape(1, 64))

    shared = dict(wip=wip_t, wout=wout_t, wg=wg_t, wu=wu_t, wd=wd_t,
                  conv_w=cw, conv_b=cb, norm_w=nw, dtb=dtbv, alog=alogv, dvec=dvecv)
    in_maps = []
    for c in range(n_cores):
        bb, half = c // 2, c % 2
        xc = np.zeros((XROWS, D_MODEL), f)
        if half == 1:
            xc[0:3] = x[bb, T - 3:T]
        xc[3:3 + T] = x[bb, half * T:(half + 1) * T]
        m = dict(shared)
        m["x_core"] = xc
        m["ggate"] = np.full((128, 1), float(half), f)
        in_maps.append(m)
    return in_maps


def kernel(**inputs):
    n_cores, n_chunks = 8, 4
    in_maps = prep_inputs(**inputs, n_cores=n_cores, n_chunks=n_chunks)
    nc, _ = build(n_cores=n_cores, n_chunks=n_chunks)
    res = run_bass_kernel_spmd(nc, in_maps, core_ids=list(range(n_cores)))
    T = n_chunks * CHUNK
    outp = np.empty((4, 2 * T, D_MODEL), np.float32)
    for c in range(n_cores):
        bb, half = c // 2, c % 2
        outp[bb, half * T:(half + 1) * T] = res.results[c]["out"]
    return outp


# version 07
# speedup vs baseline: 1.3381x; 1.2324x over previous
import numpy as np
from contextlib import ExitStack

import concourse.bass as bass
import concourse.mybir as mybir
from concourse.bass_utils import run_bass_kernel_spmd

F32 = mybir.dt.float32
BF16 = mybir.dt.bfloat16
AF = mybir.ActivationFunctionType
ALU = mybir.AluOpType
AX = mybir.AxisListType

CHUNK = 256
D_MODEL = 2048
D_INNER = 4096
D_STATE = 128
NHEADS = 64
KD = D_MODEL // 128
KI = D_INNER // 128
KH = 8192 // 128
EP = 67
E_XBC0 = 32
N_XBC = 34
H_FFN = 8192
MAGIC = 12582912.0
DBG_NO_YOFF = False
DBG_NO_YDIAG = False


def _spans(total, width=512):
    return [(s, min(width, total - s)) for s in range(0, total, width)]


class PB:
    def __init__(self, nc, block, ctx):
        self.nc, self.block, self.ctx = nc, block, ctx
        self._sems, self.n = {}, {}

    def sem(self, name):
        if name not in self._sems:
            self._sems[name] = self.ctx.enter_context(self.nc.semaphore(name))
            self.n[name] = 0
        return self._sems[name]

    def inc(self, instr, name, k=1):
        instr.then_inc(self.sem(name), k)
        self.n[name] += k
        return self.n[name]

    def wait(self, eng, name, val=None):
        if val is None:
            val = self.n.get(name, 0)
        self.sem(name)
        if val > 0:
            eng.wait_ge(self.sem(name), val)

    def step(self, eng, instr, name):
        v = self.inc(instr, name)
        eng.wait_ge(self.sem(name), v)
        return v

    def sb(self, name, shape, dtype, ctx=None, side="left"):
        return (ctx or self.ctx).enter_context(self.nc.sbuf_tensor(name, shape, dtype, side=side))

    def ps(self, name, shape, dtype=F32, ctx=None):
        return (ctx or self.ctx).enter_context(self.nc.psum_tensor(name, shape, dtype))

    def barrier(self):
        names = tuple(nm for nm in self.n if self.n[nm] > 0)
        for sec in (self.V, self.S, self.T, self.G):
            def f(e, names=names):
                for nm in names:
                    self.wait(e, nm)
            sec(f)

    def V(self, f):
        def w(v):
            self.wait(v, "V")
            f(v)
        self.block.vector(w)

    def S(self, f):
        def w(s):
            self.wait(s, "S")
            f(s)
        self.block.scalar(w)

    def T(self, f):
        def w(t):
            self.wait(t, "T")
            f(t)
        self.block.tensor(w)

    def G(self, f):
        def w(g):
            self.wait(g, "G")
            f(g)
        self.block.gpsimd(w)


def build(n_cores=8, n_chunks=4, taps=()):
    T = n_chunks * CHUNK
    XROWS = -(-(T + 3) // 128) * 128
    NTT = XROWS // 128
    NTM = T // 128
    NC = n_chunks
    TS_ = _spans(T)
    NS = len(TS_)
    taps = set(taps)

    nc = bass.Bass("TRN2", target_bir_lowering=False)

    x_core = nc.dram_tensor("x_core", [XROWS, D_MODEL], F32, kind="ExternalInput").ap()
    wip = nc.dram_tensor("wip", [EP, 128, KD, 128], F32, kind="ExternalInput").ap()
    wout = nc.dram_tensor("wout", [16, 128, KI, 128], F32, kind="ExternalInput").ap()
    wg = nc.dram_tensor("wg", [KH, 128, KD, 128], F32, kind="ExternalInput").ap()
    wu = nc.dram_tensor("wu", [KH, 128, KD, 128], F32, kind="ExternalInput").ap()
    wd = nc.dram_tensor("wd", [16, 128, KH, 128], F32, kind="ExternalInput").ap()
    conv_w = nc.dram_tensor("conv_w", [128, N_XBC * 4], F32, kind="ExternalInput").ap()
    conv_b = nc.dram_tensor("conv_b", [128, N_XBC], F32, kind="ExternalInput").ap()
    norm_w = nc.dram_tensor("norm_w", [128, 32], F32, kind="ExternalInput").ap()
    dtb = nc.dram_tensor("dtb", [64, 1], F32, kind="ExternalInput").ap()
    alog = nc.dram_tensor("alog", [64, 1], F32, kind="ExternalInput").ap()
    dvec = nc.dram_tensor("dvec", [1, 64], F32, kind="ExternalInput").ap()
    ggate = nc.dram_tensor("ggate", [128, 1], F32, kind="ExternalInput").ap()

    out = nc.dram_tensor("out", [T, D_MODEL], F32, kind="ExternalOutput").ap()
    x1_scr = nc.dram_tensor("x1_scr", [T, D_MODEL], F32, kind="Internal").ap()
    ag_in = nc.dram_tensor("ag_in", [128, NHEADS * 64], F32, kind="Internal").ap()
    ag_out = nc.dram_tensor("ag_out", [256, NHEADS * 64], F32, kind="Internal").ap()

    tap_t = {}
    for name, shape in (("xln", [KD * 128, XROWS]), ("xbc", [N_XBC * 128, T]),
                        ("dt", [64, T]), ("cs", [64, T]), ("yfm", [D_INNER, T]),
                        ("x1", [T, D_MODEL]), ("q", [D_MODEL, T]), ("pfm", [H_FFN, T]),
                        ("ssb", [128, NHEADS * 64]), ("ypre", [D_INNER, T])):
        if name in taps:
            tap_t[name] = nc.dram_tensor("tap_" + name, shape, F32, kind="ExternalOutput").ap()

    with ExitStack() as ctx:
        block = ctx.enter_context(nc.Block())
        b = PB(nc, block, ctx)

        tapbuf = b.sb("tapbuf", [128, XROWS if tap_t else 1], F32)

        def tap_fm(name, tiles, cols):
            if name not in tap_t:
                return
            dst = tap_t[name]
            buf = tapbuf
            for i, tile in enumerate(tiles):
                def v(v, i=i, tile=tile):
                    b.wait(v, "T"); b.wait(v, "S"); b.wait(v, "dT")
                    b.step(v, v.tensor_copy(buf[:, 0:cols], tile[:, 0:cols]), "V")
                b.V(v)
                def g(g, i=i):
                    b.wait(g, "V")
                    g.dma_start(out=dst[i * 128:(i + 1) * 128, 0:cols], in_=buf[:, 0:cols]).then_inc(b.sem("dT"), 16)
                    b.n["dT"] += 16
                b.G(g)

        ident_b = b.sb("ident_b", [128, 128], BF16)
        ident_f = b.sb("ident_f", [128, 128], F32)
        ones_col_b = b.sb("ones_col_b", [128, 1], BF16)
        ones_row_f = b.sb("ones_row_f", [1, 128], F32)
        mask01 = b.sb("mask01", [128, 256], F32)
        cw_sb = b.sb("cw_sb", [128, N_XBC * 4], F32)
        cb_sb = b.sb("cb_sb", [128, N_XBC], F32)
        nw_sb = b.sb("nw_sb", [128, 32], F32)
        dtb_sb = b.sb("dtb_sb", [64, 1], F32)
        alog_sb = b.sb("alog_sb", [64, 1], F32)
        expa_sb = b.sb("expa_sb", [64, 1], F32)
        dvec_sb = b.sb("dvec_sb", [1, 64], F32)
        dbc_sb = b.sb("dbc_sb", [128, 64], F32)
        ggate_sb = b.sb("ggate_sb", [128, 1], F32)

        def g0(g):
            for dst, src in ((cw_sb, conv_w), (cb_sb, conv_b), (nw_sb, norm_w),
                             (dtb_sb, dtb), (alog_sb, alog), (dvec_sb, dvec),
                             (ggate_sb, ggate)):
                g.dma_start(out=dst[:], in_=src).then_inc(b.sem("dC"), 16)
                b.n["dC"] += 16
            for tile, val in ((ident_b, 0.0), (ident_f, 0.0), (ones_col_b, 1.0),
                              (ones_row_f, 1.0), (mask01, 1.0)):
                b.inc(g.memset(tile[:], val), "G")
            b.wait(g, "G")
            b.inc(g.affine_select(out=ident_b[:], in_=ident_b[:], compare_op=ALU.not_equal,
                                  fill=1.0, base=0, pattern=[[-1, 128]], channel_multiplier=1), "G")
            b.inc(g.affine_select(out=ident_f[:], in_=ident_f[:], compare_op=ALU.not_equal,
                                  fill=1.0, base=0, pattern=[[-1, 128]], channel_multiplier=1), "G")
            b.inc(g.affine_select(out=mask01[:], in_=mask01[:], compare_op=ALU.is_ge,
                                  fill=0.0, base=0, pattern=[[1, 256]], channel_multiplier=-1), "G")
        b.G(g0)

        def s0(s):
            b.wait(s, "dC")
            b.inc(s.activation(expa_sb[:], alog_sb[:], AF.Exp), "S")
        b.S(s0)

        with ExitStack() as c0:
            dbc_ps = b.ps("dbc_ps", [128, 64], F32, c0)
            def t0(t):
                b.wait(t, "dC")
                b.wait(t, "G")
                b.inc(t.matmul(dbc_ps[:], ones_row_f[:], dvec_sb[:], start=True, stop=True), "T")
            b.T(t0)
            def v0(v):
                b.wait(v, "T")
                b.inc(v.tensor_copy(dbc_sb[:], dbc_ps[:]), "V")
            b.V(v0)


        def ln_pass(prefix, xln_fm):
            b.barrier()
            with ExitStack() as pc:
                xt = [b.sb(f"{prefix}_xt{i}", [128, D_MODEL], F32, pc) for i in range(2)]
                xlt = [b.sb(f"{prefix}_xlt{i}", [128, D_MODEL], BF16, pc) for i in range(2)]
                sq = b.sb(f"{prefix}_sq", [128, D_MODEL], F32, pc)
                sqs = b.sb(f"{prefix}_sqs", [128, D_MODEL], F32, pc)
                stat = b.sb(f"{prefix}_stat", [128, 8], F32, pc)
                trp = [b.ps(f"{prefix}_trp{i}", [128, 128], BF16, pc) for i in range(2)]
                xt_free = [0] * NTT
                xlt_free = [0] * NTT
                ev_done = [0, 0]
                for tt in range(NTT):
                    buf = tt % 2
                    dn = f"{prefix}_ld{buf}"
                    def g(g, tt=tt, buf=buf, dn=dn):
                        if tt >= 2:
                            b.wait(g, "V", xt_free[tt - 2])
                        g.dma_start(out=xt[buf][:], in_=x_core[tt * 128:(tt + 1) * 128, :]).then_inc(b.sem(dn), 16)
                        b.n[dn] = b.n.get(dn, 0) + 16
                    b.G(g)
                    def v(v, tt=tt, buf=buf, dn=dn, dv=b.n[dn]):
                        b.wait(v, dn, dv)
                        b.step(v, v.reduce_sum(stat[:, 0:1], xt[buf][:], axis=AX.X), "V")
                        b.step(v, v.tensor_scalar(stat[:, 1:2], stat[:, 0:1], 1.0 / D_MODEL, None, ALU.mult), "V")
                        xt_free[tt] = b.step(v, v.tensor_scalar(sq[:], xt[buf][:], stat[:, 1:2], None, ALU.subtract), "V")
                    b.V(v)
                    def v2(v):
                        b.step(v, v.scalar_tensor_tensor(sqs[:], sq[:], 1.0, sq[:], ALU.mult, ALU.mult,
                                                         accum_out=stat[:, 2:3]), "V")
                        b.step(v, v.tensor_scalar(stat[:, 3:4], stat[:, 2:3], 1.0 / D_MODEL, 1e-5, ALU.mult, ALU.add), "V")
                        b.inc(v.reciprocal(stat[:, 4:5], stat[:, 3:4]), "V")
                    b.V(v2)
                    def s2(s, vv=b.n["V"]):
                        b.wait(s, "V", vv)
                        b.inc(s.activation(stat[:, 5:6], stat[:, 4:5], AF.Sqrt), "S")
                    b.S(s2)
                    def v3(v, tt=tt, buf=buf, sv=b.n["S"]):
                        b.wait(v, "S", sv)
                        if tt >= 2:
                            b.wait(v, "T", xlt_free[tt - 2])
                        b.inc(v.tensor_scalar(xlt[buf][:], sq[:], stat[:, 5:6], None, ALU.mult), "V")
                    b.V(v3)
                    xlv = b.n["V"]
                    for k in range(KD):
                        pbk = k % 2
                        def t(t, tt=tt, buf=buf, k=k, pbk=pbk, xlv=xlv):
                            b.wait(t, "V", xlv)
                            b.wait(t, "V", ev_done[pbk])
                            xlt_free[tt] = b.inc(t.transpose(trp[pbk][:], xlt[buf][:, k * 128:(k + 1) * 128], ident_b[:]), "T")
                        b.T(t)
                        def s3(v, tt=tt, k=k, pbk=pbk, tv=b.n["T"]):
                            b.wait(v, "T", tv)
                            ev_done[pbk] = b.inc(v.tensor_copy(xln_fm[k][:, tt * 128:(tt + 1) * 128], trp[pbk][:]), "V")
                        b.V(s3)
                def vf(v):
                    b.step(v, v.engine_nop(), "V")
                b.V(vf)

        def p2(xln_fm, xbc_s):
            b.barrier()
            with ExitStack() as pc:
                wbuf = [b.sb(f"p2_w{i}", [128, KD * 128], BF16, pc) for i in range(2)]
                mp = [b.ps(f"p2_mp{i}", [128, 512], F32, pc) for i in range(4)]
                hp = b.ps("p2_hp", [128, (N_XBC + 1) * 3], F32, pc)
                xp = b.sb("p2_xp", [128, T + 3], F32, pc)
                dt_sb = b.sb("dt_sb", [64, T], F32, pc)
                da_sb = b.sb("da_sb", [64, T], F32, pc)
                ecs_sb = b.sb("ecs_sb", [64, T], F32, pc)
                dcy_sb = b.sb("dcy_sb", [64, T], F32, pc)
                gate_rst = b.sb("gate_rst", [64, T], F32, pc)
                cacc = b.sb("p2_cacc", [128, T], F32, pc)
                hsb = b.sb("p2_hsb", [128, (N_XBC + 1) * 3], F32, pc)
                k3 = b.sb("p2_k3", [128, N_XBC * 3], F32, pc)
                h6 = b.sb("p2_h6", [128, 6], F32, pc)

                etiles = list(range(E_XBC0, EP))
                wdma = [0, 0]
                wfree = [0, 0]
                mp_mm = [0] * 4
                mp_ev = [0] * 4
                xp_free = [0]

                def gp(g):
                    b.inc(g.memset(gate_rst[:], 1.0), "G")
                    b.wait(g, "G")
                    for c in range(NC):
                        b.inc(g.memset(gate_rst[:, c * CHUNK:c * CHUNK + 1], 0.0), "G")
                    g.dma_start(out=wbuf[0][:], in_=wip[etiles[0]].rearrange("p k m -> p (k m)")).then_inc(b.sem("p2w0"), 16)
                    b.n["p2w0"] = b.n.get("p2w0", 0) + 16
                b.G(gp)
                wdma[0] = b.n["p2w0"]

                for i, e in enumerate(etiles):
                    par = i % 2
                    gi = e - E_XBC0
                    if i + 1 < len(etiles):
                        def g(g, i=i):
                            npar = (i + 1) % 2
                            if i >= 1:
                                b.wait(g, "T", wfree[npar])
                            g.dma_start(out=wbuf[npar][:], in_=wip[etiles[i + 1]].rearrange("p k m -> p (k m)")).then_inc(b.sem(f"p2w{npar}"), 16)
                            b.n[f"p2w{npar}"] = b.n.get(f"p2w{npar}", 0) + 16
                            wdma[npar] = b.n[f"p2w{npar}"]
                        b.G(g)
                    bk0 = 2 * par
                    def t(t, i=i, par=par, bk0=bk0, gi=gi):
                        b.wait(t, f"p2w{par}", wdma[par])
                        for si in range(NS):
                            b.wait(t, "V", mp_ev[bk0 + si])
                        for si, (st, wdt) in enumerate(TS_):
                            for k in range(KD):
                                last = t.matmul(mp[bk0 + si][:, 0:wdt], wbuf[par][:, k * 128:(k + 1) * 128],
                                                xln_fm[k][:, 3 + st:3 + st + wdt],
                                                start=(k == 0), stop=(k == KD - 1))
                            mp_mm[bk0 + si] = b.inc(last, "T")
                        for k in range(KD):
                            last = t.matmul(hp[:, gi * 3:gi * 3 + 3], wbuf[par][:, k * 128:(k + 1) * 128],
                                            xln_fm[k][:, 0:3], start=(k == 0), stop=(k == KD - 1))
                        wfree[par] = b.inc(last, "T")
                    b.T(t)
                    halo_mm = b.n["T"]
                    if gi < N_XBC:
                        def v(v, i=i, bk0=bk0, gi=gi):
                            b.wait(v, "S")
                            b.wait(v, "V", xp_free[0])
                            for si, (st, wdt) in enumerate(TS_):
                                b.wait(v, "T", mp_mm[bk0 + si])
                                mp_ev[bk0 + si] = b.inc(v.tensor_copy(xp[:, 3 + st:3 + st + wdt], mp[bk0 + si][:, 0:wdt]), "V")
                            v.wait_ge(b.sem("V"), b.n["V"])
                            b.step(v, v.tensor_copy(k3[:, gi * 3:gi * 3 + 3], xp[:, 3:6]), "V")
                            w4 = cw_sb[:, gi * 4:gi * 4 + 4]
                            b.step(v, v.tensor_scalar(cacc[:, 3:T], xp[:, 3:T], w4[:, 0:1], None, ALU.mult), "V")
                            b.step(v, v.scalar_tensor_tensor(cacc[:, 3:T], xp[:, 4:T + 1], w4[:, 1:2], cacc[:, 3:T], ALU.mult, ALU.add), "V")
                            b.step(v, v.scalar_tensor_tensor(cacc[:, 3:T], xp[:, 5:T + 2], w4[:, 2:3], cacc[:, 3:T], ALU.mult, ALU.add), "V")
                            xp_free[0] = b.step(v, v.scalar_tensor_tensor(cacc[:, 3:T], xp[:, 6:T + 3], w4[:, 3:4], cacc[:, 3:T], ALU.mult, ALU.add), "V")
                        b.V(v)
                        def s(s, gi=gi, vv=None):
                            b.wait(s, "V", xp_free[0])
                            b.inc(s.activation(xbc_s[gi][:, 3:T], cacc[:, 3:T], AF.Silu,
                                               bias=cb_sb[:, gi:gi + 1]), "S")
                        b.S(s)
                    else:
                        def s(s, bk0=bk0):
                            for si, (st, wdt) in enumerate(TS_):
                                b.wait(s, "T", mp_mm[bk0 + si])
                                b.inc(s.activation(da_sb[0:64, st:st + wdt], mp[bk0 + si][0:64, 0:wdt],
                                                   AF.Exp, bias=dtb_sb[:, 0:1]), "S")
                        b.S(s)
                        def v(v, bk0=bk0, sv=b.n["S"]):
                            b.wait(v, "S", sv)
                            for si in range(NS):
                                mp_ev[bk0 + si] = b.inc(v.engine_nop(), "V")
                        b.V(v)

                def vh(v, hm=b.n["T"]):
                    b.wait(v, "T", hm)
                    b.step(v, v.tensor_copy(hsb[:], hp[:]), "V")
                b.V(vh)
                for gi in range(N_XBC):
                    def v(v, gi=gi):
                        b.wait(v, "S")
                        b.step(v, v.tensor_copy(h6[:, 0:3], hsb[:, gi * 3:gi * 3 + 3]), "V")
                        b.step(v, v.tensor_copy(h6[:, 3:6], k3[:, gi * 3:gi * 3 + 3]), "V")
                        w4 = cw_sb[:, gi * 4:gi * 4 + 4]
                        b.step(v, v.tensor_scalar(cacc[:, 0:3], h6[:, 0:3], w4[:, 0:1], None, ALU.mult), "V")
                        b.step(v, v.scalar_tensor_tensor(cacc[:, 0:3], h6[:, 1:4], w4[:, 1:2], cacc[:, 0:3], ALU.mult, ALU.add), "V")
                        b.step(v, v.scalar_tensor_tensor(cacc[:, 0:3], h6[:, 2:5], w4[:, 2:3], cacc[:, 0:3], ALU.mult, ALU.add), "V")
                        b.step(v, v.scalar_tensor_tensor(cacc[:, 0:3], h6[:, 3:6], w4[:, 3:4], cacc[:, 0:3], ALU.mult, ALU.add), "V")
                    b.V(v)
                    def s(s, gi=gi, vv=b.n["V"]):
                        b.wait(s, "V", vv)
                        b.inc(s.activation(xbc_s[gi][:, 0:3], cacc[:, 0:3], AF.Silu,
                                           bias=cb_sb[:, gi:gi + 1]), "S")
                    b.S(s)

                def sd(s):
                    b.inc(s.activation(dt_sb[0:64, :], da_sb[0:64, :], AF.Ln, bias=1.0), "S")
                b.S(sd)
                def vd(v, sv=b.n["S"]):
                    b.wait(v, "S", sv)
                    b.wait(v, "G")
                    b.step(v, v.tensor_scalar(da_sb[0:64, :], dt_sb[0:64, :], expa_sb[:, 0:1], -1.0, ALU.mult, ALU.mult), "V")
                    b.step(v, v.tensor_tensor_scan(cs_sb[0:64, :], gate_rst[:, :], da_sb[0:64, :], 0.0, ALU.mult, ALU.add), "V")
                b.V(vd)
                def se(s, vv=b.n["V"]):
                    b.wait(s, "V", vv)
                    b.inc(s.activation(ecs_sb[0:64, :], cs_sb[0:64, :], AF.Exp), "S")
                    for c in range(NC):
                        sl = slice(c * CHUNK, (c + 1) * CHUNK)
                        b.inc(s.activation(dcy_sb[0:64, sl], cs_sb[0:64, sl], AF.Exp,
                                           bias=cs_sb[:, (c + 1) * CHUNK - 1:(c + 1) * CHUNK], scale=-1.0), "S")
                b.S(se)
                dchain = b.n["S"]

                for nm, src in (("dt", dt_sb), ("cs", cs_sb)):
                    if nm in tap_t:
                        def gd(g, nm=nm, src=src):
                            b.wait(g, "V"); b.wait(g, "S")
                            g.dma_start(out=tap_t[nm][:, :], in_=src[0:64, :]).then_inc(b.sem("dT"), 16)
                            b.n["dT"] += 16
                        b.G(gd)
                trf = [b.ps(f"p2_trf{i}", [128, 64], F32, pc) for i in range(2)]
                ev = [0, 0]
                ntr = 0
                for c in range(NC):
                    for lt in range(2):
                        col = (c * 2 + lt) * 64
                        sl = slice(c * CHUNK + lt * 128, c * CHUNK + (lt + 1) * 128)
                        for src, dst in ((cs_sb, cs_tok), (ecs_sb, ecs_tok),
                                         (dt_sb, dt_tok), (dcy_sb, dcy_tok)):
                            pbk = ntr % 2
                            def t(t, src=src, sl=sl, pbk=pbk):
                                b.wait(t, "S", dchain)
                                b.wait(t, "V", ev[pbk])
                                b.inc(t.transpose(trf[pbk][:, 0:64], src[0:64, sl], ident_f[0:64, 0:64]), "T")
                            b.T(t)
                            def v(v, dst=dst, col=col, pbk=pbk, tv=b.n["T"]):
                                b.wait(v, "T", tv)
                                ev[pbk] = b.inc(v.tensor_copy(dst[:, col:col + 64], trf[pbk][:, 0:64]), "V")
                            b.V(v)
                            ntr += 1
                    def vt(v, c=c):
                        b.wait(v, "S", dchain)
                        b.step(v, v.tensor_copy(t4_st[:, c:c + 1], cs_sb[0:64, (c + 1) * CHUNK - 1:(c + 1) * CHUNK]), "V")
                    b.V(vt)
                pbk = ntr % 2
                def tt4(t, pbk=pbk):
                    b.wait(t, "V")
                    b.wait(t, "V", ev[pbk])
                    b.inc(t.transpose(trf[pbk][0:NC, 0:64], t4_st[0:64, 0:NC], ident_f[0:64, 0:64]), "T")
                b.T(tt4)
                t4sb = b.sb("p2_t4sb", [NC, 64], F32, pc)
                def vt4(v, pbk=pbk, tv=b.n["T"]):
                    b.wait(v, "T", tv)
                    ev[pbk] = b.inc(v.tensor_copy(t4sb[:], trf[pbk][0:NC, 0:64]), "V")
                b.V(vt4)
                def gt4(g, vv=b.n["V"]):
                    b.wait(g, "V", vv)
                    for c in range(NC):
                        g.dma_start(out=et_row[0:1, c * 64:(c + 1) * 64], in_=t4sb[c:c + 1, :]).then_inc(b.sem("dS"), 16)
                        b.n["dS"] += 16
                b.G(gt4)
                def st4(s, dv=b.n["dS"]):
                    b.wait(s, "dS", dv)
                    b.inc(s.activation(et_row[0:1, :], et_row[0:1, :], AF.Exp), "S")
                b.S(st4)
                def vcp(v, sv=b.n["S"]):
                    b.wait(v, "S", sv)
                    b.step(v, v.memset(etc_row[0:1, 0:64], 1.0), "V")
                    for c in range(1, NC):
                        b.step(v, v.tensor_tensor(etc_row[0:1, c * 64:(c + 1) * 64],
                                                  etc_row[0:1, (c - 1) * 64:c * 64],
                                                  et_row[0:1, (c - 1) * 64:c * 64], ALU.mult), "V")
                b.V(vcp)
                bcp = b.ps("p2_bcp", [128, NC * 64], F32, pc)
                def tbc(t, vv=b.n["V"]):
                    b.wait(t, "V", vv)
                    b.inc(t.matmul(bcp[:], ones_row_f[:], et_row[0:1, :], start=True, stop=True), "T")
                b.T(tbc)
                def vbc(v, tv=b.n["T"]):
                    b.wait(v, "T", tv)
                    b.step(v, v.tensor_copy(et_bc[:], bcp[:]), "V")
                b.V(vbc)
                def tbc2(t, vv=b.n["V"]):
                    b.wait(t, "V", vv)
                    b.inc(t.matmul(bcp[:], ones_row_f[:], etc_row[0:1, :], start=True, stop=True), "T")
                b.T(tbc2)
                def vbc2(v, tv=b.n["T"]):
                    b.wait(v, "T", tv)
                    b.step(v, v.tensor_copy(etc_bc[:], bcp[:]), "V")
                b.V(vbc2)
                def vfin(v):
                    b.wait(v, "S")
                    b.wait(v, "T")
                    b.step(v, v.engine_nop(), "V")
                b.V(vfin)

        def p3(xbc_s, yfm, S_sb):
            b.barrier()
            with ExitStack() as pc:
                xstk = [b.sb(f"p3_xs{st}", [128, D_INNER], BF16, pc) for st in range(2)]
                Xh = [b.sb(f"p3_Xh{i}", [128, 128], BF16, pc) for i in range(2)]
                Btk = [b.sb(f"p3_B{st}", [128, 128], BF16, pc) for st in range(2)]
                Bd = [b.sb(f"p3_Bd{i}", [128, 256], BF16, pc) for i in range(2)]
                GmA = b.sb("p3_GmA", [128, 256], BF16, pc)
                GmB = b.sb("p3_GmB", [128, 128], BF16, pc)
                rE = [b.sb(f"p3_rE{i}", [128, 384], F32, pc) for i in range(2)]
                Mm = [b.sb(f"p3_M{i}", [128, 384], BF16, pc) for i in range(2)]
                Sbf = [b.sb(f"p3_Sb{i}", [128, 64], BF16, pc) for i in range(2)]
                stg = [b.sb(f"p3_stg{i}", [128, 256], BF16, pc) for i in range(2)]
                oh2 = [b.sb(f"p3_oh{i}", [64, 8 * 128], F32, pc) for i in range(2)]
                Eps = [b.ps(f"p3_E{i}", [128, 384], F32, pc) for i in range(2)]
                Yps = [b.ps(f"p3_Y{i}", [128, 256], F32, pc) for i in range(2)]
                Ops = [b.ps(f"p3_O{i}", [128, 256], F32, pc) for i in range(2)]
                Sps = b.ps("p3_S", [128, 128], F32, pc)
                Trp = b.ps("p3_Tr", [128, 512], BF16, pc)

                def gz(g):
                    b.inc(g.memset(S_sb[:], 0.0), "G")
                b.G(gz)
                SZERO = b.n["G"]

                fr = {
                    "eps": [0, 0],
                    "eexp": [0, 0],
                    "mbs": [0, 0],
                    "ypo": [0, 0],
                    "sps": 0,
                    "stg": [0, 0],
                    "trp": 0,
                    "oh": [0, 0],
                    "ohgen": [0, 0],
                    "epsV": [0, 0],
                }

                def oh_build(grp):
                    gb = grp % 2
                    def g(g, grp=grp, gb=gb):
                        b.wait(g, "T", fr["oh"][gb])
                        b.inc(g.memset(oh2[gb][:], 1.0), "G")
                        b.wait(g, "G")
                        oh3 = oh2[gb].ap().rearrange("k (hh j) -> k hh j", j=128)
                        b.inc(g.affine_select(out=oh3, in_=oh3, compare_op=ALU.is_equal,
                                              fill=0.0, base=-grp * 8, pattern=[[-1, 8], [0, 128]],
                                              channel_multiplier=1), "G")
                    b.G(g)
                    fr["ohgen"][gb] = b.n["G"]

                def chunk_start(c):
                    for stt in range(2):
                        sl = slice(c * CHUNK + stt * 128, c * CHUNK + (stt + 1) * 128)
                        for grp in range(8):
                            def t(t, stt=stt, grp=grp, sl=sl):
                                b.wait(t, "V", fr["trp"])
                                for j in range(4):
                                    last = t.transpose(Trp[:, j * 128:(j + 1) * 128], xbc_s[grp * 4 + j][:, sl], ident_b[:])
                                b.inc(last, "T")
                            b.T(t)
                            def v(v, stt=stt, grp=grp, tv=b.n["T"]):
                                b.wait(v, "T", tv)
                                fr["trp"] = b.inc(v.tensor_copy(xstk[stt][:, grp * 512:(grp + 1) * 512], Trp[:]), "V")
                            b.V(v)
                        def t2(t, stt=stt, sl=sl):
                            b.wait(t, "V", fr["trp"])
                            b.inc(t.transpose(Trp[:, 0:128], xbc_s[32][:, sl], ident_b[:]), "T")
                        b.T(t2)
                        def v2(v, stt=stt, tv=b.n["T"]):
                            b.wait(v, "T", tv)
                            fr["trp"] = b.inc(v.tensor_copy(Btk[stt][:], Trp[:, 0:128]), "V")
                        b.V(v2)
                    def tg(t, c=c):
                        b.wait(t, "V")
                        last = t.matmul(Eps[0][:, 0:256], xbc_s[32][:, c * CHUNK:c * CHUNK + 128],
                                        xbc_s[33][:, c * CHUNK:(c + 1) * CHUNK], start=True, stop=True)
                        b.inc(last, "T")
                        last = t.matmul(Eps[1][:, 0:128], xbc_s[32][:, c * CHUNK + 128:(c + 1) * CHUNK],
                                        xbc_s[33][:, c * CHUNK + 128:(c + 1) * CHUNK], start=True, stop=True)
                        b.inc(last, "T")
                    b.T(tg)
                    def vg(v, tv=b.n["T"]):
                        b.wait(v, "T", tv)
                        b.step(v, v.tensor_tensor(GmA[:], Eps[0][:, 0:256], mask01[:], ALU.mult), "V")
                        b.step(v, v.tensor_tensor(GmB[:], Eps[1][:, 0:128], mask01[:, 0:128], ALU.mult), "V")
                        fr["epsV"][0] = fr["epsV"][1] = b.n["V"]
                    b.V(vg)

                def run_pipeline(c, corr):
                    A_t = [0] * NHEADS
                    BC_s = [0] * NHEADS
                    BCv = [0] * NHEADS
                    D_v = [0] * NHEADS
                    F_t = [0] * NHEADS
                    G_v = [0] * 32
                    H_t = [0] * 32
                    ccol = c * CHUNK

                    def stA(h):
                        grp = h // 8
                        gb = grp % 2
                        ep = h % 2
                        def t(t, h=h, gb=gb, ep=ep):
                            b.wait(t, "G", fr["ohgen"][gb])
                            b.wait(t, "V", fr["eps"][ep])
                            b.wait(t, "V", fr["epsV"][ep])
                            last = t.matmul(Eps[ep][:, 0:256], oh2[gb][:, (h % 8) * 128:(h % 8 + 1) * 128],
                                            cs_sb[0:64, ccol:ccol + CHUNK], start=True, stop=True)
                            A_t[h] = b.inc(last, "T")
                            fr["oh"][gb] = A_t[h]
                        b.T(t)

                    def stBC(h):
                        ep = h % 2
                        ct0 = (c * 2 + 0) * 64 + h
                        ct1 = (c * 2 + 1) * 64 + h
                        def v(v, h=h, ep=ep, ct0=ct0, ct1=ct1):
                            b.wait(v, "T", A_t[h])
                            b.wait(v, "S", fr["eexp"][ep])
                            b.inc(v.tensor_scalar(rE[ep][:, 0:256], Eps[ep][:, 0:256],
                                                  cs_tok[:, ct0:ct0 + 1], 0.0, ALU.subtract, ALU.min), "V")
                            BCv[h] = b.inc(v.tensor_scalar(rE[ep][:, 256:384], Eps[ep][:, 128:256],
                                                           cs_tok[:, ct1:ct1 + 1], 0.0, ALU.subtract, ALU.min), "V")
                            fr["eps"][ep] = BCv[h]
                        b.V(v)
                        def s(s, h=h, ep=ep):
                            b.wait(s, "V", BCv[h])
                            BC_s[h] = b.inc(s.activation(rE[ep][:, 0:384], rE[ep][:, 0:384], AF.Exp), "S")
                        b.S(s)


                    def stD(h):
                        ep = h % 2
                        hs = slice(h * 64, (h + 1) * 64)
                        ct0 = (c * 2 + 0) * 64 + h
                        ct1 = (c * 2 + 1) * 64 + h
                        def v(v, h=h, ep=ep, hs=hs, ct0=ct0, ct1=ct1):
                            b.wait(v, "S", BC_s[h])
                            b.wait(v, "T", fr["mbs"][ep])
                            b.inc(v.tensor_tensor(Mm[ep][:, 0:256], GmA[:], rE[ep][:, 0:256], ALU.mult), "V")
                            b.inc(v.tensor_tensor(Mm[ep][:, 256:384], GmB[:], rE[ep][:, 256:384], ALU.mult), "V")
                            fr["eexp"][ep] = BC_s[h]

                            b.inc(v.tensor_scalar(Bd[ep][:, 0:128], Btk[0][:], dcy_tok[:, ct0:ct0 + 1], None, ALU.mult), "V")
                            b.inc(v.tensor_scalar(Bd[ep][:, 128:256], Btk[1][:], dcy_tok[:, ct1:ct1 + 1], None, ALU.mult), "V")
                            b.inc(v.tensor_scalar(Xh[ep][:, 0:64], xstk[0][:, hs], dt_tok[:, ct0:ct0 + 1], None, ALU.mult), "V")
                            b.inc(v.tensor_scalar(Xh[ep][:, 64:128], xstk[1][:, hs], dt_tok[:, ct1:ct1 + 1], None, ALU.mult), "V")
                            if c > 0:
                                b.inc(v.tensor_copy(Sbf[ep][:], S_sb[:, hs]), "V")
                            D_v[h] = b.n["V"]
                        b.V(v)

                    def stDc(h):
                        ep = h % 2
                        hs = slice(h * 64, (h + 1) * 64)
                        def v(v, h=h, ep=ep, hs=hs):
                            b.wait(v, "T", fr["mbs"][ep])
                            D_v[h] = b.inc(v.tensor_scalar(Sbf[ep][:], S_sb[:, hs],
                                                           etc_bc[:, c * 64 + h:c * 64 + h + 1], None, ALU.mult), "V")
                        b.V(v)

                    def stF(h):
                        ep = h % 2
                        pair, par = h // 2, h % 2
                        pp = pair % 2
                        hs = slice(h * 64, (h + 1) * 64)
                        yc = par * 64
                        def t(t, h=h, ep=ep, pair=pair, par=par, pp=pp, hs=hs, yc=yc):
                            b.wait(t, "V", D_v[h])
                            if par == 0:
                                b.wait(t, "V", fr["ypo"][pp])
                                b.wait(t, "V", fr["sps"])
                            if not corr:
                                t.matmul(Yps[pp][:, yc:yc + 64], Mm[ep][:, 0:128], Xh[ep][:, 0:64], start=True, stop=True)
                                t.matmul(Yps[pp][:, 128 + yc:192 + yc], Mm[ep][:, 128:256], Xh[ep][:, 0:64], start=True, stop=False)
                                last = t.matmul(Yps[pp][:, 128 + yc:192 + yc], Mm[ep][:, 256:384], Xh[ep][:, 64:128], start=False, stop=True)
                                t.matmul(Sps[:, yc:yc + 64], Bd[ep][:, 0:128], Xh[ep][:, 0:64], start=True, stop=False)
                                last = t.matmul(Sps[:, yc:yc + 64], Bd[ep][:, 128:256], Xh[ep][:, 64:128], start=False, stop=True)
                            if corr or (c > 0 and not DBG_NO_YOFF):
                                t.matmul(Ops[pp][:, yc:yc + 64], xbc_s[33][:, ccol:ccol + 128], Sbf[ep][:], start=True, stop=True)
                                last = t.matmul(Ops[pp][:, 128 + yc:192 + yc], xbc_s[33][:, ccol + 128:ccol + 256], Sbf[ep][:], start=True, stop=True)
                            F_t[h] = b.inc(last, "T")
                            fr["mbs"][ep] = F_t[h]
                        b.T(t)

                    def stG(p):
                        pp = p % 2
                        def v(v, p=p, pp=pp):
                            b.wait(v, "T", F_t[2 * p + 1])
                            b.wait(v, "T", fr["stg"][pp])
                            b.wait(v, "G", SZERO)
                            for par in range(2):
                                h = 2 * p + par
                                hs = slice(h * 64, (h + 1) * 64)
                                yc = par * 64
                                ct0 = (c * 2 + 0) * 64 + h
                                ct1 = (c * 2 + 1) * 64 + h
                                if not corr:
                                    for lt, ct in ((0, ct0), (1, ct1)):
                                        dst = stg[pp][:, lt * 128 + yc:lt * 128 + yc + 64]
                                        b.inc(v.scalar_tensor_tensor(dst, xstk[lt][:, hs], dbc_sb[:, h:h + 1],
                                                                     Yps[pp][:, lt * 128 + yc:lt * 128 + yc + 64],
                                                                     ALU.mult, ALU.add), "V")
                                    if c > 0 and not DBG_NO_YOFF:
                                        v.wait_ge(b.sem("V"), b.n["V"])
                                        for lt, ct in ((0, ct0), (1, ct1)):
                                            dst = stg[pp][:, lt * 128 + yc:lt * 128 + yc + 64]
                                            b.inc(v.scalar_tensor_tensor(dst, Ops[pp][:, lt * 128 + yc:lt * 128 + yc + 64],
                                                                         ecs_tok[:, ct:ct + 1], dst, ALU.mult, ALU.add), "V")
                                    b.inc(v.scalar_tensor_tensor(S_sb[:, hs], S_sb[:, hs],
                                                                 et_bc[:, c * 64 + h:c * 64 + h + 1],
                                                                 Sps[:, yc:yc + 64], ALU.mult, ALU.add), "V")
                                else:
                                    for lt, ct in ((0, ct0), (1, ct1)):
                                        b.inc(v.tensor_scalar(stg[pp][:, lt * 128 + yc:lt * 128 + yc + 64],
                                                              Ops[pp][:, lt * 128 + yc:lt * 128 + yc + 64],
                                                              ecs_tok[:, ct:ct + 1], None, ALU.mult), "V")
                            G_v[p] = b.n["V"]
                            fr["ypo"][pp] = G_v[p]
                            fr["sps"] = G_v[p]
                        b.V(v)

                    def stH(p):
                        pp = p % 2
                        def t(t, p=p, pp=pp):
                            b.wait(t, "V", G_v[p])
                            b.wait(t, "V", fr["trp"])
                            for lt in range(2):
                                last = t.transpose(Trp[:, lt * 128:(lt + 1) * 128], stg[pp][:, lt * 128:(lt + 1) * 128], ident_b[:])
                            H_t[p] = b.inc(last, "T")
                            fr["stg"][pp] = H_t[p]
                        b.T(t)

                    def stI(p):
                        pair = p
                        def v(v, p=p, pair=pair):
                            b.wait(v, "T", H_t[p])
                            for lt in range(2):
                                sl = slice(ccol + lt * 128, ccol + (lt + 1) * 128)
                                if not corr:
                                    b.inc(v.tensor_copy(yfm[pair][:, sl], Trp[:, lt * 128:(lt + 1) * 128]), "V")
                                else:
                                    b.inc(v.tensor_tensor(yfm[pair][:, sl], yfm[pair][:, sl],
                                                          Trp[:, lt * 128:(lt + 1) * 128], ALU.add), "V")
                            fr["trp"] = b.n["V"]
                        b.V(v)

                    for slot in range(NHEADS + 5):
                        hF = slot - (2 if not corr else 1)
                        if not corr:
                            if slot < NHEADS:
                                stA(slot)
                                if slot % 8 == 0 and slot // 8 + 1 < 8:
                                    oh_build(slot // 8 + 1)
                            if 0 <= slot - 1 < NHEADS:
                                stBC(slot - 1)
                            if 0 <= slot - 2 < NHEADS:
                                stD(slot - 2)
                        else:
                            if slot < NHEADS:
                                stDc(slot)
                        if 2 <= hF <= NHEADS and hF % 2 == 0:
                            stG(hF // 2 - 1)
                        if 0 <= hF < NHEADS:
                            stF(hF)
                        hH = hF - 1
                        if 2 <= hH and hH % 2 == 0 and hH <= NHEADS:
                            stH(hH // 2 - 1)
                        hI = hF - 2
                        if 2 <= hI and hI % 2 == 0 and hI <= NHEADS:
                            stI(hI // 2 - 1)

                for c in range(NC):
                    chunk_start(c)
                    oh_build(0)
                    run_pipeline(c, corr=False)

                if "ssb" in tap_t:
                    def gss(g):
                        b.wait(g, "V"); b.wait(g, "T")
                        g.dma_start(out=tap_t["ssb"][:, :], in_=S_sb[:]).then_inc(b.sem("dT"), 16)
                        b.n["dT"] += 16
                    b.G(gss)
                    def vss(v, dv=b.n["dT"]):
                        b.wait(v, "dT", dv)
                        b.step(v, v.engine_nop(), "V")
                    b.V(vss)
                def gx(g):
                    b.wait(g, "V")
                    g.dma_start(out=ag_in[:], in_=S_sb[:]).then_inc(b.sem("dS"), 16)
                    b.n["dS"] += 16
                    b.wait(g, "dS")
                    g.collective_compute(
                        "AllGather", ALU.bypass,
                        replica_groups=[[2 * i, 2 * i + 1] for i in range(max(1, n_cores // 2))],
                        ins=[ag_in.opt()], outs=[ag_out.opt()],
                    ).then_inc(b.sem("CC"), 1)
                    b.n["CC"] += 1
                    b.wait(g, "CC")
                    g.dma_start(out=S_sb[:], in_=ag_out[0:128, :]).then_inc(b.sem("dS"), 16)
                    b.n["dS"] += 16
                b.G(gx)
                def vgate(v, dv=b.n["dS"]):
                    b.wait(v, "dS", dv)
                    b.step(v, v.tensor_scalar(S_sb[:], S_sb[:], ggate_sb[:, 0:1], None, ALU.mult), "V")
                b.V(vgate)

                for c in range(NC):
                    run_pipeline(c, corr=True)

                def vfin(v):
                    b.wait(v, "T"); b.wait(v, "S")
                    b.step(v, v.engine_nop(), "V")
                b.V(vfin)


        def p4a(xln_fm, yfm, ofm):
            b.barrier()
            with ExitStack() as pc:
                wob = [b.sb(f"p4_wo{i}", [128, KI * 128], BF16, pc, side="right") for i in range(2)]
                mp = [b.ps(f"p4_mp{i}", [128, 512], F32, pc) for i in range(4)]
                vps = [b.ps(f"p4_vp{si}", [1, 512], F32, pc) for si in range(NS)]
                zs = b.sb("p4_zs", [128, 512], F32, pc, side="right")
                sqb = b.sb("p4_sqb", [128, 512], BF16, pc, side="right")
                rrow = b.sb("p4_rrow", [1, T], F32, pc, side="right")
                rbc = b.sb("p4_rbc", [128, T], F32, pc, side="right")
                wdma = [0, 0]; wfree = [0, 0]
                mp_mm = [0] * 4; mp_ev = [0] * 4
                vps_mm = 0

                def gp(g):
                    g.dma_start(out=wob[0][:, 0:KD * 128], in_=wip[0].rearrange("p k m -> p (k m)")).then_inc(b.sem("p4z0"), 16)
                    b.n["p4z0"] = b.n.get("p4z0", 0) + 16
                b.G(gp)
                wdma[0] = b.n["p4z0"]

                for zt in range(32):
                    par = zt % 2
                    bk0 = 2 * par
                    if zt + 1 < 32:
                        def g(g, zt=zt):
                            npar = (zt + 1) % 2
                            if zt >= 1:
                                b.wait(g, "T", wfree[npar])
                            g.dma_start(out=wob[npar][:, 0:KD * 128], in_=wip[zt + 1].rearrange("p k m -> p (k m)")).then_inc(b.sem(f"p4z{npar}"), 16)
                            b.n[f"p4z{npar}"] = b.n.get(f"p4z{npar}", 0) + 16
                            wdma[npar] = b.n[f"p4z{npar}"]
                        b.G(g)
                    def t(t, zt=zt, par=par, bk0=bk0):
                        b.wait(t, f"p4z{par}", wdma[par])
                        for si in range(NS):
                            b.wait(t, "V", mp_ev[bk0 + si])
                        for si, (st, wdt) in enumerate(TS_):
                            for k in range(KD):
                                last = t.matmul(mp[bk0 + si][:, 0:wdt], wob[par][:, k * 128:(k + 1) * 128],
                                                xln_fm[k][:, 3 + st:3 + st + wdt],
                                                start=(k == 0), stop=(k == KD - 1))
                            mp_mm[bk0 + si] = b.inc(last, "T")
                        wfree[par] = b.n["T"]
                    b.T(t)
                    for si, (st, wdt) in enumerate(TS_):
                        def s(s, bk0=bk0, si=si, wdt=wdt):
                            b.wait(s, "T", mp_mm[bk0 + si])
                            b.inc(s.activation(zs[:, 0:wdt], mp[bk0 + si][:, 0:wdt], AF.Silu), "S")
                        b.S(s)
                        def v(v, zt=zt, bk0=bk0, si=si, st=st, wdt=wdt, sv=b.n["S"]):
                            b.wait(v, "S", sv)
                            mp_ev[bk0 + si] = b.step(v, v.tensor_tensor(yfm[zt][:, st:st + wdt], yfm[zt][:, st:st + wdt],
                                                                        zs[:, 0:wdt], ALU.mult), "V")
                        b.V(v)
                        def s2(v, zt=zt, st=st, wdt=wdt):
                            b.inc(v.tensor_tensor(sqb[:, 0:wdt], yfm[zt][:, st:st + wdt],
                                                  yfm[zt][:, st:st + wdt], ALU.mult), "V")
                        b.V(s2)
                        def t2(t, zt=zt, si=si, wdt=wdt, sv=b.n["V"]):
                            nonlocal vps_mm
                            b.wait(t, "V", sv)
                            last = t.matmul(vps[si][0:1, 0:wdt], ones_col_b[:], sqb[:, 0:wdt],
                                            start=(zt == 0), stop=(zt == 31))
                            vps_mm = b.inc(last, "T")
                        b.T(t2)

                def vr(v):
                    b.wait(v, "T", vps_mm)
                    for si, (st, wdt) in enumerate(TS_):
                        b.step(v, v.tensor_scalar(rrow[0:1, st:st + wdt], vps[si][0:1, 0:wdt],
                                                  1.0 / D_INNER, 1e-5, ALU.mult, ALU.add), "V")
                    b.inc(v.reciprocal(rrow[0:1, :], rrow[0:1, :]), "V")
                b.V(vr)
                def sr(s, vv=b.n["V"]):
                    b.wait(s, "V", vv)
                    b.inc(s.activation(rrow[0:1, :], rrow[0:1, :], AF.Sqrt), "S")
                b.S(sr)
                def tr(t, sv=b.n["S"]):
                    b.wait(t, "S", sv)
                    for si, (st, wdt) in enumerate(TS_):
                        b.wait(t, "V", mp_ev[si])
                        last = t.matmul(mp[si][:, 0:wdt], ones_row_f[:], rrow[0:1, st:st + wdt], start=True, stop=True)
                        mp_mm[si] = b.inc(last, "T")
                b.T(tr)
                for si, (st, wdt) in enumerate(TS_):
                    def v(v, si=si, st=st, wdt=wdt):
                        b.wait(v, "T", mp_mm[si])
                        mp_ev[si] = b.step(v, v.tensor_copy(rbc[:, st:st + wdt], mp[si][:, 0:wdt]), "V")
                    b.V(v)
                for p in range(32):
                    def v(v, p=p):
                        b.inc(v.scalar_tensor_tensor(yfm[p][:], yfm[p][:], nw_sb[:, p:p + 1], rbc[:], ALU.mult, ALU.mult), "V")
                    b.V(v)
                ynv = b.n["V"]
                tap_fm("yfm", yfm, T)

                def gp2(g):
                    b.wait(g, "T", wfree[0]); b.wait(g, "T", wfree[1])
                    g.dma_start(out=wob[0][:], in_=wout[0].rearrange("p k m -> p (k m)")).then_inc(b.sem("p4o0"), 16)
                    b.n["p4o0"] = b.n.get("p4o0", 0) + 16
                    wdma[0] = b.n["p4o0"]
                b.G(gp2)
                for dp in range(16):
                    par = dp % 2
                    bk0 = 2 * par
                    if dp + 1 < 16:
                        def g(g, dp=dp):
                            npar = (dp + 1) % 2
                            if dp >= 1:
                                b.wait(g, "T", wfree[npar])
                            g.dma_start(out=wob[npar][:], in_=wout[dp + 1].rearrange("p k m -> p (k m)")).then_inc(b.sem(f"p4o{npar}"), 16)
                            b.n[f"p4o{npar}"] = b.n.get(f"p4o{npar}", 0) + 16
                            wdma[npar] = b.n[f"p4o{npar}"]
                        b.G(g)
                    def t(t, dp=dp, par=par, bk0=bk0):
                        b.wait(t, f"p4o{par}", wdma[par])
                        b.wait(t, "V", ynv)
                        for si in range(NS):
                            b.wait(t, "V", mp_ev[bk0 + si])
                        for si, (st, wdt) in enumerate(TS_):
                            for k in range(KI):
                                last = t.matmul(mp[bk0 + si][:, 0:wdt], wob[par][:, k * 128:(k + 1) * 128],
                                                yfm[k][:, st:st + wdt], start=(k == 0), stop=(k == KI - 1))
                            mp_mm[bk0 + si] = b.inc(last, "T")
                        wfree[par] = b.n["T"]
                    b.T(t)
                    for si, (st, wdt) in enumerate(TS_):
                        def v(v, dp=dp, bk0=bk0, si=si, st=st, wdt=wdt):
                            b.wait(v, "T", mp_mm[bk0 + si])
                            mp_ev[bk0 + si] = b.step(v, v.tensor_copy(ofm[dp][:, st:st + wdt], mp[bk0 + si][:, 0:wdt]), "V")
                        b.V(v)
                def vfin(v):
                    b.wait(v, "T"); b.wait(v, "S")
                    b.step(v, v.engine_nop(), "V")
                b.V(vfin)

        def p4b(ofm, q_fm):
            b.barrier()
            with ExitStack() as pc:
                xm = [b.sb(f"p4b_xm{i}", [128, D_MODEL], F32, pc, side="right") for i in range(2)]
                x1t = b.sb("p4b_x1t", [128, D_MODEL], F32, pc, side="right")
                qtk = b.sb("p4b_qtk", [128, D_MODEL], BF16, pc, side="right")
                st2 = b.sb("p4b_st2", [128, 8], F32, pc, side="right")
                sq2 = b.sb("p4b_sq2", [128, D_MODEL], F32, pc, side="right")
                sq3 = b.sb("p4b_sq3", [128, D_MODEL], F32, pc, side="right")
                trp2 = [b.ps(f"p4b_tp{i}", [128, 128], F32, pc) for i in range(2)]
                trq = [b.ps(f"p4b_tq{i}", [128, 128], BF16, pc) for i in range(2)]
                tr_ev2 = [0, 0]
                trq_ev = [0, 0]
                xm_free = [0] * NTM
                parkd = 0
                for tt in range(NTM):
                    buf = tt % 2
                    def g(g, tt=tt, buf=buf):
                        if tt >= 2:
                            b.wait(g, "V", xm_free[tt - 2])
                        g.dma_start(out=xm[buf][:], in_=x_core[3 + tt * 128:3 + (tt + 1) * 128, :]).then_inc(b.sem(f"p4x{buf}"), 16)
                        b.n[f"p4x{buf}"] = b.n.get(f"p4x{buf}", 0) + 16
                    b.G(g)
                    x0d = b.n[f"p4x{buf}"]
                    for dp in range(16):
                        pbk = dp % 2
                        def t(t, tt=tt, dp=dp, pbk=pbk):
                            b.wait(t, "V", tr_ev2[pbk])
                            b.inc(t.transpose(trp2[pbk][:], ofm[dp][:, tt * 128:(tt + 1) * 128], ident_f[:]), "T")
                        b.T(t)
                        def v(v, tt=tt, dp=dp, pbk=pbk, buf=buf, tv=b.n["T"], x0d=x0d, parkd=parkd):
                            b.wait(v, "T", tv)
                            b.wait(v, f"p4x{buf}", x0d)
                            b.wait(v, "dP", parkd)
                            tr_ev2[pbk] = b.inc(v.tensor_tensor(x1t[:, dp * 128:(dp + 1) * 128], trp2[pbk][:],
                                                                xm[buf][:, dp * 128:(dp + 1) * 128], ALU.add), "V")
                        b.V(v)
                    xm_free[tt] = b.n["V"]
                    x1v = b.n["V"]
                    if "x1" in tap_t:
                        def gx(g, tt=tt, x1v=x1v):
                            b.wait(g, "V", x1v)
                            g.dma_start(out=tap_t["x1"][tt * 128:(tt + 1) * 128, :], in_=x1t[:]).then_inc(b.sem("dP"), 16)
                            b.n["dP"] += 16
                        b.G(gx)
                    def gp(g, tt=tt, x1v=x1v):
                        b.wait(g, "V", x1v)
                        g.dma_start(out=x1_scr[tt * 128:(tt + 1) * 128, :], in_=x1t[:]).then_inc(b.sem("dP"), 16)
                        b.n["dP"] += 16
                    b.G(gp)
                    parkd = b.n["dP"]
                    def v2(v):
                        b.step(v, v.reduce_sum(st2[:, 0:1], x1t[:], axis=AX.X), "V")
                        b.step(v, v.tensor_scalar(st2[:, 1:2], st2[:, 0:1], 1.0 / D_MODEL, None, ALU.mult), "V")
                        b.step(v, v.tensor_scalar(sq2[:], x1t[:], st2[:, 1:2], None, ALU.subtract), "V")
                    b.V(v2)
                    def v3(v):
                        b.step(v, v.scalar_tensor_tensor(sq3[:], sq2[:], 1.0, sq2[:], ALU.mult, ALU.mult,
                                                         accum_out=st2[:, 2:3]), "V")
                        b.step(v, v.tensor_scalar(st2[:, 3:4], st2[:, 2:3], 1.0 / D_MODEL, 1e-5, ALU.mult, ALU.add), "V")
                        b.inc(v.reciprocal(st2[:, 4:5], st2[:, 3:4]), "V")
                    b.V(v3)
                    def s3(s, vv=b.n["V"]):
                        b.wait(s, "V", vv)
                        b.inc(s.activation(st2[:, 5:6], st2[:, 4:5], AF.Sqrt), "S")
                    b.S(s3)
                    def v4(v, sv=b.n["S"]):
                        b.wait(v, "S", sv)
                        b.step(v, v.tensor_scalar(sq2[:], sq2[:], st2[:, 5:6], None, ALU.mult), "V")
                        b.step(v, v.reduce_sum(st2[:, 0:1], sq2[:], axis=AX.X), "V")
                        b.step(v, v.tensor_scalar(st2[:, 1:2], st2[:, 0:1], 1.0 / D_MODEL, None, ALU.mult), "V")
                        b.step(v, v.tensor_scalar(sq2[:], sq2[:], st2[:, 1:2], None, ALU.subtract), "V")
                    b.V(v4)
                    def v5(v):
                        b.step(v, v.scalar_tensor_tensor(sq3[:], sq2[:], 1.0, sq2[:], ALU.mult, ALU.mult,
                                                         accum_out=st2[:, 2:3]), "V")
                        b.step(v, v.tensor_scalar(st2[:, 3:4], st2[:, 2:3], 1.0 / D_MODEL, 1e-8, ALU.mult, ALU.add), "V")
                        b.inc(v.reciprocal(st2[:, 4:5], st2[:, 3:4]), "V")
                    b.V(v5)
                    def s5(s, vv=b.n["V"]):
                        b.wait(s, "V", vv)
                        b.inc(s.activation(st2[:, 5:6], st2[:, 4:5], AF.Sqrt), "S")
                    b.S(s5)
                    def v6(v, sv=b.n["S"]):
                        b.wait(v, "S", sv)
                        b.step(v, v.tensor_scalar(sq2[:], sq2[:], st2[:, 5:6], None, ALU.mult), "V")
                        b.step(v, v.reduce_max(st2[:, 6:7], sq2[:], axis=AX.X, apply_absolute_value=True), "V")
                        b.step(v, v.reciprocal(st2[:, 7:8], st2[:, 6:7]), "V")
                        b.step(v, v.tensor_scalar(sq2[:], sq2[:], st2[:, 7:8], 127.0, ALU.mult, ALU.mult), "V")
                        b.step(v, v.tensor_scalar(sq2[:], sq2[:], MAGIC, None, ALU.add), "V")
                        b.step(v, v.tensor_scalar(sq2[:], sq2[:], MAGIC, None, ALU.subtract), "V")
                        b.step(v, v.tensor_scalar(qtk[:], sq2[:], st2[:, 6:7], 1.0 / 127.0, ALU.mult, ALU.mult), "V")
                    b.V(v6)
                    qv = b.n["V"]
                    for k in range(KD):
                        pbk = k % 2
                        def t2(t, tt=tt, k=k, pbk=pbk, qv=qv):
                            b.wait(t, "V", qv)
                            b.wait(t, "V", trq_ev[pbk])
                            b.inc(t.transpose(trq[pbk][:], qtk[:, k * 128:(k + 1) * 128], ident_b[:]), "T")
                        b.T(t2)
                        def s6(v, tt=tt, k=k, pbk=pbk, tv=b.n["T"]):
                            b.wait(v, "T", tv)
                            trq_ev[pbk] = b.inc(v.tensor_copy(q_fm[k][:, tt * 128:(tt + 1) * 128], trq[pbk][:]), "V")
                        b.V(s6)
                    def v7(v, tv=b.n["T"]):
                        b.wait(v, "T", tv)
                        b.step(v, v.engine_nop(), "V")
                    b.V(v7)
                def vfin(v):
                    b.wait(v, "S"); b.wait(v, "T")
                    b.step(v, v.engine_nop(), "V")
                b.V(vfin)
            tap_fm("q", q_fm, T)

        def p5a(q_fm, pfm):
            b.barrier()
            with ExitStack() as pc:
                gwb = [b.sb(f"p5_gw{i}", [128, KD * 128], BF16, pc) for i in range(2)]
                uwb = [b.sb(f"p5_uw{i}", [128, KD * 128], BF16, pc) for i in range(2)]
                gsb = b.sb("p5_gs", [128, 512], F32, pc)
                psg = [b.ps(f"p5_g{i}", [128, 512], F32, pc) for i in range(2)]
                psu = [b.ps(f"p5_u{i}", [128, 512], F32, pc) for i in range(2)]
                wdma = [0, 0]; wfree = [0, 0]
                g_mm = [0, 0]; u_mm = [0, 0]; gu_ev = [0, 0]

                def gp(g):
                    g.dma_start(out=gwb[0][:], in_=wg[0].rearrange("p k m -> p (k m)")).then_inc(b.sem("p5g0"), 16)
                    g.dma_start(out=uwb[0][:], in_=wu[0].rearrange("p k m -> p (k m)")).then_inc(b.sem("p5g0"), 16)
                    b.n["p5g0"] = b.n.get("p5g0", 0) + 32
                b.G(gp)
                wdma[0] = b.n["p5g0"]

                for j in range(KH):
                    par = j % 2
                    if j + 1 < KH:
                        def g(g, j=j):
                            npar = (j + 1) % 2
                            if j >= 1:
                                b.wait(g, "T", wfree[npar])
                            g.dma_start(out=gwb[npar][:], in_=wg[j + 1].rearrange("p k m -> p (k m)")).then_inc(b.sem(f"p5g{npar}"), 16)
                            g.dma_start(out=uwb[npar][:], in_=wu[j + 1].rearrange("p k m -> p (k m)")).then_inc(b.sem(f"p5g{npar}"), 16)
                            b.n[f"p5g{npar}"] = b.n.get(f"p5g{npar}", 0) + 32
                            wdma[npar] = b.n[f"p5g{npar}"]
                        b.G(g)
                    def s(v, j=j, par=par):
                        b.wait(v, f"p5g{par}", wdma[par])
                        if j >= 2:
                            b.wait(v, "T", wfree[par])
                        for wb in (gwb[par], uwb[par]):
                            b.step(v, v.tensor_scalar(wb[:], wb[:], 1e30, 1.0, ALU.mult, ALU.min), "V")
                            b.step(v, v.tensor_scalar(wb[:], wb[:], -1.0, None, ALU.max), "V")
                    b.V(s)
                    def t(t, j=j, par=par, sv=b.n["V"]):
                        b.wait(t, "V", sv)
                        for si in range(NS):
                            b.wait(t, "V", gu_ev[si % 2])
                        for si, (st, wdt) in enumerate(TS_):
                            for k in range(KD):
                                last = t.matmul(psg[si % 2][:, 0:wdt], gwb[par][:, k * 128:(k + 1) * 128],
                                                q_fm[k][:, st:st + wdt], start=(k == 0), stop=(k == KD - 1))
                            g_mm[si % 2] = b.inc(last, "T")
                            for k in range(KD):
                                last = t.matmul(psu[si % 2][:, 0:wdt], uwb[par][:, k * 128:(k + 1) * 128],
                                                q_fm[k][:, st:st + wdt], start=(k == 0), stop=(k == KD - 1))
                            u_mm[si % 2] = b.inc(last, "T")
                        wfree[par] = b.n["T"]
                    b.T(t)
                    for si, (st, wdt) in enumerate(TS_):
                        def s2(s, si=si, wdt=wdt):
                            b.wait(s, "T", g_mm[si % 2])
                            b.inc(s.activation(gsb[:, 0:wdt], psg[si % 2][:, 0:wdt], AF.Silu), "S")
                        b.S(s2)
                        def v(v, j=j, si=si, st=st, wdt=wdt, sv=b.n["S"]):
                            b.wait(v, "S", sv)
                            b.wait(v, "T", u_mm[si % 2])
                            gu_ev[si % 2] = b.step(v, v.tensor_tensor(pfm[j][:, st:st + wdt], gsb[:, 0:wdt],
                                                                      psu[si % 2][:, 0:wdt], ALU.mult), "V")
                        b.V(v)
                def vfin(v):
                    b.wait(v, "T"); b.wait(v, "S")
                    b.step(v, v.engine_nop(), "V")
                b.V(vfin)
            tap_fm("pfm", pfm, T)

        def p5b(pfm):
            b.barrier()
            with ExitStack() as pc:
                dwb = [b.sb(f"p5_dw{i}", [128, KH * 128], BF16, pc, side="right") for i in range(2)]
                dfm = [b.sb(f"p5_df{dp}", [128, T], BF16, pc, side="right") for dp in range(16)]
                ob = [b.sb(f"p5_ob{i}", [128, D_MODEL], F32, pc, side="right") for i in range(1)]
                psd = [b.ps(f"p5_d{i}", [128, 512], F32, pc) for i in range(4)]
                trp3 = [b.ps(f"p5_t{i}", [128, 128], BF16, pc) for i in range(2)]
                ddma = [0, 0]
                wsgn = [0, 0]
                wfree = [0, 0]
                d_mm = [0] * 4
                d_ev = [0] * 4
                tr_ev3 = [0, 0]
                ob_done = [0, 0]

                pv = b.n["V"]
                for dp in range(16):
                    par = dp % 2
                    bk0 = 2 * par
                    def g(g, dp=dp, par=par):
                        if dp >= 2:
                            b.wait(g, "T", wfree[par])
                        g.dma_start(out=dwb[par][:], in_=wd[dp].rearrange("p k m -> p (k m)")).then_inc(b.sem(f"p5d{par}"), 16)
                        b.n[f"p5d{par}"] = b.n.get(f"p5d{par}", 0) + 16
                        ddma[par] = b.n[f"p5d{par}"]
                    b.G(g)
                    def sv(v, dp=dp, par=par):
                        b.wait(v, f"p5d{par}", ddma[par])
                        if dp >= 2:
                            b.wait(v, "T", wfree[par])
                        b.inc(v.tensor_scalar(dwb[par][:], dwb[par][:], 1e30, 1.0, ALU.mult, ALU.min), "V")
                        v.wait_ge(b.sem("V"), b.n["V"])
                        wsgn[par] = b.inc(v.tensor_scalar(dwb[par][:], dwb[par][:], -1.0, None, ALU.max), "V")
                    b.V(sv)
                    def t(t, dp=dp, par=par, bk0=bk0, pv=pv):
                        b.wait(t, "V", wsgn[par])
                        b.wait(t, "V", pv)
                        for si, (st, wdt) in enumerate(TS_):
                            b.wait(t, "V", d_ev[bk0 + si])
                            for jb in range(KH):
                                last = t.matmul(psd[bk0 + si][:, 0:wdt], dwb[par][:, jb * 128:(jb + 1) * 128],
                                                pfm[jb][:, st:st + wdt], start=(jb == 0), stop=(jb == KH - 1))
                            d_mm[bk0 + si] = b.inc(last, "T")
                        wfree[par] = b.n["T"]
                    b.T(t)
                    for si, (st, wdt) in enumerate(TS_):
                        def v(v, dp=dp, bk0=bk0, si=si, st=st, wdt=wdt):
                            b.wait(v, "T", d_mm[bk0 + si])
                            d_ev[bk0 + si] = b.inc(v.tensor_copy(dfm[dp][:, st:st + wdt], psd[bk0 + si][:, 0:wdt]), "V")
                        b.V(v)
                dfmv = b.n["V"]
                for tt in range(NTM):
                    def g2(g, tt=tt, odv=ob_done[0]):
                        b.wait(g, "V")
                        b.wait(g, "dO0", odv)
                        g.dma_start(out=ob[0][:], in_=x1_scr[tt * 128:(tt + 1) * 128, :]).then_inc(b.sem("p5xr"), 16)
                        b.n["p5xr"] = b.n.get("p5xr", 0) + 16
                    b.G(g2)
                    xrd = b.n["p5xr"]
                    for dp in range(16):
                        pbk = dp % 2
                        def t2(t, tt=tt, dp=dp, pbk=pbk, dfmv=dfmv):
                            b.wait(t, "V", dfmv)
                            b.wait(t, "V", tr_ev3[pbk])
                            b.inc(t.transpose(trp3[pbk][:], dfm[dp][:, tt * 128:(tt + 1) * 128], ident_b[:]), "T")
                        b.T(t2)
                        def v2(v, dp=dp, pbk=pbk, tv=b.n["T"], xrd=xrd):
                            b.wait(v, "T", tv)
                            b.wait(v, "p5xr", xrd)
                            tr_ev3[pbk] = b.inc(v.tensor_tensor(ob[0][:, dp * 128:(dp + 1) * 128],
                                                                ob[0][:, dp * 128:(dp + 1) * 128],
                                                                trp3[pbk][:], ALU.add), "V")
                        b.V(v2)
                    obv = b.n["V"]
                    def g3(g, tt=tt, obv=obv):
                        b.wait(g, "V", obv)
                        g.dma_start(out=out[tt * 128:(tt + 1) * 128, :], in_=ob[0][:]).then_inc(b.sem("dO0"), 16)
                        b.n["dO0"] = b.n.get("dO0", 0) + 16
                    b.G(g3)
                    ob_done[0] = b.n["dO0"]
                def vfin(v):
                    b.wait(v, "T")
                    b.step(v, v.engine_nop(), "V")
                b.V(vfin)

        xbc_ctx = ExitStack()
        xbc_s = [b.sb(f"xbc{g}", [128, T], BF16, xbc_ctx, side="right") for g in range(N_XBC)]
        csgrp_ctx = ExitStack()
        cs_sb = b.sb("cs_sb", [64, T], F32, csgrp_ctx, side="right")
        cs_tok = b.sb("cs_tok", [128, NC * 128], F32, csgrp_ctx, side="right")
        ecs_tok = b.sb("ecs_tok", [128, NC * 128], F32, csgrp_ctx, side="right")
        dt_tok = b.sb("dt_tok", [128, NC * 128], F32, csgrp_ctx, side="right")
        dcy_tok = b.sb("dcy_tok", [128, NC * 128], F32, csgrp_ctx, side="right")
        et_row = b.sb("et_row", [1, NC * 64], F32, csgrp_ctx, side="right")
        etc_row = b.sb("etc_row", [1, NC * 64], F32, csgrp_ctx, side="right")
        et_bc = b.sb("et_bc", [128, NC * 64], F32, csgrp_ctx, side="right")
        etc_bc = b.sb("etc_bc", [128, NC * 64], F32, csgrp_ctx, side="right")
        t4_st = b.sb("t4_st", [64, NC], F32, csgrp_ctx, side="right")

        with ExitStack() as x1ctx:
            xln_fm = [b.sb(f"xlnA{k}", [128, XROWS], BF16, x1ctx) for k in range(KD)]
            ln_pass("p1", xln_fm)
            p2(xln_fm, xbc_s)
            tap_fm("xln", xln_fm, XROWS)
        if "xbc" in tap_t:
            tap_fm("xbc", xbc_s, T)

        yfm_ctx = ExitStack()
        yfm = [b.sb(f"yfm{p}", [128, T], BF16, yfm_ctx) for p in range(32)]
        with ExitStack() as p3ctx:
            S_sb = b.sb("S_sb", [128, NHEADS * 64], F32, p3ctx)
            p3(xbc_s, yfm, S_sb)
        csgrp_ctx.close()
        xbc_ctx.close()

        ofm_ctx = ExitStack()
        ofm = [b.sb(f"p4_ofm{dp}", [128, T], F32, ofm_ctx, side="right") for dp in range(16)]
        with ExitStack() as x2ctx:
            xln_fm2 = [b.sb(f"xlnB{k}", [128, XROWS], BF16, x2ctx) for k in range(KD)]
            ln_pass("p4ln", xln_fm2)
            p4a(xln_fm2, yfm, ofm)
        yfm_ctx.close()

        q_ctx = ExitStack()
        q_fm = [b.sb(f"qfm{k}", [128, T], BF16, q_ctx) for k in range(KD)]
        p4b(ofm, q_fm)
        ofm_ctx.close()

        pfm_ctx = ExitStack()
        pfm = [b.sb(f"pfm{j}", [128, T], BF16, pfm_ctx, side="right") for j in range(KH)]
        p5a(q_fm, pfm)
        q_ctx.close()
        p5b(pfm)
        pfm_ctx.close()

        def fin(sync):
            for name in sorted(b.n):
                if name in ("T", "V", "S", "G"):
                    continue
                if b.n[name]:
                    b.wait(sync, name)
            for name in ("T", "V", "S", "G"):
                if b.n.get(name):
                    b.wait(sync, name)
        block.sync(fin)

    return nc, tap_t


def _tile_w(wT, kblocks, mblocks):
    K, M = wT.shape
    assert K == kblocks * 128 and M == mblocks * 128
    return np.ascontiguousarray(wT.reshape(kblocks, 128, mblocks, 128).transpose(2, 1, 0, 3))


def prep_inputs(L0_attn_x, L0_attn_in_proj_w, L0_attn_conv_w, L0_attn_conv_b,
                L0_attn_dt_bias, L0_attn_A_log, L0_attn_D, L0_attn_norm_w,
                L0_attn_out_proj_w, L0_ffn_W_g, L0_ffn_W_u, L0_ffn_W_d,
                n_cores=8, n_chunks=4):
    T = n_chunks * CHUNK
    XROWS = -(-(T + 3) // 128) * 128
    f = np.float32
    x = np.asarray(L0_attn_x, f)

    wip_p = np.zeros((EP * 128, D_MODEL), f)
    wip_p[:2 * D_INNER + 2 * D_STATE + NHEADS] = np.asarray(L0_attn_in_proj_w, f)
    wip_t = _tile_w(np.ascontiguousarray(wip_p.T), KD, EP)
    wout_t = _tile_w(np.ascontiguousarray(np.asarray(L0_attn_out_proj_w, f).T), KI, 16)
    wg_t = _tile_w(np.ascontiguousarray(np.asarray(L0_ffn_W_g, f).T), KD, KH)
    wu_t = _tile_w(np.ascontiguousarray(np.asarray(L0_ffn_W_u, f).T), KD, KH)
    wd_t = _tile_w(np.ascontiguousarray(np.asarray(L0_ffn_W_d, f).T), KH, 16)
    cw = np.ascontiguousarray(np.asarray(L0_attn_conv_w, f).reshape(N_XBC, 128, 4).transpose(1, 0, 2).reshape(128, N_XBC * 4))
    cb = np.ascontiguousarray(np.asarray(L0_attn_conv_b, f).reshape(N_XBC, 128).T)
    nw = np.ascontiguousarray(np.asarray(L0_attn_norm_w, f).reshape(32, 128).T)
    dtbv = np.ascontiguousarray(np.asarray(L0_attn_dt_bias, f).reshape(64, 1))
    alogv = np.ascontiguousarray(np.asarray(L0_attn_A_log, f).reshape(64, 1))
    dvecv = np.ascontiguousarray(np.asarray(L0_attn_D, f).reshape(1, 64))

    shared = dict(wip=wip_t, wout=wout_t, wg=wg_t, wu=wu_t, wd=wd_t,
                  conv_w=cw, conv_b=cb, norm_w=nw, dtb=dtbv, alog=alogv, dvec=dvecv)
    in_maps = []
    for c in range(n_cores):
        bb, half = c // 2, c % 2
        xc = np.zeros((XROWS, D_MODEL), f)
        if half == 1:
            xc[0:3] = x[bb, T - 3:T]
        xc[3:3 + T] = x[bb, half * T:(half + 1) * T]
        m = dict(shared)
        m["x_core"] = xc
        m["ggate"] = np.full((128, 1), float(half), f)
        in_maps.append(m)
    return in_maps


def kernel(**inputs):
    n_cores, n_chunks = 8, 4
    in_maps = prep_inputs(**inputs, n_cores=n_cores, n_chunks=n_chunks)
    nc, _ = build(n_cores=n_cores, n_chunks=n_chunks)
    res = run_bass_kernel_spmd(nc, in_maps, core_ids=list(range(n_cores)))
    T = n_chunks * CHUNK
    outp = np.empty((4, 2 * T, D_MODEL), np.float32)
    for c in range(n_cores):
        bb, half = c // 2, c % 2
        outp[bb, half * T:(half + 1) * T] = res.results[c]["out"]
    return outp


# version 08
# speedup vs baseline: 1.3568x; 1.0140x over previous
import numpy as np
from contextlib import ExitStack

import concourse.bass as bass
import concourse.mybir as mybir
from concourse.bass_utils import run_bass_kernel_spmd

F32 = mybir.dt.float32
BF16 = mybir.dt.bfloat16
AF = mybir.ActivationFunctionType
ALU = mybir.AluOpType
AX = mybir.AxisListType

CHUNK = 256
D_MODEL = 2048
D_INNER = 4096
D_STATE = 128
NHEADS = 64
KD = D_MODEL // 128
KI = D_INNER // 128
KH = 8192 // 128
EP = 67
E_XBC0 = 32
N_XBC = 34
H_FFN = 8192
MAGIC = 12582912.0
DBG_NO_YOFF = False
DBG_NO_YDIAG = False


def _spans(total, width=512):
    return [(s, min(width, total - s)) for s in range(0, total, width)]


class PB:
    def __init__(self, nc, block, ctx):
        self.nc, self.block, self.ctx = nc, block, ctx
        self._sems, self.n = {}, {}

    def sem(self, name):
        if name not in self._sems:
            self._sems[name] = self.ctx.enter_context(self.nc.semaphore(name))
            self.n[name] = 0
        return self._sems[name]

    def inc(self, instr, name, k=1):
        instr.then_inc(self.sem(name), k)
        self.n[name] += k
        return self.n[name]

    def wait(self, eng, name, val=None):
        if val is None:
            val = self.n.get(name, 0)
        self.sem(name)
        if val > 0:
            eng.wait_ge(self.sem(name), val)

    def step(self, eng, instr, name):
        v = self.inc(instr, name)
        eng.wait_ge(self.sem(name), v)
        return v

    def sb(self, name, shape, dtype, ctx=None, side="left"):
        return (ctx or self.ctx).enter_context(self.nc.sbuf_tensor(name, shape, dtype, side=side))

    def ps(self, name, shape, dtype=F32, ctx=None):
        return (ctx or self.ctx).enter_context(self.nc.psum_tensor(name, shape, dtype))

    def barrier(self):
        names = tuple(nm for nm in self.n if self.n[nm] > 0)
        for sec in (self.V, self.S, self.T, self.G):
            def f(e, names=names):
                for nm in names:
                    self.wait(e, nm)
            sec(f)

    def V(self, f):
        def w(v):
            self.wait(v, "V")
            f(v)
        self.block.vector(w)

    def S(self, f):
        def w(s):
            self.wait(s, "S")
            f(s)
        self.block.scalar(w)

    def T(self, f):
        def w(t):
            self.wait(t, "T")
            f(t)
        self.block.tensor(w)

    def G(self, f):
        def w(g):
            self.wait(g, "G")
            f(g)
        self.block.gpsimd(w)


def build(n_cores=8, n_chunks=4, taps=()):
    T = n_chunks * CHUNK
    XROWS = -(-(T + 3) // 128) * 128
    NTT = XROWS // 128
    NTM = T // 128
    NC = n_chunks
    TS_ = _spans(T)
    NS = len(TS_)
    taps = set(taps)

    nc = bass.Bass("TRN2", target_bir_lowering=False)

    x_core = nc.dram_tensor("x_core", [XROWS, D_MODEL], F32, kind="ExternalInput").ap()
    wip = nc.dram_tensor("wip", [EP, 128, KD, 128], F32, kind="ExternalInput").ap()
    wout = nc.dram_tensor("wout", [16, 128, KI, 128], F32, kind="ExternalInput").ap()
    wg = nc.dram_tensor("wg", [KH, 128, KD, 128], F32, kind="ExternalInput").ap()
    wu = nc.dram_tensor("wu", [KH, 128, KD, 128], F32, kind="ExternalInput").ap()
    wd = nc.dram_tensor("wd", [16, 128, KH, 128], F32, kind="ExternalInput").ap()
    conv_w = nc.dram_tensor("conv_w", [128, N_XBC * 4], F32, kind="ExternalInput").ap()
    conv_b = nc.dram_tensor("conv_b", [128, N_XBC], F32, kind="ExternalInput").ap()
    norm_w = nc.dram_tensor("norm_w", [128, 32], F32, kind="ExternalInput").ap()
    dtb = nc.dram_tensor("dtb", [64, 1], F32, kind="ExternalInput").ap()
    alog = nc.dram_tensor("alog", [64, 1], F32, kind="ExternalInput").ap()
    dvec = nc.dram_tensor("dvec", [1, 64], F32, kind="ExternalInput").ap()
    ggate = nc.dram_tensor("ggate", [128, 1], F32, kind="ExternalInput").ap()

    out = nc.dram_tensor("out", [T, D_MODEL], F32, kind="ExternalOutput").ap()
    x1_scr = nc.dram_tensor("x1_scr", [T, D_MODEL], F32, kind="Internal").ap()
    ag_in = nc.dram_tensor("ag_in", [128, NHEADS * 64], F32, kind="Internal").ap()
    ag_out = nc.dram_tensor("ag_out", [256, NHEADS * 64], F32, kind="Internal").ap()

    tap_t = {}
    for name, shape in (("xln", [KD * 128, XROWS]), ("xbc", [N_XBC * 128, T]),
                        ("dt", [64, T]), ("cs", [64, T]), ("yfm", [D_INNER, T]),
                        ("x1", [T, D_MODEL]), ("q", [D_MODEL, T]), ("pfm", [H_FFN, T]),
                        ("ssb", [128, NHEADS * 64]), ("ypre", [D_INNER, T])):
        if name in taps:
            tap_t[name] = nc.dram_tensor("tap_" + name, shape, F32, kind="ExternalOutput").ap()

    with ExitStack() as ctx:
        block = ctx.enter_context(nc.Block())
        b = PB(nc, block, ctx)

        tapbuf = b.sb("tapbuf", [128, XROWS if tap_t else 1], F32)

        def tap_fm(name, tiles, cols):
            if name not in tap_t:
                return
            dst = tap_t[name]
            buf = tapbuf
            for i, tile in enumerate(tiles):
                def v(v, i=i, tile=tile):
                    b.wait(v, "T"); b.wait(v, "S"); b.wait(v, "dT")
                    b.step(v, v.tensor_copy(buf[:, 0:cols], tile[:, 0:cols]), "V")
                b.V(v)
                def g(g, i=i):
                    b.wait(g, "V")
                    g.dma_start(out=dst[i * 128:(i + 1) * 128, 0:cols], in_=buf[:, 0:cols]).then_inc(b.sem("dT"), 16)
                    b.n["dT"] += 16
                b.G(g)

        ident_b = b.sb("ident_b", [128, 128], BF16)
        ident_f = b.sb("ident_f", [128, 128], F32)
        ones_col_b = b.sb("ones_col_b", [128, 1], BF16)
        ones_row_f = b.sb("ones_row_f", [1, 128], F32)
        mask01 = b.sb("mask01", [128, 256], F32)
        cw_sb = b.sb("cw_sb", [128, N_XBC * 4], F32)
        cb_sb = b.sb("cb_sb", [128, N_XBC], F32)
        nw_sb = b.sb("nw_sb", [128, 32], F32)
        dtb_sb = b.sb("dtb_sb", [64, 1], F32)
        alog_sb = b.sb("alog_sb", [64, 1], F32)
        expa_sb = b.sb("expa_sb", [64, 1], F32)
        dvec_sb = b.sb("dvec_sb", [1, 64], F32)
        dbc_sb = b.sb("dbc_sb", [128, 64], F32)
        ggate_sb = b.sb("ggate_sb", [128, 1], F32)

        def g0(g):
            for dst, src in ((cw_sb, conv_w), (cb_sb, conv_b), (nw_sb, norm_w),
                             (dtb_sb, dtb), (alog_sb, alog), (dvec_sb, dvec),
                             (ggate_sb, ggate)):
                g.dma_start(out=dst[:], in_=src).then_inc(b.sem("dC"), 16)
                b.n["dC"] += 16
            for tile, val in ((ident_b, 0.0), (ident_f, 0.0), (ones_col_b, 1.0),
                              (ones_row_f, 1.0), (mask01, 1.0)):
                b.inc(g.memset(tile[:], val), "G")
            b.wait(g, "G")
            b.inc(g.affine_select(out=ident_b[:], in_=ident_b[:], compare_op=ALU.not_equal,
                                  fill=1.0, base=0, pattern=[[-1, 128]], channel_multiplier=1), "G")
            b.inc(g.affine_select(out=ident_f[:], in_=ident_f[:], compare_op=ALU.not_equal,
                                  fill=1.0, base=0, pattern=[[-1, 128]], channel_multiplier=1), "G")
            b.inc(g.affine_select(out=mask01[:], in_=mask01[:], compare_op=ALU.is_ge,
                                  fill=0.0, base=0, pattern=[[1, 256]], channel_multiplier=-1), "G")
        b.G(g0)

        def s0(s):
            b.wait(s, "dC")
            b.inc(s.activation(expa_sb[:], alog_sb[:], AF.Exp), "S")
        b.S(s0)

        with ExitStack() as c0:
            dbc_ps = b.ps("dbc_ps", [128, 64], F32, c0)
            def t0(t):
                b.wait(t, "dC")
                b.wait(t, "G")
                b.inc(t.matmul(dbc_ps[:], ones_row_f[:], dvec_sb[:], start=True, stop=True), "T")
            b.T(t0)
            def v0(v):
                b.wait(v, "T")
                b.inc(v.tensor_copy(dbc_sb[:], dbc_ps[:]), "V")
            b.V(v0)


        def ln_pass(prefix, xln_fm):
            b.barrier()
            with ExitStack() as pc:
                xt = [b.sb(f"{prefix}_xt{i}", [128, D_MODEL], F32, pc) for i in range(2)]
                xlt = [b.sb(f"{prefix}_xlt{i}", [128, D_MODEL], BF16, pc) for i in range(2)]
                sq = b.sb(f"{prefix}_sq", [128, D_MODEL], F32, pc)
                sqs = b.sb(f"{prefix}_sqs", [128, D_MODEL], F32, pc)
                stat = b.sb(f"{prefix}_stat", [128, 8], F32, pc)
                trp = [b.ps(f"{prefix}_trp{i}", [128, 128], BF16, pc) for i in range(2)]
                xt_free = [0] * NTT
                xlt_free = [0] * NTT
                ev_done = [0, 0]
                for tt in range(NTT):
                    buf = tt % 2
                    dn = f"{prefix}_ld{buf}"
                    def g(g, tt=tt, buf=buf, dn=dn):
                        if tt >= 2:
                            b.wait(g, "V", xt_free[tt - 2])
                        g.dma_start(out=xt[buf][:], in_=x_core[tt * 128:(tt + 1) * 128, :]).then_inc(b.sem(dn), 16)
                        b.n[dn] = b.n.get(dn, 0) + 16
                    b.G(g)
                    def v(v, tt=tt, buf=buf, dn=dn, dv=b.n[dn]):
                        b.wait(v, dn, dv)
                        b.step(v, v.reduce_sum(stat[:, 0:1], xt[buf][:], axis=AX.X), "V")
                        b.step(v, v.tensor_scalar(stat[:, 1:2], stat[:, 0:1], 1.0 / D_MODEL, None, ALU.mult), "V")
                        xt_free[tt] = b.step(v, v.tensor_scalar(sq[:], xt[buf][:], stat[:, 1:2], None, ALU.subtract), "V")
                    b.V(v)
                    def v2(v):
                        b.step(v, v.scalar_tensor_tensor(sqs[:], sq[:], 1.0, sq[:], ALU.mult, ALU.mult,
                                                         accum_out=stat[:, 2:3]), "V")
                        b.step(v, v.tensor_scalar(stat[:, 3:4], stat[:, 2:3], 1.0 / D_MODEL, 1e-5, ALU.mult, ALU.add), "V")
                        b.inc(v.reciprocal(stat[:, 4:5], stat[:, 3:4]), "V")
                    b.V(v2)
                    def s2(s, vv=b.n["V"]):
                        b.wait(s, "V", vv)
                        b.inc(s.activation(stat[:, 5:6], stat[:, 4:5], AF.Sqrt), "S")
                    b.S(s2)
                    def v3(v, tt=tt, buf=buf, sv=b.n["S"]):
                        b.wait(v, "S", sv)
                        if tt >= 2:
                            b.wait(v, "T", xlt_free[tt - 2])
                        b.inc(v.tensor_scalar(xlt[buf][:], sq[:], stat[:, 5:6], None, ALU.mult), "V")
                    b.V(v3)
                    xlv = b.n["V"]
                    for k in range(KD):
                        pbk = k % 2
                        def t(t, tt=tt, buf=buf, k=k, pbk=pbk, xlv=xlv):
                            b.wait(t, "V", xlv)
                            b.wait(t, "V", ev_done[pbk])
                            xlt_free[tt] = b.inc(t.transpose(trp[pbk][:], xlt[buf][:, k * 128:(k + 1) * 128], ident_b[:]), "T")
                        b.T(t)
                        def s3(v, tt=tt, k=k, pbk=pbk, tv=b.n["T"]):
                            b.wait(v, "T", tv)
                            ev_done[pbk] = b.inc(v.tensor_copy(xln_fm[k][:, tt * 128:(tt + 1) * 128], trp[pbk][:]), "V")
                        b.V(s3)
                def vf(v):
                    b.step(v, v.engine_nop(), "V")
                b.V(vf)

        def p2(xln_fm, xbc_s):
            b.barrier()
            with ExitStack() as pc:
                wbuf = [b.sb(f"p2_w{i}", [128, KD * 128], BF16, pc) for i in range(2)]
                mp = [b.ps(f"p2_mp{i}", [128, 512], F32, pc) for i in range(4)]
                hp = b.ps("p2_hp", [128, (N_XBC + 1) * 3], F32, pc)
                xp = b.sb("p2_xp", [128, T + 3], F32, pc)
                dt_sb = b.sb("dt_sb", [64, T], F32, pc)
                da_sb = b.sb("da_sb", [64, T], F32, pc)
                ecs_sb = b.sb("ecs_sb", [64, T], F32, pc)
                dcy_sb = b.sb("dcy_sb", [64, T], F32, pc)
                gate_rst = b.sb("gate_rst", [64, T], F32, pc)
                cacc = b.sb("p2_cacc", [128, T], F32, pc)
                hsb = b.sb("p2_hsb", [128, (N_XBC + 1) * 3], F32, pc)
                k3 = b.sb("p2_k3", [128, N_XBC * 3], F32, pc)
                h6 = b.sb("p2_h6", [128, 6], F32, pc)

                etiles = list(range(E_XBC0, EP))
                wdma = [0, 0]
                wfree = [0, 0]
                mp_mm = [0] * 4
                mp_ev = [0] * 4
                xp_free = [0]

                def gp(g):
                    b.inc(g.memset(gate_rst[:], 1.0), "G")
                    b.wait(g, "G")
                    for c in range(NC):
                        b.inc(g.memset(gate_rst[:, c * CHUNK:c * CHUNK + 1], 0.0), "G")
                    g.dma_start(out=wbuf[0][:], in_=wip[etiles[0]].rearrange("p k m -> p (k m)")).then_inc(b.sem("p2w0"), 16)
                    b.n["p2w0"] = b.n.get("p2w0", 0) + 16
                b.G(gp)
                wdma[0] = b.n["p2w0"]

                for i, e in enumerate(etiles):
                    par = i % 2
                    gi = e - E_XBC0
                    if i + 1 < len(etiles):
                        def g(g, i=i):
                            npar = (i + 1) % 2
                            if i >= 1:
                                b.wait(g, "T", wfree[npar])
                            g.dma_start(out=wbuf[npar][:], in_=wip[etiles[i + 1]].rearrange("p k m -> p (k m)")).then_inc(b.sem(f"p2w{npar}"), 16)
                            b.n[f"p2w{npar}"] = b.n.get(f"p2w{npar}", 0) + 16
                            wdma[npar] = b.n[f"p2w{npar}"]
                        b.G(g)
                    bk0 = 2 * par
                    def t(t, i=i, par=par, bk0=bk0, gi=gi):
                        b.wait(t, f"p2w{par}", wdma[par])
                        for si in range(NS):
                            b.wait(t, "V", mp_ev[bk0 + si])
                        for si, (st, wdt) in enumerate(TS_):
                            for k in range(KD):
                                last = t.matmul(mp[bk0 + si][:, 0:wdt], wbuf[par][:, k * 128:(k + 1) * 128],
                                                xln_fm[k][:, 3 + st:3 + st + wdt],
                                                start=(k == 0), stop=(k == KD - 1))
                            mp_mm[bk0 + si] = b.inc(last, "T")
                        for k in range(KD):
                            last = t.matmul(hp[:, gi * 3:gi * 3 + 3], wbuf[par][:, k * 128:(k + 1) * 128],
                                            xln_fm[k][:, 0:3], start=(k == 0), stop=(k == KD - 1))
                        wfree[par] = b.inc(last, "T")
                    b.T(t)
                    halo_mm = b.n["T"]
                    if gi < N_XBC:
                        def v(v, i=i, bk0=bk0, gi=gi):
                            b.wait(v, "S")
                            b.wait(v, "V", xp_free[0])
                            for si, (st, wdt) in enumerate(TS_):
                                b.wait(v, "T", mp_mm[bk0 + si])
                                mp_ev[bk0 + si] = b.inc(v.tensor_copy(xp[:, 3 + st:3 + st + wdt], mp[bk0 + si][:, 0:wdt]), "V")
                            v.wait_ge(b.sem("V"), b.n["V"])
                            b.step(v, v.tensor_copy(k3[:, gi * 3:gi * 3 + 3], xp[:, 3:6]), "V")
                            w4 = cw_sb[:, gi * 4:gi * 4 + 4]
                            b.step(v, v.tensor_scalar(cacc[:, 3:T], xp[:, 3:T], w4[:, 0:1], None, ALU.mult), "V")
                            b.step(v, v.scalar_tensor_tensor(cacc[:, 3:T], xp[:, 4:T + 1], w4[:, 1:2], cacc[:, 3:T], ALU.mult, ALU.add), "V")
                            b.step(v, v.scalar_tensor_tensor(cacc[:, 3:T], xp[:, 5:T + 2], w4[:, 2:3], cacc[:, 3:T], ALU.mult, ALU.add), "V")
                            xp_free[0] = b.step(v, v.scalar_tensor_tensor(cacc[:, 3:T], xp[:, 6:T + 3], w4[:, 3:4], cacc[:, 3:T], ALU.mult, ALU.add), "V")
                        b.V(v)
                        def s(s, gi=gi, vv=None):
                            b.wait(s, "V", xp_free[0])
                            b.inc(s.activation(xbc_s[gi][:, 3:T], cacc[:, 3:T], AF.Silu,
                                               bias=cb_sb[:, gi:gi + 1]), "S")
                        b.S(s)
                    else:
                        def s(s, bk0=bk0):
                            for si, (st, wdt) in enumerate(TS_):
                                b.wait(s, "T", mp_mm[bk0 + si])
                                b.inc(s.activation(da_sb[0:64, st:st + wdt], mp[bk0 + si][0:64, 0:wdt],
                                                   AF.Exp, bias=dtb_sb[:, 0:1]), "S")
                        b.S(s)
                        def v(v, bk0=bk0, sv=b.n["S"]):
                            b.wait(v, "S", sv)
                            for si in range(NS):
                                mp_ev[bk0 + si] = b.inc(v.engine_nop(), "V")
                        b.V(v)

                def vh(v, hm=b.n["T"]):
                    b.wait(v, "T", hm)
                    b.step(v, v.tensor_copy(hsb[:], hp[:]), "V")
                b.V(vh)
                for gi in range(N_XBC):
                    def v(v, gi=gi):
                        b.wait(v, "S")
                        b.step(v, v.tensor_copy(h6[:, 0:3], hsb[:, gi * 3:gi * 3 + 3]), "V")
                        b.step(v, v.tensor_copy(h6[:, 3:6], k3[:, gi * 3:gi * 3 + 3]), "V")
                        w4 = cw_sb[:, gi * 4:gi * 4 + 4]
                        b.step(v, v.tensor_scalar(cacc[:, 0:3], h6[:, 0:3], w4[:, 0:1], None, ALU.mult), "V")
                        b.step(v, v.scalar_tensor_tensor(cacc[:, 0:3], h6[:, 1:4], w4[:, 1:2], cacc[:, 0:3], ALU.mult, ALU.add), "V")
                        b.step(v, v.scalar_tensor_tensor(cacc[:, 0:3], h6[:, 2:5], w4[:, 2:3], cacc[:, 0:3], ALU.mult, ALU.add), "V")
                        b.step(v, v.scalar_tensor_tensor(cacc[:, 0:3], h6[:, 3:6], w4[:, 3:4], cacc[:, 0:3], ALU.mult, ALU.add), "V")
                    b.V(v)
                    def s(s, gi=gi, vv=b.n["V"]):
                        b.wait(s, "V", vv)
                        b.inc(s.activation(xbc_s[gi][:, 0:3], cacc[:, 0:3], AF.Silu,
                                           bias=cb_sb[:, gi:gi + 1]), "S")
                    b.S(s)

                def sd(s):
                    b.inc(s.activation(dt_sb[0:64, :], da_sb[0:64, :], AF.Ln, bias=1.0), "S")
                b.S(sd)
                def vd(v, sv=b.n["S"]):
                    b.wait(v, "S", sv)
                    b.wait(v, "G")
                    b.step(v, v.tensor_scalar(da_sb[0:64, :], dt_sb[0:64, :], expa_sb[:, 0:1], -1.0, ALU.mult, ALU.mult), "V")
                    b.step(v, v.tensor_tensor_scan(cs_sb[0:64, :], gate_rst[:, :], da_sb[0:64, :], 0.0, ALU.mult, ALU.add), "V")
                b.V(vd)
                def se(s, vv=b.n["V"]):
                    b.wait(s, "V", vv)
                    b.inc(s.activation(ecs_sb[0:64, :], cs_sb[0:64, :], AF.Exp), "S")
                    for c in range(NC):
                        sl = slice(c * CHUNK, (c + 1) * CHUNK)
                        b.inc(s.activation(dcy_sb[0:64, sl], cs_sb[0:64, sl], AF.Exp,
                                           bias=cs_sb[:, (c + 1) * CHUNK - 1:(c + 1) * CHUNK], scale=-1.0), "S")
                b.S(se)
                dchain = b.n["S"]

                for nm, src in (("dt", dt_sb), ("cs", cs_sb)):
                    if nm in tap_t:
                        def gd(g, nm=nm, src=src):
                            b.wait(g, "V"); b.wait(g, "S")
                            g.dma_start(out=tap_t[nm][:, :], in_=src[0:64, :]).then_inc(b.sem("dT"), 16)
                            b.n["dT"] += 16
                        b.G(gd)
                trf = [b.ps(f"p2_trf{i}", [128, 64], F32, pc) for i in range(2)]
                ev = [0, 0]
                ntr = 0
                for c in range(NC):
                    for lt in range(2):
                        col = (c * 2 + lt) * 64
                        sl = slice(c * CHUNK + lt * 128, c * CHUNK + (lt + 1) * 128)
                        for src, dst in ((cs_sb, cs_tok), (ecs_sb, ecs_tok),
                                         (dt_sb, dt_tok), (dcy_sb, dcy_tok)):
                            pbk = ntr % 2
                            def t(t, src=src, sl=sl, pbk=pbk):
                                b.wait(t, "S", dchain)
                                b.wait(t, "V", ev[pbk])
                                b.inc(t.transpose(trf[pbk][:, 0:64], src[0:64, sl], ident_f[0:64, 0:64]), "T")
                            b.T(t)
                            def v(v, dst=dst, col=col, pbk=pbk, tv=b.n["T"]):
                                b.wait(v, "T", tv)
                                ev[pbk] = b.inc(v.tensor_copy(dst[:, col:col + 64], trf[pbk][:, 0:64]), "V")
                            b.V(v)
                            ntr += 1
                    def vt(v, c=c):
                        b.wait(v, "S", dchain)
                        b.step(v, v.tensor_copy(t4_st[:, c:c + 1], cs_sb[0:64, (c + 1) * CHUNK - 1:(c + 1) * CHUNK]), "V")
                    b.V(vt)
                pbk = ntr % 2
                def tt4(t, pbk=pbk):
                    b.wait(t, "V")
                    b.wait(t, "V", ev[pbk])
                    b.inc(t.transpose(trf[pbk][0:NC, 0:64], t4_st[0:64, 0:NC], ident_f[0:64, 0:64]), "T")
                b.T(tt4)
                t4sb = b.sb("p2_t4sb", [NC, 64], F32, pc)
                def vt4(v, pbk=pbk, tv=b.n["T"]):
                    b.wait(v, "T", tv)
                    ev[pbk] = b.inc(v.tensor_copy(t4sb[:], trf[pbk][0:NC, 0:64]), "V")
                b.V(vt4)
                def gt4(g, vv=b.n["V"]):
                    b.wait(g, "V", vv)
                    for c in range(NC):
                        g.dma_start(out=et_row[0:1, c * 64:(c + 1) * 64], in_=t4sb[c:c + 1, :]).then_inc(b.sem("dS"), 16)
                        b.n["dS"] += 16
                b.G(gt4)
                def st4(s, dv=b.n["dS"]):
                    b.wait(s, "dS", dv)
                    b.inc(s.activation(et_row[0:1, :], et_row[0:1, :], AF.Exp), "S")
                b.S(st4)
                def vcp(v, sv=b.n["S"]):
                    b.wait(v, "S", sv)
                    b.step(v, v.memset(etc_row[0:1, 0:64], 1.0), "V")
                    for c in range(1, NC):
                        b.step(v, v.tensor_tensor(etc_row[0:1, c * 64:(c + 1) * 64],
                                                  etc_row[0:1, (c - 1) * 64:c * 64],
                                                  et_row[0:1, (c - 1) * 64:c * 64], ALU.mult), "V")
                b.V(vcp)
                bcp = b.ps("p2_bcp", [128, NC * 64], F32, pc)
                def tbc(t, vv=b.n["V"]):
                    b.wait(t, "V", vv)
                    b.inc(t.matmul(bcp[:], ones_row_f[:], et_row[0:1, :], start=True, stop=True), "T")
                b.T(tbc)
                def vbc(v, tv=b.n["T"]):
                    b.wait(v, "T", tv)
                    b.step(v, v.tensor_copy(et_bc[:], bcp[:]), "V")
                b.V(vbc)
                def tbc2(t, vv=b.n["V"]):
                    b.wait(t, "V", vv)
                    b.inc(t.matmul(bcp[:], ones_row_f[:], etc_row[0:1, :], start=True, stop=True), "T")
                b.T(tbc2)
                def vbc2(v, tv=b.n["T"]):
                    b.wait(v, "T", tv)
                    b.step(v, v.tensor_copy(etc_bc[:], bcp[:]), "V")
                b.V(vbc2)
                def vfin(v):
                    b.wait(v, "S")
                    b.wait(v, "T")
                    b.step(v, v.engine_nop(), "V")
                b.V(vfin)

        def p3(xbc_s, yfm, S_sb):
            b.barrier()
            with ExitStack() as pc:
                xstk = [b.sb(f"p3_xs{st}", [128, D_INNER], BF16, pc) for st in range(2)]
                Xh = [b.sb(f"p3_Xh{i}", [128, 128], BF16, pc) for i in range(2)]
                Btk = [b.sb(f"p3_B{st}", [128, 128], BF16, pc) for st in range(2)]
                Bd = [b.sb(f"p3_Bd{i}", [128, 256], BF16, pc) for i in range(2)]
                GmA = b.sb("p3_GmA", [128, 256], BF16, pc)
                GmB = b.sb("p3_GmB", [128, 128], BF16, pc)
                rE = [b.sb(f"p3_rE{i}", [128, 384], F32, pc) for i in range(2)]
                Mm = [b.sb(f"p3_M{i}", [128, 384], BF16, pc) for i in range(2)]
                Sbf = [b.sb(f"p3_Sb{i}", [128, 64], BF16, pc) for i in range(2)]
                stg = [b.sb(f"p3_stg{i}", [128, 256], BF16, pc) for i in range(2)]
                oh2 = [b.sb(f"p3_oh{i}", [64, 8 * 128], F32, pc) for i in range(2)]
                Eps = [b.ps(f"p3_E{i}", [128, 384], F32, pc) for i in range(2)]
                Yps = [b.ps(f"p3_Y{i}", [128, 256], F32, pc) for i in range(2)]
                Ops = [b.ps(f"p3_O{i}", [128, 256], F32, pc) for i in range(2)]
                Sps = b.ps("p3_S", [128, 128], F32, pc)
                Trp = b.ps("p3_Tr", [128, 512], BF16, pc)

                def gz(g):
                    b.inc(g.memset(S_sb[:], 0.0), "G")
                b.G(gz)
                SZERO = b.n["G"]

                fr = {
                    "eps": [0, 0],
                    "eexp": [0, 0],
                    "mbs": [0, 0],
                    "ypo": [0, 0],
                    "sps": 0,
                    "stg": [0, 0],
                    "trp": 0,
                    "oh": [0, 0],
                    "ohgen": [0, 0],
                    "epsV": [0, 0],
                }

                def oh_build(grp):
                    gb = grp % 2
                    def g(g, grp=grp, gb=gb):
                        b.wait(g, "T", fr["oh"][gb])
                        b.inc(g.memset(oh2[gb][:], 1.0), "G")
                        b.wait(g, "G")
                        oh3 = oh2[gb].ap().rearrange("k (hh j) -> k hh j", j=128)
                        b.inc(g.affine_select(out=oh3, in_=oh3, compare_op=ALU.is_equal,
                                              fill=0.0, base=-grp * 8, pattern=[[-1, 8], [0, 128]],
                                              channel_multiplier=1), "G")
                    b.G(g)
                    fr["ohgen"][gb] = b.n["G"]

                def chunk_start(c):
                    for stt in range(2):
                        sl = slice(c * CHUNK + stt * 128, c * CHUNK + (stt + 1) * 128)
                        for grp in range(8):
                            def t(t, stt=stt, grp=grp, sl=sl):
                                b.wait(t, "V", fr["trp"])
                                for j in range(4):
                                    last = t.transpose(Trp[:, j * 128:(j + 1) * 128], xbc_s[grp * 4 + j][:, sl], ident_b[:])
                                b.inc(last, "T")
                            b.T(t)
                            def v(v, stt=stt, grp=grp, tv=b.n["T"]):
                                b.wait(v, "T", tv)
                                fr["trp"] = b.inc(v.tensor_copy(xstk[stt][:, grp * 512:(grp + 1) * 512], Trp[:]), "V")
                            b.V(v)
                        def t2(t, stt=stt, sl=sl):
                            b.wait(t, "V", fr["trp"])
                            b.inc(t.transpose(Trp[:, 0:128], xbc_s[32][:, sl], ident_b[:]), "T")
                        b.T(t2)
                        def v2(v, stt=stt, tv=b.n["T"]):
                            b.wait(v, "T", tv)
                            fr["trp"] = b.inc(v.tensor_copy(Btk[stt][:], Trp[:, 0:128]), "V")
                        b.V(v2)
                    def tg(t, c=c):
                        b.wait(t, "V")
                        last = t.matmul(Eps[0][:, 0:256], xbc_s[32][:, c * CHUNK:c * CHUNK + 128],
                                        xbc_s[33][:, c * CHUNK:(c + 1) * CHUNK], start=True, stop=True)
                        b.inc(last, "T")
                        last = t.matmul(Eps[1][:, 0:128], xbc_s[32][:, c * CHUNK + 128:(c + 1) * CHUNK],
                                        xbc_s[33][:, c * CHUNK + 128:(c + 1) * CHUNK], start=True, stop=True)
                        b.inc(last, "T")
                    b.T(tg)
                    def vg(v, tv=b.n["T"]):
                        b.wait(v, "T", tv)
                        b.step(v, v.tensor_tensor(GmA[:], Eps[0][:, 0:256], mask01[:], ALU.mult), "V")
                        b.step(v, v.tensor_tensor(GmB[:], Eps[1][:, 0:128], mask01[:, 0:128], ALU.mult), "V")
                        fr["epsV"][0] = fr["epsV"][1] = b.n["V"]
                    b.V(vg)

                def run_pipeline(c, corr):
                    A_t = [0] * NHEADS
                    BC_s = [0] * NHEADS
                    BCv = [0] * NHEADS
                    S_pred = b.n["S"]
                    D_v = [0] * NHEADS
                    F_t = [0] * NHEADS
                    G_v = [0] * 32
                    H_t = [0] * 32
                    ccol = c * CHUNK

                    def stA(h):
                        grp = h // 8
                        gb = grp % 2
                        ep = h % 2
                        def t(t, h=h, gb=gb, ep=ep):
                            b.wait(t, "G", fr["ohgen"][gb])
                            b.wait(t, "V", fr["eps"][ep])
                            b.wait(t, "V", fr["epsV"][ep])
                            last = t.matmul(Eps[ep][:, 0:256], oh2[gb][:, (h % 8) * 128:(h % 8 + 1) * 128],
                                            cs_sb[0:64, ccol:ccol + CHUNK], start=True, stop=True)
                            A_t[h] = b.inc(last, "T")
                            fr["oh"][gb] = A_t[h]
                        b.T(t)

                    def stBC(h):
                        ep = h % 2
                        ct0 = (c * 2 + 0) * 64 + h
                        ct1 = (c * 2 + 1) * 64 + h
                        def v(v, h=h, ep=ep, ct0=ct0, ct1=ct1):
                            b.wait(v, "T", A_t[h])
                            b.wait(v, "S", fr["eexp"][ep])
                            b.inc(v.tensor_scalar(rE[ep][:, 0:256], Eps[ep][:, 0:256],
                                                  cs_tok[:, ct0:ct0 + 1], 0.0, ALU.subtract, ALU.min), "V")
                            BCv[h] = b.inc(v.tensor_scalar(rE[ep][:, 256:384], Eps[ep][:, 128:256],
                                                           cs_tok[:, ct1:ct1 + 1], 0.0, ALU.subtract, ALU.min), "V")
                            fr["eps"][ep] = BCv[h]
                        b.V(v)
                        BC_s[h] = S_pred + h + 1


                    def stD(h):
                        ep = h % 2
                        hs = slice(h * 64, (h + 1) * 64)
                        ct0 = (c * 2 + 0) * 64 + h
                        ct1 = (c * 2 + 1) * 64 + h
                        def v(v, h=h, ep=ep, hs=hs, ct0=ct0, ct1=ct1):
                            b.wait(v, "S", BC_s[h])
                            b.wait(v, "T", fr["mbs"][ep])
                            b.inc(v.tensor_tensor(Mm[ep][:, 0:256], GmA[:], rE[ep][:, 0:256], ALU.mult), "V")
                            b.inc(v.tensor_tensor(Mm[ep][:, 256:384], GmB[:], rE[ep][:, 256:384], ALU.mult), "V")
                            fr["eexp"][ep] = BC_s[h]

                            b.inc(v.tensor_scalar(Bd[ep][:, 0:128], Btk[0][:], dcy_tok[:, ct0:ct0 + 1], None, ALU.mult), "V")
                            b.inc(v.tensor_scalar(Bd[ep][:, 128:256], Btk[1][:], dcy_tok[:, ct1:ct1 + 1], None, ALU.mult), "V")
                            b.inc(v.tensor_scalar(Xh[ep][:, 0:64], xstk[0][:, hs], dt_tok[:, ct0:ct0 + 1], None, ALU.mult), "V")
                            b.inc(v.tensor_scalar(Xh[ep][:, 64:128], xstk[1][:, hs], dt_tok[:, ct1:ct1 + 1], None, ALU.mult), "V")
                            if c > 0:
                                b.inc(v.tensor_copy(Sbf[ep][:], S_sb[:, hs]), "V")
                            D_v[h] = b.n["V"]
                        b.V(v)

                    def stDc(h):
                        ep = h % 2
                        hs = slice(h * 64, (h + 1) * 64)
                        def v(v, h=h, ep=ep, hs=hs):
                            b.wait(v, "T", fr["mbs"][ep])
                            D_v[h] = b.inc(v.tensor_scalar(Sbf[ep][:], S_sb[:, hs],
                                                           etc_bc[:, c * 64 + h:c * 64 + h + 1], None, ALU.mult), "V")
                        b.V(v)

                    def stF(h):
                        ep = h % 2
                        pair, par = h // 2, h % 2
                        pp = pair % 2
                        hs = slice(h * 64, (h + 1) * 64)
                        yc = par * 64
                        def t(t, h=h, ep=ep, pair=pair, par=par, pp=pp, hs=hs, yc=yc):
                            b.wait(t, "V", D_v[h])
                            if par == 0:
                                b.wait(t, "V", fr["ypo"][pp])
                                b.wait(t, "V", fr["sps"])
                            if not corr:
                                t.matmul(Yps[pp][:, yc:yc + 64], Mm[ep][:, 0:128], Xh[ep][:, 0:64], start=True, stop=True)
                                t.matmul(Yps[pp][:, 128 + yc:192 + yc], Mm[ep][:, 128:256], Xh[ep][:, 0:64], start=True, stop=False)
                                last = t.matmul(Yps[pp][:, 128 + yc:192 + yc], Mm[ep][:, 256:384], Xh[ep][:, 64:128], start=False, stop=True)
                                t.matmul(Sps[:, yc:yc + 64], Bd[ep][:, 0:128], Xh[ep][:, 0:64], start=True, stop=False)
                                last = t.matmul(Sps[:, yc:yc + 64], Bd[ep][:, 128:256], Xh[ep][:, 64:128], start=False, stop=True)
                            if corr or (c > 0 and not DBG_NO_YOFF):
                                t.matmul(Ops[pp][:, yc:yc + 64], xbc_s[33][:, ccol:ccol + 128], Sbf[ep][:], start=True, stop=True)
                                last = t.matmul(Ops[pp][:, 128 + yc:192 + yc], xbc_s[33][:, ccol + 128:ccol + 256], Sbf[ep][:], start=True, stop=True)
                            F_t[h] = b.inc(last, "T")
                            fr["mbs"][ep] = F_t[h]
                        b.T(t)

                    def stG(p):
                        pp = p % 2
                        def v(v, p=p, pp=pp):
                            b.wait(v, "T", F_t[2 * p + 1])
                            b.wait(v, "T", fr["stg"][pp])
                            b.wait(v, "G", SZERO)
                            for par in range(2):
                                h = 2 * p + par
                                hs = slice(h * 64, (h + 1) * 64)
                                yc = par * 64
                                ct0 = (c * 2 + 0) * 64 + h
                                ct1 = (c * 2 + 1) * 64 + h
                                if not corr:
                                    for lt, ct in ((0, ct0), (1, ct1)):
                                        dst = stg[pp][:, lt * 128 + yc:lt * 128 + yc + 64]
                                        b.inc(v.scalar_tensor_tensor(dst, xstk[lt][:, hs], dbc_sb[:, h:h + 1],
                                                                     Yps[pp][:, lt * 128 + yc:lt * 128 + yc + 64],
                                                                     ALU.mult, ALU.add), "V")
                                    if c > 0 and not DBG_NO_YOFF:
                                        v.wait_ge(b.sem("V"), b.n["V"])
                                        for lt, ct in ((0, ct0), (1, ct1)):
                                            dst = stg[pp][:, lt * 128 + yc:lt * 128 + yc + 64]
                                            b.inc(v.scalar_tensor_tensor(dst, Ops[pp][:, lt * 128 + yc:lt * 128 + yc + 64],
                                                                         ecs_tok[:, ct:ct + 1], dst, ALU.mult, ALU.add), "V")
                                    b.inc(v.scalar_tensor_tensor(S_sb[:, hs], S_sb[:, hs],
                                                                 et_bc[:, c * 64 + h:c * 64 + h + 1],
                                                                 Sps[:, yc:yc + 64], ALU.mult, ALU.add), "V")
                                else:
                                    for lt, ct in ((0, ct0), (1, ct1)):
                                        b.inc(v.tensor_scalar(stg[pp][:, lt * 128 + yc:lt * 128 + yc + 64],
                                                              Ops[pp][:, lt * 128 + yc:lt * 128 + yc + 64],
                                                              ecs_tok[:, ct:ct + 1], None, ALU.mult), "V")
                            G_v[p] = b.n["V"]
                            fr["ypo"][pp] = G_v[p]
                            fr["sps"] = G_v[p]
                        b.V(v)

                    def stH(p):
                        pp = p % 2
                        def t(t, p=p, pp=pp):
                            b.wait(t, "V", G_v[p])
                            b.wait(t, "V", fr["trp"])
                            for lt in range(2):
                                last = t.transpose(Trp[:, lt * 128:(lt + 1) * 128], stg[pp][:, lt * 128:(lt + 1) * 128], ident_b[:])
                            H_t[p] = b.inc(last, "T")
                            fr["stg"][pp] = H_t[p]
                        b.T(t)

                    def stI(p):
                        pair = p
                        def v(v, p=p, pair=pair):
                            b.wait(v, "T", H_t[p])
                            for lt in range(2):
                                sl = slice(ccol + lt * 128, ccol + (lt + 1) * 128)
                                if not corr:
                                    b.inc(v.tensor_copy(yfm[pair][:, sl], Trp[:, lt * 128:(lt + 1) * 128]), "V")
                                else:
                                    b.inc(v.tensor_tensor(yfm[pair][:, sl], yfm[pair][:, sl],
                                                          Trp[:, lt * 128:(lt + 1) * 128], ALU.add), "V")
                            fr["trp"] = b.n["V"]
                        b.V(v)

                    def s_all(s):
                        for h in range(NHEADS):
                            ep = h % 2
                            b.wait(s, "V", BCv[h])
                            b.inc(s.activation(rE[ep][:, 0:384], rE[ep][:, 0:384], AF.Exp), "S")
                    for slot in range(NHEADS + 5):
                        hF = slot - (2 if not corr else 1)
                        if not corr:
                            if slot < NHEADS:
                                stA(slot)
                                if slot % 8 == 0 and slot // 8 + 1 < 8:
                                    oh_build(slot // 8 + 1)
                            if 0 <= slot - 1 < NHEADS:
                                stBC(slot - 1)
                            if 0 <= slot - 2 < NHEADS:
                                stD(slot - 2)
                        else:
                            if slot < NHEADS:
                                stDc(slot)
                        if 2 <= hF <= NHEADS and hF % 2 == 0:
                            stG(hF // 2 - 1)
                        if 0 <= hF < NHEADS:
                            stF(hF)
                        hH = hF - 1
                        if 2 <= hH and hH % 2 == 0 and hH <= NHEADS:
                            stH(hH // 2 - 1)
                        hI = hF - 2
                        if 2 <= hI and hI % 2 == 0 and hI <= NHEADS:
                            stI(hI // 2 - 1)
                    if not corr:
                        b.S(s_all)
                        assert b.n["S"] == S_pred + NHEADS, (b.n["S"], S_pred)

                for c in range(NC):
                    chunk_start(c)
                    oh_build(0)
                    run_pipeline(c, corr=False)

                if "ssb" in tap_t:
                    def gss(g):
                        b.wait(g, "V"); b.wait(g, "T")
                        g.dma_start(out=tap_t["ssb"][:, :], in_=S_sb[:]).then_inc(b.sem("dT"), 16)
                        b.n["dT"] += 16
                    b.G(gss)
                    def vss(v, dv=b.n["dT"]):
                        b.wait(v, "dT", dv)
                        b.step(v, v.engine_nop(), "V")
                    b.V(vss)
                def gx(g):
                    b.wait(g, "V")
                    g.dma_start(out=ag_in[:], in_=S_sb[:]).then_inc(b.sem("dS"), 16)
                    b.n["dS"] += 16
                    b.wait(g, "dS")
                    g.collective_compute(
                        "AllGather", ALU.bypass,
                        replica_groups=[[2 * i, 2 * i + 1] for i in range(max(1, n_cores // 2))],
                        ins=[ag_in.opt()], outs=[ag_out.opt()],
                    ).then_inc(b.sem("CC"), 1)
                    b.n["CC"] += 1
                    b.wait(g, "CC")
                    g.dma_start(out=S_sb[:], in_=ag_out[0:128, :]).then_inc(b.sem("dS"), 16)
                    b.n["dS"] += 16
                b.G(gx)
                def vgate(v, dv=b.n["dS"]):
                    b.wait(v, "dS", dv)
                    b.step(v, v.tensor_scalar(S_sb[:], S_sb[:], ggate_sb[:, 0:1], None, ALU.mult), "V")
                b.V(vgate)

                for c in range(NC):
                    run_pipeline(c, corr=True)

                def vfin(v):
                    b.wait(v, "T"); b.wait(v, "S")
                    b.step(v, v.engine_nop(), "V")
                b.V(vfin)


        def p4a(xln_fm, yfm, ofm):
            b.barrier()
            with ExitStack() as pc:
                wob = [b.sb(f"p4_wo{i}", [128, KI * 128], BF16, pc, side="right") for i in range(2)]
                mp = [b.ps(f"p4_mp{i}", [128, 512], F32, pc) for i in range(4)]
                vps = [b.ps(f"p4_vp{si}", [1, 512], F32, pc) for si in range(NS)]
                zs = [b.sb(f"p4_zs{i}", [128, 512], F32, pc, side="right") for i in range(2)]
                sqb = b.sb("p4_sqb", [128, 512], BF16, pc, side="right")
                rrow = b.sb("p4_rrow", [1, T], F32, pc, side="right")
                rbc = b.sb("p4_rbc", [128, T], F32, pc, side="right")
                wdma = [0, 0]; wfree = [0, 0]
                mp_mm = [0] * 4; mp_ev = [0] * 4
                vps_mm = 0

                def gp(g):
                    g.dma_start(out=wob[0][:, 0:KD * 128], in_=wip[0].rearrange("p k m -> p (k m)")).then_inc(b.sem("p4z0"), 16)
                    b.n["p4z0"] = b.n.get("p4z0", 0) + 16
                b.G(gp)
                wdma[0] = b.n["p4z0"]

                S_pred4 = b.n["S"]
                silu_jobs = []
                zs_rd = [0, 0]
                for zt in range(32):
                    par = zt % 2
                    bk0 = 2 * par
                    if zt + 1 < 32:
                        def g(g, zt=zt):
                            npar = (zt + 1) % 2
                            if zt >= 1:
                                b.wait(g, "T", wfree[npar])
                            g.dma_start(out=wob[npar][:, 0:KD * 128], in_=wip[zt + 1].rearrange("p k m -> p (k m)")).then_inc(b.sem(f"p4z{npar}"), 16)
                            b.n[f"p4z{npar}"] = b.n.get(f"p4z{npar}", 0) + 16
                            wdma[npar] = b.n[f"p4z{npar}"]
                        b.G(g)
                    def t(t, zt=zt, par=par, bk0=bk0):
                        b.wait(t, f"p4z{par}", wdma[par])
                        for si in range(NS):
                            b.wait(t, "V", mp_ev[bk0 + si])
                        for si, (st, wdt) in enumerate(TS_):
                            for k in range(KD):
                                last = t.matmul(mp[bk0 + si][:, 0:wdt], wob[par][:, k * 128:(k + 1) * 128],
                                                xln_fm[k][:, 3 + st:3 + st + wdt],
                                                start=(k == 0), stop=(k == KD - 1))
                            mp_mm[bk0 + si] = b.inc(last, "T")
                        wfree[par] = b.n["T"]
                    b.T(t)
                    for si, (st, wdt) in enumerate(TS_):
                        idx = len(silu_jobs)
                        zb = idx % 2
                        silu_jobs.append((bk0 + si, zb, wdt, mp_mm[bk0 + si], zs_rd[zb]))
                        sv = S_pred4 + idx + 1
                        def v(v, zt=zt, bk0=bk0, si=si, st=st, wdt=wdt, sv=sv, zb=zb):
                            b.wait(v, "S", sv)
                            mp_ev[bk0 + si] = b.step(v, v.tensor_tensor(yfm[zt][:, st:st + wdt], yfm[zt][:, st:st + wdt],
                                                                        zs[zb][:, 0:wdt], ALU.mult), "V")
                            zs_rd[zb] = b.n["V"]
                        b.V(v)
                        def s2(v, zt=zt, st=st, wdt=wdt):
                            b.inc(v.tensor_tensor(sqb[:, 0:wdt], yfm[zt][:, st:st + wdt],
                                                  yfm[zt][:, st:st + wdt], ALU.mult), "V")
                        b.V(s2)
                        def t2(t, zt=zt, si=si, wdt=wdt, sv=b.n["V"]):
                            nonlocal vps_mm
                            b.wait(t, "V", sv)
                            last = t.matmul(vps[si][0:1, 0:wdt], ones_col_b[:], sqb[:, 0:wdt],
                                            start=(zt == 0), stop=(zt == 31))
                            vps_mm = b.inc(last, "T")
                        b.T(t2)
                def s_all4(s):
                    for (mpix, zb, wdt, mmv, zrd) in silu_jobs:
                        b.wait(s, "T", mmv)
                        b.wait(s, "V", zrd)
                        b.inc(s.activation(zs[zb][:, 0:wdt], mp[mpix][:, 0:wdt], AF.Silu), "S")
                b.S(s_all4)
                assert b.n["S"] == S_pred4 + len(silu_jobs)

                def vr(v):
                    b.wait(v, "T", vps_mm)
                    for si, (st, wdt) in enumerate(TS_):
                        b.step(v, v.tensor_scalar(rrow[0:1, st:st + wdt], vps[si][0:1, 0:wdt],
                                                  1.0 / D_INNER, 1e-5, ALU.mult, ALU.add), "V")
                    b.inc(v.reciprocal(rrow[0:1, :], rrow[0:1, :]), "V")
                b.V(vr)
                def sr(s, vv=b.n["V"]):
                    b.wait(s, "V", vv)
                    b.inc(s.activation(rrow[0:1, :], rrow[0:1, :], AF.Sqrt), "S")
                b.S(sr)
                def tr(t, sv=b.n["S"]):
                    b.wait(t, "S", sv)
                    for si, (st, wdt) in enumerate(TS_):
                        b.wait(t, "V", mp_ev[si])
                        last = t.matmul(mp[si][:, 0:wdt], ones_row_f[:], rrow[0:1, st:st + wdt], start=True, stop=True)
                        mp_mm[si] = b.inc(last, "T")
                b.T(tr)
                for si, (st, wdt) in enumerate(TS_):
                    def v(v, si=si, st=st, wdt=wdt):
                        b.wait(v, "T", mp_mm[si])
                        mp_ev[si] = b.step(v, v.tensor_copy(rbc[:, st:st + wdt], mp[si][:, 0:wdt]), "V")
                    b.V(v)
                for p in range(32):
                    def v(v, p=p):
                        b.inc(v.scalar_tensor_tensor(yfm[p][:], yfm[p][:], nw_sb[:, p:p + 1], rbc[:], ALU.mult, ALU.mult), "V")
                    b.V(v)
                ynv = b.n["V"]
                tap_fm("yfm", yfm, T)

                def gp2(g):
                    b.wait(g, "T", wfree[0]); b.wait(g, "T", wfree[1])
                    g.dma_start(out=wob[0][:], in_=wout[0].rearrange("p k m -> p (k m)")).then_inc(b.sem("p4o0"), 16)
                    b.n["p4o0"] = b.n.get("p4o0", 0) + 16
                    wdma[0] = b.n["p4o0"]
                b.G(gp2)
                for dp in range(16):
                    par = dp % 2
                    bk0 = 2 * par
                    if dp + 1 < 16:
                        def g(g, dp=dp):
                            npar = (dp + 1) % 2
                            if dp >= 1:
                                b.wait(g, "T", wfree[npar])
                            g.dma_start(out=wob[npar][:], in_=wout[dp + 1].rearrange("p k m -> p (k m)")).then_inc(b.sem(f"p4o{npar}"), 16)
                            b.n[f"p4o{npar}"] = b.n.get(f"p4o{npar}", 0) + 16
                            wdma[npar] = b.n[f"p4o{npar}"]
                        b.G(g)
                    def t(t, dp=dp, par=par, bk0=bk0):
                        b.wait(t, f"p4o{par}", wdma[par])
                        b.wait(t, "V", ynv)
                        for si in range(NS):
                            b.wait(t, "V", mp_ev[bk0 + si])
                        for si, (st, wdt) in enumerate(TS_):
                            for k in range(KI):
                                last = t.matmul(mp[bk0 + si][:, 0:wdt], wob[par][:, k * 128:(k + 1) * 128],
                                                yfm[k][:, st:st + wdt], start=(k == 0), stop=(k == KI - 1))
                            mp_mm[bk0 + si] = b.inc(last, "T")
                        wfree[par] = b.n["T"]
                    b.T(t)
                    for si, (st, wdt) in enumerate(TS_):
                        def v(v, dp=dp, bk0=bk0, si=si, st=st, wdt=wdt):
                            b.wait(v, "T", mp_mm[bk0 + si])
                            mp_ev[bk0 + si] = b.step(v, v.tensor_copy(ofm[dp][:, st:st + wdt], mp[bk0 + si][:, 0:wdt]), "V")
                        b.V(v)
                def vfin(v):
                    b.wait(v, "T"); b.wait(v, "S")
                    b.step(v, v.engine_nop(), "V")
                b.V(vfin)

        def p4b(ofm, q_fm):
            b.barrier()
            with ExitStack() as pc:
                xm = [b.sb(f"p4b_xm{i}", [128, D_MODEL], F32, pc, side="right") for i in range(2)]
                x1t = b.sb("p4b_x1t", [128, D_MODEL], F32, pc, side="right")
                qtk = b.sb("p4b_qtk", [128, D_MODEL], BF16, pc, side="right")
                st2 = b.sb("p4b_st2", [128, 8], F32, pc, side="right")
                sq2 = b.sb("p4b_sq2", [128, D_MODEL], F32, pc, side="right")
                sq3 = b.sb("p4b_sq3", [128, D_MODEL], F32, pc, side="right")
                trp2 = [b.ps(f"p4b_tp{i}", [128, 128], F32, pc) for i in range(2)]
                trq = [b.ps(f"p4b_tq{i}", [128, 128], BF16, pc) for i in range(2)]
                tr_ev2 = [0, 0]
                trq_ev = [0, 0]
                xm_free = [0] * NTM
                parkd = 0
                for tt in range(NTM):
                    buf = tt % 2
                    def g(g, tt=tt, buf=buf):
                        if tt >= 2:
                            b.wait(g, "V", xm_free[tt - 2])
                        g.dma_start(out=xm[buf][:], in_=x_core[3 + tt * 128:3 + (tt + 1) * 128, :]).then_inc(b.sem(f"p4x{buf}"), 16)
                        b.n[f"p4x{buf}"] = b.n.get(f"p4x{buf}", 0) + 16
                    b.G(g)
                    x0d = b.n[f"p4x{buf}"]
                    for dp in range(16):
                        pbk = dp % 2
                        def t(t, tt=tt, dp=dp, pbk=pbk):
                            b.wait(t, "V", tr_ev2[pbk])
                            b.inc(t.transpose(trp2[pbk][:], ofm[dp][:, tt * 128:(tt + 1) * 128], ident_f[:]), "T")
                        b.T(t)
                        def v(v, tt=tt, dp=dp, pbk=pbk, buf=buf, tv=b.n["T"], x0d=x0d, parkd=parkd):
                            b.wait(v, "T", tv)
                            b.wait(v, f"p4x{buf}", x0d)
                            b.wait(v, "dP", parkd)
                            tr_ev2[pbk] = b.inc(v.tensor_tensor(x1t[:, dp * 128:(dp + 1) * 128], trp2[pbk][:],
                                                                xm[buf][:, dp * 128:(dp + 1) * 128], ALU.add), "V")
                        b.V(v)
                    xm_free[tt] = b.n["V"]
                    x1v = b.n["V"]
                    if "x1" in tap_t:
                        def gx(g, tt=tt, x1v=x1v):
                            b.wait(g, "V", x1v)
                            g.dma_start(out=tap_t["x1"][tt * 128:(tt + 1) * 128, :], in_=x1t[:]).then_inc(b.sem("dP"), 16)
                            b.n["dP"] += 16
                        b.G(gx)
                    def gp(g, tt=tt, x1v=x1v):
                        b.wait(g, "V", x1v)
                        g.dma_start(out=x1_scr[tt * 128:(tt + 1) * 128, :], in_=x1t[:]).then_inc(b.sem("dP"), 16)
                        b.n["dP"] += 16
                    b.G(gp)
                    parkd = b.n["dP"]
                    def v2(v):
                        b.step(v, v.reduce_sum(st2[:, 0:1], x1t[:], axis=AX.X), "V")
                        b.step(v, v.tensor_scalar(st2[:, 1:2], st2[:, 0:1], 1.0 / D_MODEL, None, ALU.mult), "V")
                        b.step(v, v.tensor_scalar(sq2[:], x1t[:], st2[:, 1:2], None, ALU.subtract), "V")
                    b.V(v2)
                    def v3(v):
                        b.step(v, v.scalar_tensor_tensor(sq3[:], sq2[:], 1.0, sq2[:], ALU.mult, ALU.mult,
                                                         accum_out=st2[:, 2:3]), "V")
                        b.step(v, v.tensor_scalar(st2[:, 3:4], st2[:, 2:3], 1.0 / D_MODEL, 1e-5, ALU.mult, ALU.add), "V")
                        b.inc(v.reciprocal(st2[:, 4:5], st2[:, 3:4]), "V")
                    b.V(v3)
                    def s3(s, vv=b.n["V"]):
                        b.wait(s, "V", vv)
                        b.inc(s.activation(st2[:, 5:6], st2[:, 4:5], AF.Sqrt), "S")
                    b.S(s3)
                    def v4(v, sv=b.n["S"]):
                        b.wait(v, "S", sv)
                        b.step(v, v.tensor_scalar(sq2[:], sq2[:], st2[:, 5:6], None, ALU.mult), "V")
                        b.step(v, v.reduce_sum(st2[:, 0:1], sq2[:], axis=AX.X), "V")
                        b.step(v, v.tensor_scalar(st2[:, 1:2], st2[:, 0:1], 1.0 / D_MODEL, None, ALU.mult), "V")
                        b.step(v, v.tensor_scalar(sq2[:], sq2[:], st2[:, 1:2], None, ALU.subtract), "V")
                    b.V(v4)
                    def v5(v):
                        b.step(v, v.scalar_tensor_tensor(sq3[:], sq2[:], 1.0, sq2[:], ALU.mult, ALU.mult,
                                                         accum_out=st2[:, 2:3]), "V")
                        b.step(v, v.tensor_scalar(st2[:, 3:4], st2[:, 2:3], 1.0 / D_MODEL, 1e-8, ALU.mult, ALU.add), "V")
                        b.inc(v.reciprocal(st2[:, 4:5], st2[:, 3:4]), "V")
                    b.V(v5)
                    def s5(s, vv=b.n["V"]):
                        b.wait(s, "V", vv)
                        b.inc(s.activation(st2[:, 5:6], st2[:, 4:5], AF.Sqrt), "S")
                    b.S(s5)
                    def v6(v, sv=b.n["S"]):
                        b.wait(v, "S", sv)
                        b.step(v, v.tensor_scalar(sq2[:], sq2[:], st2[:, 5:6], None, ALU.mult), "V")
                        b.step(v, v.reduce_max(st2[:, 6:7], sq2[:], axis=AX.X, apply_absolute_value=True), "V")
                        b.step(v, v.reciprocal(st2[:, 7:8], st2[:, 6:7]), "V")
                        b.step(v, v.tensor_scalar(sq2[:], sq2[:], st2[:, 7:8], 127.0, ALU.mult, ALU.mult), "V")
                        b.step(v, v.tensor_scalar(sq2[:], sq2[:], MAGIC, None, ALU.add), "V")
                        b.step(v, v.tensor_scalar(sq2[:], sq2[:], MAGIC, None, ALU.subtract), "V")
                        b.step(v, v.tensor_scalar(qtk[:], sq2[:], st2[:, 6:7], 1.0 / 127.0, ALU.mult, ALU.mult), "V")
                    b.V(v6)
                    qv = b.n["V"]
                    for k in range(KD):
                        pbk = k % 2
                        def t2(t, tt=tt, k=k, pbk=pbk, qv=qv):
                            b.wait(t, "V", qv)
                            b.wait(t, "V", trq_ev[pbk])
                            b.inc(t.transpose(trq[pbk][:], qtk[:, k * 128:(k + 1) * 128], ident_b[:]), "T")
                        b.T(t2)
                        def s6(v, tt=tt, k=k, pbk=pbk, tv=b.n["T"]):
                            b.wait(v, "T", tv)
                            trq_ev[pbk] = b.inc(v.tensor_copy(q_fm[k][:, tt * 128:(tt + 1) * 128], trq[pbk][:]), "V")
                        b.V(s6)
                    def v7(v, tv=b.n["T"]):
                        b.wait(v, "T", tv)
                        b.step(v, v.engine_nop(), "V")
                    b.V(v7)
                def vfin(v):
                    b.wait(v, "S"); b.wait(v, "T")
                    b.step(v, v.engine_nop(), "V")
                b.V(vfin)
            tap_fm("q", q_fm, T)

        def p5a(q_fm, pfm):
            b.barrier()
            with ExitStack() as pc:
                gwb = [b.sb(f"p5_gw{i}", [128, KD * 128], BF16, pc) for i in range(2)]
                uwb = [b.sb(f"p5_uw{i}", [128, KD * 128], BF16, pc) for i in range(2)]
                gsb = [b.sb(f"p5_gs{i}", [128, 512], F32, pc) for i in range(2)]
                psg = [b.ps(f"p5_g{i}", [128, 512], F32, pc) for i in range(2)]
                psu = [b.ps(f"p5_u{i}", [128, 512], F32, pc) for i in range(2)]
                wdma = [0, 0]; wfree = [0, 0]
                g_mm = [0, 0]; u_mm = [0, 0]; gu_ev = [0, 0]

                def gp(g):
                    g.dma_start(out=gwb[0][:], in_=wg[0].rearrange("p k m -> p (k m)")).then_inc(b.sem("p5g0"), 16)
                    g.dma_start(out=uwb[0][:], in_=wu[0].rearrange("p k m -> p (k m)")).then_inc(b.sem("p5g0"), 16)
                    b.n["p5g0"] = b.n.get("p5g0", 0) + 32
                b.G(gp)
                wdma[0] = b.n["p5g0"]

                S_pred5 = b.n["S"]
                silu_jobs = []
                gs_rd = [0, 0]
                for j in range(KH):
                    par = j % 2
                    if j + 1 < KH:
                        def g(g, j=j):
                            npar = (j + 1) % 2
                            if j >= 1:
                                b.wait(g, "T", wfree[npar])
                            g.dma_start(out=gwb[npar][:], in_=wg[j + 1].rearrange("p k m -> p (k m)")).then_inc(b.sem(f"p5g{npar}"), 16)
                            g.dma_start(out=uwb[npar][:], in_=wu[j + 1].rearrange("p k m -> p (k m)")).then_inc(b.sem(f"p5g{npar}"), 16)
                            b.n[f"p5g{npar}"] = b.n.get(f"p5g{npar}", 0) + 32
                            wdma[npar] = b.n[f"p5g{npar}"]
                        b.G(g)
                    def s(v, j=j, par=par):
                        b.wait(v, f"p5g{par}", wdma[par])
                        if j >= 2:
                            b.wait(v, "T", wfree[par])
                        for wb in (gwb[par], uwb[par]):
                            b.step(v, v.tensor_scalar(wb[:], wb[:], 1e30, 1.0, ALU.mult, ALU.min), "V")
                            b.step(v, v.tensor_scalar(wb[:], wb[:], -1.0, None, ALU.max), "V")
                    b.V(s)
                    def t(t, j=j, par=par, sv=b.n["V"]):
                        b.wait(t, "V", sv)
                        for si in range(NS):
                            b.wait(t, "V", gu_ev[si % 2])
                        for si, (st, wdt) in enumerate(TS_):
                            for k in range(KD):
                                last = t.matmul(psg[si % 2][:, 0:wdt], gwb[par][:, k * 128:(k + 1) * 128],
                                                q_fm[k][:, st:st + wdt], start=(k == 0), stop=(k == KD - 1))
                            g_mm[si % 2] = b.inc(last, "T")
                            for k in range(KD):
                                last = t.matmul(psu[si % 2][:, 0:wdt], uwb[par][:, k * 128:(k + 1) * 128],
                                                q_fm[k][:, st:st + wdt], start=(k == 0), stop=(k == KD - 1))
                            u_mm[si % 2] = b.inc(last, "T")
                        wfree[par] = b.n["T"]
                    b.T(t)
                    for si, (st, wdt) in enumerate(TS_):
                        idx = len(silu_jobs)
                        zb = idx % 2
                        silu_jobs.append((si % 2, zb, wdt, g_mm[si % 2], gs_rd[zb]))
                        sv = S_pred5 + idx + 1
                        def v(v, j=j, si=si, st=st, wdt=wdt, sv=sv, zb=zb):
                            b.wait(v, "S", sv)
                            b.wait(v, "T", u_mm[si % 2])
                            gu_ev[si % 2] = b.step(v, v.tensor_tensor(pfm[j][:, st:st + wdt], gsb[zb][:, 0:wdt],
                                                                      psu[si % 2][:, 0:wdt], ALU.mult), "V")
                            gs_rd[zb] = b.n["V"]
                        b.V(v)
                def s_all5(s):
                    for (pgix, zb, wdt, gmv, grd) in silu_jobs:
                        b.wait(s, "T", gmv)
                        b.wait(s, "V", grd)
                        b.inc(s.activation(gsb[zb][:, 0:wdt], psg[pgix][:, 0:wdt], AF.Silu), "S")
                b.S(s_all5)
                assert b.n["S"] == S_pred5 + len(silu_jobs)
                def vfin(v):
                    b.wait(v, "T"); b.wait(v, "S")
                    b.step(v, v.engine_nop(), "V")
                b.V(vfin)
            tap_fm("pfm", pfm, T)

        def p5b(pfm):
            b.barrier()
            with ExitStack() as pc:
                dwb = [b.sb(f"p5_dw{i}", [128, KH * 128], BF16, pc, side="right") for i in range(2)]
                dfm = [b.sb(f"p5_df{dp}", [128, T], BF16, pc, side="right") for dp in range(16)]
                ob = [b.sb(f"p5_ob{i}", [128, D_MODEL], F32, pc, side="right") for i in range(1)]
                psd = [b.ps(f"p5_d{i}", [128, 512], F32, pc) for i in range(4)]
                trp3 = [b.ps(f"p5_t{i}", [128, 128], BF16, pc) for i in range(2)]
                ddma = [0, 0]
                wsgn = [0, 0]
                wfree = [0, 0]
                d_mm = [0] * 4
                d_ev = [0] * 4
                tr_ev3 = [0, 0]
                ob_done = [0, 0]

                pv = b.n["V"]
                for dp in range(16):
                    par = dp % 2
                    bk0 = 2 * par
                    def g(g, dp=dp, par=par):
                        if dp >= 2:
                            b.wait(g, "T", wfree[par])
                        g.dma_start(out=dwb[par][:], in_=wd[dp].rearrange("p k m -> p (k m)")).then_inc(b.sem(f"p5d{par}"), 16)
                        b.n[f"p5d{par}"] = b.n.get(f"p5d{par}", 0) + 16
                        ddma[par] = b.n[f"p5d{par}"]
                    b.G(g)
                    def sv(v, dp=dp, par=par):
                        b.wait(v, f"p5d{par}", ddma[par])
                        if dp >= 2:
                            b.wait(v, "T", wfree[par])
                        b.inc(v.tensor_scalar(dwb[par][:], dwb[par][:], 1e30, 1.0, ALU.mult, ALU.min), "V")
                        v.wait_ge(b.sem("V"), b.n["V"])
                        wsgn[par] = b.inc(v.tensor_scalar(dwb[par][:], dwb[par][:], -1.0, None, ALU.max), "V")
                    b.V(sv)
                    def t(t, dp=dp, par=par, bk0=bk0, pv=pv):
                        b.wait(t, "V", wsgn[par])
                        b.wait(t, "V", pv)
                        for si, (st, wdt) in enumerate(TS_):
                            b.wait(t, "V", d_ev[bk0 + si])
                            for jb in range(KH):
                                last = t.matmul(psd[bk0 + si][:, 0:wdt], dwb[par][:, jb * 128:(jb + 1) * 128],
                                                pfm[jb][:, st:st + wdt], start=(jb == 0), stop=(jb == KH - 1))
                            d_mm[bk0 + si] = b.inc(last, "T")
                        wfree[par] = b.n["T"]
                    b.T(t)
                    for si, (st, wdt) in enumerate(TS_):
                        def v(v, dp=dp, bk0=bk0, si=si, st=st, wdt=wdt):
                            b.wait(v, "T", d_mm[bk0 + si])
                            d_ev[bk0 + si] = b.inc(v.tensor_copy(dfm[dp][:, st:st + wdt], psd[bk0 + si][:, 0:wdt]), "V")
                        b.V(v)
                dfmv = b.n["V"]
                for tt in range(NTM):
                    def g2(g, tt=tt, odv=ob_done[0]):
                        b.wait(g, "V")
                        b.wait(g, "dO0", odv)
                        g.dma_start(out=ob[0][:], in_=x1_scr[tt * 128:(tt + 1) * 128, :]).then_inc(b.sem("p5xr"), 16)
                        b.n["p5xr"] = b.n.get("p5xr", 0) + 16
                    b.G(g2)
                    xrd = b.n["p5xr"]
                    for dp in range(16):
                        pbk = dp % 2
                        def t2(t, tt=tt, dp=dp, pbk=pbk, dfmv=dfmv):
                            b.wait(t, "V", dfmv)
                            b.wait(t, "V", tr_ev3[pbk])
                            b.inc(t.transpose(trp3[pbk][:], dfm[dp][:, tt * 128:(tt + 1) * 128], ident_b[:]), "T")
                        b.T(t2)
                        def v2(v, dp=dp, pbk=pbk, tv=b.n["T"], xrd=xrd):
                            b.wait(v, "T", tv)
                            b.wait(v, "p5xr", xrd)
                            tr_ev3[pbk] = b.inc(v.tensor_tensor(ob[0][:, dp * 128:(dp + 1) * 128],
                                                                ob[0][:, dp * 128:(dp + 1) * 128],
                                                                trp3[pbk][:], ALU.add), "V")
                        b.V(v2)
                    obv = b.n["V"]
                    def g3(g, tt=tt, obv=obv):
                        b.wait(g, "V", obv)
                        g.dma_start(out=out[tt * 128:(tt + 1) * 128, :], in_=ob[0][:]).then_inc(b.sem("dO0"), 16)
                        b.n["dO0"] = b.n.get("dO0", 0) + 16
                    b.G(g3)
                    ob_done[0] = b.n["dO0"]
                def vfin(v):
                    b.wait(v, "T")
                    b.step(v, v.engine_nop(), "V")
                b.V(vfin)

        xbc_ctx = ExitStack()
        xbc_s = [b.sb(f"xbc{g}", [128, T], BF16, xbc_ctx, side="right") for g in range(N_XBC)]
        csgrp_ctx = ExitStack()
        cs_sb = b.sb("cs_sb", [64, T], F32, csgrp_ctx, side="right")
        cs_tok = b.sb("cs_tok", [128, NC * 128], F32, csgrp_ctx, side="right")
        ecs_tok = b.sb("ecs_tok", [128, NC * 128], F32, csgrp_ctx, side="right")
        dt_tok = b.sb("dt_tok", [128, NC * 128], F32, csgrp_ctx, side="right")
        dcy_tok = b.sb("dcy_tok", [128, NC * 128], F32, csgrp_ctx, side="right")
        et_row = b.sb("et_row", [1, NC * 64], F32, csgrp_ctx, side="right")
        etc_row = b.sb("etc_row", [1, NC * 64], F32, csgrp_ctx, side="right")
        et_bc = b.sb("et_bc", [128, NC * 64], F32, csgrp_ctx, side="right")
        etc_bc = b.sb("etc_bc", [128, NC * 64], F32, csgrp_ctx, side="right")
        t4_st = b.sb("t4_st", [64, NC], F32, csgrp_ctx, side="right")

        with ExitStack() as x1ctx:
            xln_fm = [b.sb(f"xlnA{k}", [128, XROWS], BF16, x1ctx) for k in range(KD)]
            ln_pass("p1", xln_fm)
            p2(xln_fm, xbc_s)
            tap_fm("xln", xln_fm, XROWS)
        if "xbc" in tap_t:
            tap_fm("xbc", xbc_s, T)

        yfm_ctx = ExitStack()
        yfm = [b.sb(f"yfm{p}", [128, T], BF16, yfm_ctx) for p in range(32)]
        with ExitStack() as p3ctx:
            S_sb = b.sb("S_sb", [128, NHEADS * 64], F32, p3ctx)
            p3(xbc_s, yfm, S_sb)
        csgrp_ctx.close()
        xbc_ctx.close()

        ofm_ctx = ExitStack()
        ofm = [b.sb(f"p4_ofm{dp}", [128, T], F32, ofm_ctx, side="right") for dp in range(16)]
        with ExitStack() as x2ctx:
            xln_fm2 = [b.sb(f"xlnB{k}", [128, XROWS], BF16, x2ctx) for k in range(KD)]
            ln_pass("p4ln", xln_fm2)
            p4a(xln_fm2, yfm, ofm)
        yfm_ctx.close()

        q_ctx = ExitStack()
        q_fm = [b.sb(f"qfm{k}", [128, T], BF16, q_ctx) for k in range(KD)]
        p4b(ofm, q_fm)
        ofm_ctx.close()

        pfm_ctx = ExitStack()
        pfm = [b.sb(f"pfm{j}", [128, T], BF16, pfm_ctx, side="right") for j in range(KH)]
        p5a(q_fm, pfm)
        q_ctx.close()
        p5b(pfm)
        pfm_ctx.close()

        def fin(sync):
            for name in sorted(b.n):
                if name in ("T", "V", "S", "G"):
                    continue
                if b.n[name]:
                    b.wait(sync, name)
            for name in ("T", "V", "S", "G"):
                if b.n.get(name):
                    b.wait(sync, name)
        block.sync(fin)

    return nc, tap_t


def _tile_w(wT, kblocks, mblocks):
    K, M = wT.shape
    assert K == kblocks * 128 and M == mblocks * 128
    return np.ascontiguousarray(wT.reshape(kblocks, 128, mblocks, 128).transpose(2, 1, 0, 3))


def prep_inputs(L0_attn_x, L0_attn_in_proj_w, L0_attn_conv_w, L0_attn_conv_b,
                L0_attn_dt_bias, L0_attn_A_log, L0_attn_D, L0_attn_norm_w,
                L0_attn_out_proj_w, L0_ffn_W_g, L0_ffn_W_u, L0_ffn_W_d,
                n_cores=8, n_chunks=4):
    T = n_chunks * CHUNK
    XROWS = -(-(T + 3) // 128) * 128
    f = np.float32
    x = np.asarray(L0_attn_x, f)

    wip_p = np.zeros((EP * 128, D_MODEL), f)
    wip_p[:2 * D_INNER + 2 * D_STATE + NHEADS] = np.asarray(L0_attn_in_proj_w, f)
    wip_t = _tile_w(np.ascontiguousarray(wip_p.T), KD, EP)
    wout_t = _tile_w(np.ascontiguousarray(np.asarray(L0_attn_out_proj_w, f).T), KI, 16)
    wg_t = _tile_w(np.ascontiguousarray(np.asarray(L0_ffn_W_g, f).T), KD, KH)
    wu_t = _tile_w(np.ascontiguousarray(np.asarray(L0_ffn_W_u, f).T), KD, KH)
    wd_t = _tile_w(np.ascontiguousarray(np.asarray(L0_ffn_W_d, f).T), KH, 16)
    cw = np.ascontiguousarray(np.asarray(L0_attn_conv_w, f).reshape(N_XBC, 128, 4).transpose(1, 0, 2).reshape(128, N_XBC * 4))
    cb = np.ascontiguousarray(np.asarray(L0_attn_conv_b, f).reshape(N_XBC, 128).T)
    nw = np.ascontiguousarray(np.asarray(L0_attn_norm_w, f).reshape(32, 128).T)
    dtbv = np.ascontiguousarray(np.asarray(L0_attn_dt_bias, f).reshape(64, 1))
    alogv = np.ascontiguousarray(np.asarray(L0_attn_A_log, f).reshape(64, 1))
    dvecv = np.ascontiguousarray(np.asarray(L0_attn_D, f).reshape(1, 64))

    shared = dict(wip=wip_t, wout=wout_t, wg=wg_t, wu=wu_t, wd=wd_t,
                  conv_w=cw, conv_b=cb, norm_w=nw, dtb=dtbv, alog=alogv, dvec=dvecv)
    in_maps = []
    for c in range(n_cores):
        bb, half = c // 2, c % 2
        xc = np.zeros((XROWS, D_MODEL), f)
        if half == 1:
            xc[0:3] = x[bb, T - 3:T]
        xc[3:3 + T] = x[bb, half * T:(half + 1) * T]
        m = dict(shared)
        m["x_core"] = xc
        m["ggate"] = np.full((128, 1), float(half), f)
        in_maps.append(m)
    return in_maps


def kernel(**inputs):
    n_cores, n_chunks = 8, 4
    in_maps = prep_inputs(**inputs, n_cores=n_cores, n_chunks=n_chunks)
    nc, _ = build(n_cores=n_cores, n_chunks=n_chunks)
    res = run_bass_kernel_spmd(nc, in_maps, core_ids=list(range(n_cores)))
    T = n_chunks * CHUNK
    outp = np.empty((4, 2 * T, D_MODEL), np.float32)
    for c in range(n_cores):
        bb, half = c // 2, c % 2
        outp[bb, half * T:(half + 1) * T] = res.results[c]["out"]
    return outp
